# version 10
# baseline (speedup 1.0000x reference)
from contextlib import ExitStack
import numpy as np
import ml_dtypes
import concourse.bass as bass
import concourse.mybir as mybir
from concourse.bass_utils import run_bass_kernel_spmd
from concourse.bass_core_rust import IndirectOffsetOnAxis

F32 = mybir.dt.float32
BF16 = mybir.dt.bfloat16
AF = mybir.ActivationFunctionType
ALU = mybir.AluOpType
AX = mybir.AxisListType

D = 1024
T = 4096
NK = 8
NEXP = 15
FT = 256
NFT = T // FT
CH = 128
NCH = T // CH
AG = 512
NAG = T // AG
BF = ml_dtypes.bfloat16
SPARSE = True
STAGE = "full"
DEBUG_DUMP = False


class Ctr:
    def __init__(self, nc, name):
        self.sem = nc.alloc_semaphore(name=name)
        self.n = 0

    def inc(self, inst, k=1):
        inst.then_inc(self.sem, k)
        self.n += k
        return (self, self.n)

    def tok(self):
        return (self, self.n)


class KB:
    def __init__(self, nc):
        self.nc = nc
        self.t = nc.tensor
        self.v = nc.vector
        self.s = nc.scalar
        self.g = nc.gpsimd
        self.q = nc.sync
        self.seen = {}
        self.mm = Ctr(nc, "mm")
        self.ev = Ctr(nc, "ev")
        self.vv = Ctr(nc, "vv")
        self.pp = Ctr(nc, "pp")
        self.dq = Ctr(nc, "dq")
        self.dg = Ctr(nc, "dg")
        self.dxp = [Ctr(nc, "dx0"), Ctr(nc, "dx1")]
        self.dbp = [Ctr(nc, "db0"), Ctr(nc, "db1")]
        self.dwp = [Ctr(nc, "dw0"), Ctr(nc, "dw1")]
        self.dsp = [Ctr(nc, "ds0"), Ctr(nc, "ds1")]
        self.dst = Ctr(nc, "dst")
        self.ecast = [Ctr(nc, f"ec{i}") for i in range(NEXP)]
        self.acast = Ctr(nc, "acast")
        self.dU = [Ctr(nc, "dUa"), Ctr(nc, "dUb")]
        self.dug = [Ctr(nc, "dug0"), Ctr(nc, "dug1")]
        self.dsc = Ctr(nc, "dsc")
        self.dscp = [Ctr(nc, "dscp0"), Ctr(nc, "dscp1")]
        self.dmcp = [Ctr(nc, "dmcp0"), Ctr(nc, "dmcp1")]
        self.dwg = [Ctr(nc, "dwg0"), Ctr(nc, "dwg1")]
        self.dbg = [Ctr(nc, "dbg0"), Ctr(nc, "dbg1")]
        self.dmt = [Ctr(nc, "dmt0"), Ctr(nc, "dmt1")]
        self.dut = [Ctr(nc, "dut0"), Ctr(nc, "dut1")]
        self.dy = [Ctr(nc, "dya"), Ctr(nc, "dyb")]
        self.dR = [Ctr(nc, "dR0"), Ctr(nc, "dR1")]
        self.dsen = Ctr(nc, "dsen")

    def W(self, eng, *toks):
        for tk in toks:
            if tk is None:
                continue
            ctr, val = tk
            if val <= 0:
                continue
            key = (id(eng), id(ctr))
            if self.seen.get(key, -1) >= val:
                continue
            self.seen[key] = val
            eng.wait_ge(ctr.sem, val)


def _sel_const():
    sel = np.zeros((NEXP, NEXP * 128), dtype=BF)
    for e in range(NEXP):
        sel[e, e * 128:(e + 1) * 128] = 1
    return sel


def build_program(phases=("f1", "at", "f2")):
    nc = bass.Bass("TRN2", target_bir_lowering=False, debug=False)
    kb = KB(nc)
    t, v, s, g, q = kb.t, kb.v, kb.s, kb.g, kb.q

    ap = {}
    def din(name, shape, dtype=F32):
        ap[name] = nc.dram_tensor(name, list(shape), dtype, kind="ExternalInput").ap()
        return ap[name]

    X = din("X", (D, T))
    for nm in ("wqT", "wkT", "wvT", "woT", "sW1", "sW2"):
        din(nm, (D, D))
    din("eW1", (NEXP, 128, NK, D))
    din("eW2", (NEXP, 128, NK, D))
    din("eb1sp", (NEXP * 128, NK))
    din("rW", (128, NK * NEXP))
    din("rb", (NEXP, 1))
    for nm in ("bq", "bk", "bv", "bo", "sb1", "sb2"):
        din(nm, (128, NK))
    din("eb1", (128, NEXP * NK))
    din("eb2", (NEXP, D))
    Y = nc.dram_tensor("Y", [D, T], F32, kind="ExternalOutput").ap()

    X1 = nc.dram_tensor("X1", [D, T], F32, kind="Internal").ap()
    X2 = nc.dram_tensor("X2", [D, T], F32, kind="Internal").ap()
    wbf = {}
    for nm in ("wqT", "wkT", "wvT", "woT"):
        wbf[nm] = nc.dram_tensor(nm + "_bf", [D, D], BF16, kind="Internal").ap()
    eW12 = nc.dram_tensor("eW12_bf", [NEXP, 128, 2, NK, D], BF16, kind="Internal").ap()
    wbf["eW12"] = eW12
    wbf["eW1"] = eW12[:, :, 0]
    wbf["eW2"] = eW12[:, :, 1]
    wbf["eb2d"] = nc.dram_tensor("eb2_bfd", [NEXP, D], BF16, kind="Internal").ap()
    NSLOT = 63 * 256
    wbf["Ud"] = nc.dram_tensor("Ud", [T, D], BF16, kind="Internal").ap()
    wbf["metad"] = nc.dram_tensor("metad", [NSLOT, 16], F32, kind="Internal").ap()
    wbf["R3d"] = nc.dram_tensor("R3d", [3 * T + 2, D], BF16, kind="Internal").ap()
    wbf["eb2j"] = nc.dram_tensor("eb2j", [64, D], BF16, kind="Internal").ap()

    ident_bf_d = nc.inline_tensor(np.eye(128, dtype=BF), name="c_ident_bf").ap()
    ident_f_d = nc.inline_tensor(np.eye(128, dtype=np.float32), name="c_ident_f").ap()
    triu_d = nc.inline_tensor(np.triu(np.ones((128, 128), dtype=np.float32)).astype(BF),
                              name="c_triu").ap()
    sel_d = nc.inline_tensor(_sel_const(), name="c_sel").ap()
    ones_col_d = nc.inline_tensor(np.ones((128, 1), dtype=BF), name="c_ones_col").ap()
    ones_row_d = nc.inline_tensor(np.ones((1, 128), dtype=BF), name="c_ones_row").ap()
    iota128c_d = nc.inline_tensor(np.arange(128, dtype=np.float32).reshape(128, 1), name="c_iota128c").ap()
    iota15b_d = nc.inline_tensor(np.tile(np.arange(NEXP, dtype=np.float32), (128, 1)), name="c_iota15b").ap()
    iota64r_d = nc.inline_tensor(np.arange(64, dtype=np.float32).reshape(1, 64), name="c_iota64r").ap()
    ones15f_d = nc.inline_tensor(np.ones((NEXP, 1), dtype=np.float32), name="c_ones15f").ap()
    ident15f_d = nc.inline_tensor(np.eye(NEXP, dtype=np.float32), name="c_ident15f").ap()

    if DEBUG_DUMP:
        wbf["dbg"] = nc.dram_tensor("DBG", [128, 132], F32, kind="ExternalOutput").ap()
    pio = {"f1": [X, X1], "at": [X1, X2], "f2": [X2, Y]}
    pio[phases[-1]][1] = Y

    with (
        nc.sbuf_tensor("consts", [128, 8], F32) as consts,
        nc.sbuf_tensor("negone", [128, 16], F32) as negone,
        nc.sbuf_tensor("ones_col", [128, 1], BF16) as ones_col,
        nc.sbuf_tensor("ones_row", [1, 128], BF16) as ones_row,
        nc.sbuf_tensor("ident_bf", [128, 128], BF16) as ident_bf,
        nc.sbuf_tensor("ident_f", [128, 128], F32) as ident_f,
        nc.sbuf_tensor("triu", [128, 128], BF16) as triu,
        nc.sbuf_tensor("sel", [NEXP, NEXP * 128], BF16) as sel,
        nc.sbuf_tensor("rw_bf", [128, NK * NEXP], BF16) as rw_bf,
        nc.sbuf_tensor("rb_s", [NEXP, 1], F32) as rb_s,
        nc.sbuf_tensor("eb2_bf", [NEXP, D], BF16) as eb2_bf,
        nc.sbuf_tensor("bcols", [128, 8 * NK], F32) as bcols,
        nc.sbuf_tensor("eb1_s", [128, NEXP * NK], F32) as eb1_s,
        nc.sbuf_tensor("iota128c", [128, 1], F32) as iota128c,
        nc.sbuf_tensor("iota15b", [128, NEXP], F32) as iota15b,
        nc.sbuf_tensor("iota64r", [1, 64], F32) as iota64r,
        nc.sbuf_tensor("ones15f", [NEXP, 1], F32) as ones15f,
        nc.sbuf_tensor("ident15f", [NEXP, NEXP], F32) as ident15f,
    ):
        for val, idx in ((0.5, 0), (-0.5, 1), (1.0 / 1024, 2), (1e-5, 3), (1e-6, 4)):
            nc.const_aps.register(F32, val, consts[:, idx:idx + 1])

        col = lambda buf, i, k: buf[:, i * NK + k:i * NK + k + 1]
        bias = dict(
            bq=lambda k: col(bcols, 0, k), bk=lambda k: col(bcols, 1, k),
            bv=lambda k: col(bcols, 2, k), bo=lambda k: col(bcols, 3, k),
            sb1=lambda k: col(bcols, 4, k), sb2=lambda k: col(bcols, 5, k),
            nbq=lambda k: col(bcols, 6, k), nbk=lambda k: col(bcols, 7, k),
            eb1=lambda e, k: eb1_s[:, e * NK + k:e * NK + k + 1],
        )

        for val, idx in ((0.5, 0), (-0.5, 1), (1.0 / 1024, 2), (1e-5, 3), (1e-6, 4)):
            v.memset(consts[:, idx:idx + 1], float(val))
        init_vv = kb.vv.inc(v.memset(negone[:], -1.0))

        ld_tok = None
        for dst_t, src_ap in (
            (ones_col[:], ones_col_d), (ones_row[:], ones_row_d),
            (ident_bf[:], ident_bf_d), (ident_f[:], ident_f_d), (triu[:], triu_d),
            (sel[:], sel_d), (rb_s[:], ap["rb"]), (eb1_s[:], ap["eb1"]),
            (iota128c[:], iota128c_d), (iota15b[:], iota15b_d),
            (iota64r[:], iota64r_d), (ones15f[:], ones15f_d), (ident15f[:], ident15f_d),
        ):
            ld_tok = kb.dq.inc(q.dma_start(out=dst_t, in_=src_ap), 16)
        for i, nm in enumerate(("bq", "bk", "bv", "bo", "sb1", "sb2")):
            ld_tok = kb.dq.inc(q.dma_start(out=bcols[:, i * NK:(i + 1) * NK], in_=ap[nm]), 16)

        g.dma_start(out=rw_bf[:], in_=ap["rW"]).then_inc(kb.dbg[0].sem, 16)
        kb.dbg[0].n += 16
        rw_tok = kb.dbg[0].inc(g.dma_start(out=eb2_bf[:], in_=ap["eb2"]), 16)

        kb.W(v, ld_tok)
        v.tensor_scalar(bcols[:, 6 * NK:7 * NK], bcols[:, 0 * NK:1 * NK], -1.0, None, ALU.mult)
        bias_tok = kb.vv.inc(
            v.tensor_scalar(bcols[:, 7 * NK:8 * NK], bcols[:, 1 * NK:2 * NK], -1.0, None, ALU.mult))

        cast_tok = {}
        cast_jobs = []
        eb2d_tok = kb.dbg[1].inc(g.dma_start(out=wbf["eb2d"], in_=ap["eb2"]), 16)

        def mk_ecast(e):
            def run():
                kb.ecast[e].inc(g.dma_start(out=wbf["eW1"][e], in_=ap["eW1"][e]), 16)
                cast_tok[e] = kb.ecast[e].inc(g.dma_start(out=wbf["eW2"][e], in_=ap["eW2"][e]), 16)
            return run

        def mk_acast(nm):
            def run():
                cast_tok[nm] = kb.acast.inc(g.dma_start(out=wbf[nm], in_=ap[nm]), 16)
            return run
        for e in range(NEXP):
            cast_jobs.append(mk_ecast(e))
        for nm in ("wqT", "wkT", "wvT", "woT"):
            cast_jobs.append(mk_acast(nm))

        toks = dict(bias=bias_tok, rw=rw_tok, init=init_vv, cast=cast_tok,
                    eb2d=eb2d_tok, cast_jobs=cast_jobs)
        env = dict(consts=consts, negone=negone, ones_col=ones_col,
                   ones_row=ones_row, ident_bf=ident_bf, ident_f=ident_f,
                   triu=triu, sel=sel, rw_bf=rw_bf, rb_s=rb_s, eb2_bf=eb2_bf,
                   bias=bias, iota128c=iota128c, iota15b=iota15b,
                   iota64r=iota64r, ones15f=ones15f, ident15f=ident15f)

        for ph in phases:
            src, dstp = pio[ph]
            if ph in ("f1", "f2"):
                (emit_ffn_sparse if SPARSE else emit_ffn)(nc, kb, ap, wbf, src, dstp, env, toks)
            else:
                while cast_jobs:
                    cast_jobs.pop(0)()
                emit_attn(nc, kb, ap, wbf, src, dstp, env, toks)

        for ctr in (kb.dq, kb.dg, kb.acast, *kb.ecast, *kb.dxp, *kb.dbp,
                    *kb.dwp, *kb.dsp, *kb.dU, *kb.dug, kb.dsc, *kb.dscp, *kb.dmcp, *kb.dwg,
                    *kb.dbg, *kb.dmt, *kb.dut, *kb.dy, *kb.dR, kb.dsen):
            kb.W(q, ctr.tok())
    return nc




def emit_ffn_sparse(nc, kb, ap, wbf, src, dst, env, toks):
    t, v, s, g, q = kb.t, kb.v, kb.s, kb.g, kb.q
    mm, ev, vv = kb.mm, kb.ev, kb.vv
    bias = env["bias"]
    src3 = src.rearrange("(k p) n -> p k n", p=128)
    dst3 = dst.rearrange("(k p) n -> p k n", p=128)
    NJ = 63
    CAP = 256
    NSLOT = NJ * CAP
    SENT = float(3 * T)
    NSLAB = T // 128
    w12f = wbf["eW12"].rearrange("e p t c n -> (e p) (t c n)")
    eb1f = ap["eb1sp"]
    entry = [kb.dsp[0].tok(), kb.dsp[1].tok(), kb.dy[0].tok(), kb.dy[1].tok(), kb.dR[0].tok(), kb.dR[1].tok()]
    octx = ExitStack()
    widx = octx.enter_context(nc.sbuf_tensor(None, [128, 64], mybir.dt.int32))
    with ExitStack() as ctx:
        en = ctx.enter_context
        xt = en(nc.sbuf_tensor(None, [128, 2, NK, FT], F32))
        xbf = en(nc.sbuf_tensor(None, [128, 2, NK, FT], BF16))
        x2b = en(nc.sbuf_tensor(None, [128, NK, FT], BF16))
        u = en(nc.sbuf_tensor(None, [128, 2, NK, FT], BF16))
        t1 = en(nc.sbuf_tensor(None, [128, 2, NK, FT], F32))
        bcast = en(nc.sbuf_tensor(None, [128, 2 * FT], F32))
        srow = en(nc.sbuf_tensor(None, [1, 3 * FT], F32))
        srbf = en(nc.sbuf_tensor(None, [1, 2 * FT], BF16))
        Lrt = en(nc.sbuf_tensor(None, [NEXP, FT], F32))
        Ltm = en(nc.sbuf_tensor(None, [128, 2, NEXP], F32))
        smx = en(nc.sbuf_tensor(None, [128, 2, 4, NEXP], F32))
        smk = en(nc.sbuf_tensor(None, [128, 2, NEXP], mybir.dt.int32))
        smr = en(nc.sbuf_tensor(None, [128, 2, 8], F32))
        keepm = en(nc.sbuf_tensor(None, [128, 2, NEXP], F32))
        wmtm = en(nc.sbuf_tensor(None, [128, 2, NEXP], F32))
        u_tm = en(nc.sbuf_tensor(None, [128, 2, D], BF16))
        sw1r = en(nc.sbuf_tensor(None, [128, NK, D], BF16))
        sw2r = en(nc.sbuf_tensor(None, [128, NK, D], BF16))
        wmfm = en(nc.sbuf_tensor(None, [NEXP, 3, T], BF16))
        ecols = en(nc.sbuf_tensor(None, [128, NSLAB, 4], F32))
        ccols = en(nc.sbuf_tensor(None, [128, NSLAB, 4], F32))
        msum = en(nc.sbuf_tensor(None, [15, T], F32))
        rankf = en(nc.sbuf_tensor(None, [15, T], F32))
        prodf = en(nc.sbuf_tensor(None, [15, T], F32))
        slotw = en(nc.sbuf_tensor(None, [1, T], F32))
        r15c = en(nc.sbuf_tensor(None, [15, 4], F32))
        rA = en(nc.sbuf_tensor(None, [1, 64], F32))
        rB = en(nc.sbuf_tensor(None, [1, 64], F32))
        rC = en(nc.sbuf_tensor(None, [1, 64], F32))
        rI = en(nc.sbuf_tensor(None, [1, 64], mybir.dt.int32))
        rBf = en(nc.sbuf_tensor(None, [1, 64], BF16))
        ecall = en(nc.sbuf_tensor(None, [128, 64], F32))
        ej32 = en(nc.sbuf_tensor(None, [64, 1], mybir.dt.int32))
        eb2all = en(nc.sbuf_tensor(None, [64, D], BF16))
        sentf = en(nc.sbuf_tensor(None, [128, 1024], F32))
        mcall = en(nc.sbuf_tensor(None, [128, 3, 32, 16], F32))
        slc32 = en(nc.sbuf_tensor(None, [128, 2, 32], mybir.dt.int32))
        acc = en(nc.psum_tensor(None, [128, NK * FT], F32))
        mA = en(nc.psum_tensor(None, [128, 512], F32))
        mB = en(nc.psum_tensor(None, [128, 512], F32))

        for eng in (t, v, s, g, q):
            kb.W(eng, *entry)
        kb.W(s, toks["init"], toks["bias"])

        g.dma_start(out=sw1r[:], in_=ap["sW1"].rearrange("(c p) n -> p c n", p=128)).then_inc(kb.dg.sem, 16)
        kb.dg.n += 16
        sw_tok = kb.dg.inc(g.dma_start(out=sw2r[:], in_=ap["sW2"].rearrange("(c p) n -> p c n", p=128)), 16)

        accs = lambda i: acc[:, i * FT:(i + 1) * FT]
        s1t = lambda dk, hk: sw1r[:, dk, hk * 128:(hk + 1) * 128]
        s2t = lambda hk, ik: sw2r[:, hk, ik * 128:(ik + 1) * 128]
        m_row = srow[0:1, 0:FT]
        msq_row = srow[0:1, FT:2 * FT]
        var_row = srow[0:1, 2 * FT:3 * FT]
        rstd_bf = srbf[0:1, 0:FT]
        mrstd_bf = srbf[0:1, FT:2 * FT]

        xtok = [None, None]
        xbtok = [None, None]
        x2tok = None
        srow_w = None
        srbf_w = None
        bc_rd = None
        u_rd = [None, None]
        t1_rd = [None, None]
        mA_free = None
        mB_free = None
        acc_free = None
        hps_rd = [None, None]
        utp_rd = None
        utm_rd = [None, None]
        ltm_w = None
        cwtm_w = None
        wm_rd = None

        def mk_fill(ki3, sb3):
            def run():
                if (ki3, sb3) == (0, 0):
                    kb.W(g, kb.dq.tok(), kb.dmcp[0].tok(), kb.dmcp[1].tok())
                g.tensor_scalar(mcall[:, ki3, sb3, 1:2], env["iota128c"][:],
                                float(ki3 * T + sb3 * 128), None, ALU.add)
                kb.pp.inc(g.tensor_scalar(mcall[:, ki3, sb3, 2:3], env["iota128c"][:],
                                          float(sb3 * 128), None, ALU.add))
            return run
        fill_jobs = [mk_fill(ki3, sb3) for ki3 in range(3) for sb3 in range(NSLAB)]
        smem0 = kb.vv.inc(v.memset(sentf[:], SENT))
        kb.W(q, smem0)
        sen_tok = None
        meta3 = wbf["metad"].rearrange("(a p) n -> p a n", p=128)
        for i in range(2):
            sen_tok = kb.dsen.inc(
                q.dma_start(out=meta3[:, i * 63:(i + 1) * 63, :],
                            in_=sentf[:, 0:63 * 16].rearrange("p (a n) -> p a n", n=16)), 16)

        def load_x(ti):
            p = ti % 2
            kb.W(q, xtok[p])
            a_tok = kb.dxp[p].inc(q.dma_start(out=xt[:, p], in_=src3[:, :, ti * FT:(ti + 1) * FT]), 16)
            kb.W(g, xbtok[p])
            b_tok = kb.dbp[p].inc(g.dma_start(out=xbf[:, p], in_=src3[:, :, ti * FT:(ti + 1) * FT]), 16)
            return a_tok, b_tok

        x_next = load_x(0)
        for ti in range(NFT):
            p = ti % 2
            x_tok, xb_tok = x_next
            if ti + 1 < NFT:
                x_next = load_x(ti + 1)
            if toks["cast_jobs"]:
                toks["cast_jobs"].pop(0)()
            for _ in range(7):
                if fill_jobs:
                    fill_jobs.pop(0)()

            kb.W(s, x_tok, x2tok)
            for k in range(NK):
                ii = s.activation(x2b[:, k], xt[:, p, k], AF.Square)
            sq_tok = ev.inc(ii)
            kb.W(t, xb_tok, sq_tok, mA_free)
            for k in range(NK):
                t.matmul(mA[0:1, 0:FT], env["ones_col"][:], xbf[:, p, k],
                         start=(k == 0), stop=(k == NK - 1))
            for k in range(NK):
                ii = t.matmul(mA[0:1, FT:2 * FT], env["ones_col"][:], x2b[:, k],
                              start=(k == 0), stop=(k == NK - 1))
            stats_tok = mm.inc(ii)
            xbtok[p] = stats_tok
            kb.W(v, stats_tok, srow_w)
            c_tok = vv.inc(v.tensor_scalar(m_row, mA[0:1, 0:FT], 1.0 / 1024, None, ALU.mult))
            kb.W(v, c_tok)
            c_tok = vv.inc(v.tensor_mul(msq_row, m_row, m_row))
            kb.W(v, c_tok)
            c_tok = vv.inc(v.scalar_tensor_tensor(var_row, mA[0:1, FT:2 * FT], 1.0 / 1024,
                                                  msq_row, ALU.mult, ALU.subtract))
            kb.W(s, c_tok)
            e_tok = ev.inc(s.activation(var_row, var_row, AF.Ln, bias=1e-5))
            kb.W(s, e_tok, srbf_w)
            e_tok = ev.inc(s.activation(rstd_bf, var_row, AF.Exp, scale=-0.5))
            kb.W(v, e_tok)
            c_tok = vv.inc(v.tensor_mul(mrstd_bf, m_row, rstd_bf))
            srow_w = c_tok
            kb.W(t, c_tok)
            t.matmul(mA[:, 0:FT], env["ones_row"][:], rstd_bf, start=True, stop=True)
            ii = t.matmul(mA[:, FT:2 * FT], env["ones_row"][:], mrstd_bf, start=True, stop=True)
            bc_tok = mm.inc(ii)
            srbf_w = bc_tok
            kb.W(s, bc_tok, bc_rd)
            s.copy(bcast[:, 0:FT], mA[:, 0:FT])
            bce_tok = ev.inc(s.copy(bcast[:, FT:2 * FT], mA[:, FT:2 * FT]))
            mA_free = bce_tok
            kb.W(v, bce_tok, u_rd[p], t1_rd[p])
            for k in range(NK):
                ii = v.tensor_mul(t1[:, p, k], xt[:, p, k], bcast[:, 0:FT])
            mul_tok = vv.inc(ii)
            kb.W(v, mul_tok)
            for k in range(NK):
                ii = v.tensor_sub(u[:, p, k], t1[:, p, k], bcast[:, FT:2 * FT])
            u_tok = vv.inc(ii)
            bc_rd = u_tok

            with nc.psum_tensor("utp%d" % (ti % 2), [128, D], BF16, allow_name_mangling=True) as utp:
                for cc in range(2):
                    kb.W(t, u_tok, utp_rd)
                    for dk in range(NK):
                        ii = t.transpose(utp[:, dk * 128:(dk + 1) * 128],
                                         u[:, p, dk, cc * 128:(cc + 1) * 128], env["ident_bf"][:])
                    tr_tok = mm.inc(ii)
                    kb.W(v, tr_tok, utm_rd[cc])
                    uc_tok = vv.inc(v.tensor_copy(u_tm[:, cc], utp[:]))
                    utp_rd = uc_tok
                    kb.W(q, uc_tok)
                    utm_rd[cc] = kb.dU[cc].inc(
                        q.dma_start(out=wbf["Ud"][ti * FT + cc * 128: ti * FT + (cc + 1) * 128, :],
                                    in_=u_tm[:, cc]), 16)

            kb.W(t, u_tok, toks["rw"], mB_free)
            for k in range(NK):
                ii = t.matmul(mB[0:NEXP, 0:FT], env["rw_bf"][:, k * NEXP:(k + 1) * NEXP],
                              u[:, p, k], start=(k == 0), stop=(k == NK - 1))
            rt_tok = mm.inc(ii)
            kb.W(v, rt_tok)
            lrt_tok = vv.inc(v.tensor_scalar(Lrt[:], mB[0:NEXP, 0:FT], env["rb_s"][:], None, ALU.add))
            kb.W(t, lrt_tok, ltm_w)
            t.transpose(mB[:, FT:FT + NEXP], Lrt[:, 0:128], env["ident_f"][0:NEXP, 0:NEXP])
            ii = t.transpose(mB[:, FT + NEXP:FT + 2 * NEXP], Lrt[:, 128:256],
                             env["ident_f"][0:NEXP, 0:NEXP])
            ltr_tok = mm.inc(ii)
            kb.W(v, ltr_tok)
            v.tensor_copy(Ltm[:, 0], mB[:, FT:FT + NEXP])
            ltm_tok = vv.inc(v.tensor_copy(Ltm[:, 1], mB[:, FT + NEXP:FT + 2 * NEXP]))
            ltm_w = ltm_tok

            P = lambda sl: smx[:, sl, 0]
            P1 = lambda sl: smx[:, sl, 1]
            P2 = lambda sl: smx[:, sl, 2]
            WMT = lambda sl: smx[:, sl, 3]
            MK = lambda sl: smk[:, sl]
            m1n = lambda sl: smr[:, sl, 0:1]
            m1p = lambda sl: smr[:, sl, 1:2]
            m2r = lambda sl: smr[:, sl, 2:3]
            m3r = lambda sl: smr[:, sl, 3:4]
            rr = lambda sl: smr[:, sl, 4:5]
            ek_c = lambda sl, k: ecols[:, ti * 2 + sl, k:k + 1]
            cw_c = lambda sl, k: ccols[:, ti * 2 + sl, k:k + 1]

            kb.W(v, ltm_tok, toks["init"])
            for sl in (0, 1):
                ii = v.reduce_max(m1n(sl), Ltm[:, sl], axis=AX.X, negate=True)
            c_tok = vv.inc(ii)
            kb.W(s, c_tok)
            for sl in (0, 1):
                ii = s.activation(P(sl), Ltm[:, sl], AF.Exp, bias=m1n(sl))
            e_tok = ev.inc(ii)
            kb.W(v, e_tok)
            for sl in (0, 1):
                v.reduce_sum(rr(sl), P(sl), axis=AX.X)
            for sl in (0, 1):
                ii = v.reduce_max(m1p(sl), P(sl), axis=AX.X)
            c_tok = vv.inc(ii)
            kb.W(v, c_tok)
            for sl in (0, 1):
                ii = v.reciprocal(rr(sl), rr(sl))
            c_tok = vv.inc(ii)
            kb.W(v, c_tok)
            for sl in (0, 1):
                ii = v.tensor_scalar(WMT(sl), P(sl), m1p(sl), None, ALU.is_ge)
            c_tok = vv.inc(ii)
            kb.W(v, c_tok)
            for sl in (0, 1):
                ii = v.scalar_tensor_tensor(P1(sl), WMT(sl), -2.0, P(sl), ALU.mult, ALU.add)
            c_tok = vv.inc(ii)
            kb.W(v, c_tok)
            for sl in (0, 1):
                ii = v.reduce_max(m2r(sl), P1(sl), axis=AX.X)
            c_tok = vv.inc(ii)
            kb.W(v, c_tok)
            for sl in (0, 1):
                ii = v.tensor_scalar(WMT(sl), P1(sl), m2r(sl), None, ALU.is_ge)
            c_tok = vv.inc(ii)
            kb.W(v, c_tok)
            for sl in (0, 1):
                ii = v.scalar_tensor_tensor(P2(sl), WMT(sl), -2.0, P1(sl), ALU.mult, ALU.add)
            c_tok = vv.inc(ii)
            kb.W(v, c_tok)
            for sl in (0, 1):
                ii = v.reduce_max(m3r(sl), P2(sl), axis=AX.X)
            top_tok = vv.inc(ii)

            way_tok = top_tok
            for ki, mk in ((0, m1p), (1, m2r), (2, m3r)):
                kb.W(v, way_tok)
                for sl in (0, 1):
                    ii = v.tensor_scalar(wmtm[:, sl], P(sl), mk(sl), None, ALU.is_equal)
                wt1 = vv.inc(ii)
                kb.W(v, wt1)
                for sl in (0, 1):
                    ii = v.tensor_mul(WMT(sl), wmtm[:, sl], env["iota15b"][:])
                wt2 = vv.inc(ii)
                kb.W(v, wt2)
                for sl in (0, 1):
                    ii = v.reduce_sum(ek_c(sl, ki), WMT(sl), axis=AX.X)
                vv.inc(ii)
                for sl in (0, 1):
                    ii = v.tensor_scalar(cw_c(sl, ki), mk(sl), rr(sl), 0.5, ALU.mult, ALU.mult)
                wt3 = vv.inc(ii)
                kb.W(t, wt1, cwtm_w)
                for sl in (0, 1):
                    ii = t.transpose(mB[0:NEXP, 256 + sl * 128: 256 + (sl + 1) * 128],
                                     wmtm[:, sl], env["ident_f"][:])
                wt4 = mm.inc(ii)
                kb.W(v, wt4)
                for sl in (0, 1):
                    stile = ti * 2 + sl
                    ii = v.tensor_copy(wmfm[:, ki, stile * 128:(stile + 1) * 128],
                                       mB[0:NEXP, 256 + sl * 128:256 + (sl + 1) * 128])
                way_tok = vv.inc(ii)
                cwtm_w = way_tok
            wm_rd = way_tok
            mB_free = way_tok

            with nc.psum_tensor("hps%d" % (ti % 2), [128, 1024], F32, allow_name_mangling=True) as hps:
                hsl = lambda b: hps[:, b * 512:b * 512 + FT]
                h_rd = mm.tok()
                for j in range(NK):
                    slot = j % 2
                    kb.W(t, sw_tok, u_tok, hps_rd[slot], utp_rd)
                    for dk in range(NK):
                        ii = t.matmul(hsl(slot), s1t(dk, j), u[:, p, dk],
                                      start=(dk == 0), stop=(dk == NK - 1))
                    w1tok = mm.inc(ii)
                    kb.W(s, w1tok)
                    ii = s.activation(x2b[:, j], hsl(slot), AF.Relu, bias=bias["sb1"](j))
                    rl_tok = ev.inc(ii)
                    hps_rd[slot] = rl_tok
                    kb.W(t, rl_tok)
                    if j == 0:
                        kb.W(t, acc_free)
                    for ik in range(NK):
                        ii = t.matmul(accs(ik), s2t(j, ik), x2b[:, j],
                                      start=(j == 0 and ik % 2 == 0),
                                      stop=(j == NK - 1 and ik % 2 == 1))
                    mm.inc(ii)
            x2tok = mm.tok()

            acc_tok = mm.tok()
            kb.W(v, acc_tok, t1_rd[p])
            for ik in range(NK):
                ii = v.tensor_scalar(t1[:, p, ik], accs(ik), bias["sb2"](ik), 0.5, ALU.add, ALU.mult)
            fe_tok = vv.inc(ii)
            acc_free = fe_tok
            kb.W(v, fe_tok)
            for ik in range(NK):
                ii = v.tensor_add(xt[:, p, ik], xt[:, p, ik], t1[:, p, ik])
            add_tok = vv.inc(ii)
            t1_rd[p] = add_tok
            u_rd[p] = mm.tok()
            kb.W(q, add_tok)
            kb.dst.n += 16
            xtok[p] = kb.dsp[p].inc(q.dma_start(out=dst3[:, :, ti * FT:(ti + 1) * FT], in_=xt[:, p]), 16)

        while toks["cast_jobs"]:
            toks["cast_jobs"].pop(0)()
        while fill_jobs:
            fill_jobs.pop(0)()
        static_fill = kb.pp.tok()

        if STAGE == "A":
            ctx.close()
            octx.close()
            return
        kb.W(v, wm_rd)
        c_tok = vv.inc(v.tensor_add(msum[:], wmfm[:, 0, :], wmfm[:, 1, :]))
        kb.W(v, c_tok)
        ms_tok = vv.inc(v.tensor_add(msum[:], msum[:], wmfm[:, 2, :]))
        kb.W(v, ms_tok)
        cnt_tok = vv.inc(v.reduce_sum(r15c[:, 0:1], msum[:], axis=AX.X))
        kb.W(t, cnt_tok, mA_free)
        ii = t.transpose(mA[0:1, 0:NEXP], r15c[:, 0:1], env["ident15f"][:])
        ct_tok = mm.inc(ii)
        kb.W(v, ct_tok)
        c_tok = vv.inc(v.tensor_copy(rA[0:1, 0:NEXP], mA[0:1, 0:NEXP]))
        kb.W(v, c_tok)
        c_tok = vv.inc(v.memset(rB[0:1, 0:NEXP], 0.0))
        for m2 in range(16):
            kb.W(v, c_tok)
            c_tok = vv.inc(v.tensor_scalar(rC[0:1, 0:NEXP], rA[0:1, 0:NEXP],
                                           float(256 * m2), None, ALU.is_gt))
            kb.W(v, c_tok)
            c_tok = vv.inc(v.tensor_add(rB[0:1, 0:NEXP], rB[0:1, 0:NEXP], rC[0:1, 0:NEXP]))
        kb.W(v, c_tok)
        c_tok = vv.inc(v.memset(rC[0:1, 0:NEXP], 0.0))
        kb.W(v, c_tok)
        c_tok = vv.inc(v.tensor_tensor_scan(rA[0:1, 0:NEXP], rB[0:1, 0:NEXP],
                                            rC[0:1, 0:NEXP], 0.0, ALU.add, ALU.add))
        kb.W(v, c_tok)
        jb_tok = vv.inc(v.tensor_sub(rA[0:1, 0:NEXP], rA[0:1, 0:NEXP], rB[0:1, 0:NEXP]))
        kb.W(v, jb_tok)
        c_tok = vv.inc(v.tensor_scalar(rC[0:1, 0:NEXP], rA[0:1, 0:NEXP], 256.0, None, ALU.mult))
        kb.W(t, c_tok)
        ii = t.transpose(mA[0:NEXP, 256:257], rC[0:1, 0:NEXP], env["ident_f"][0:1, 0:1])
        sb_tok = mm.inc(ii)
        kb.W(v, sb_tok)
        sbc_tok = vv.inc(v.tensor_copy(r15c[:, 1:2], mA[0:NEXP, 256:257]))
        kb.W(v, sbc_tok)
        c_tok = vv.inc(v.memset(rB[0:1, 0:64], 0.0))
        for e in range(NEXP):
            kb.W(v, c_tok)
            c_tok = vv.inc(v.tensor_scalar(rC[0:1, 0:64], env["iota64r"][:],
                                           rA[0:1, e:e + 1], None, ALU.is_ge))
            kb.W(v, c_tok)
            c_tok = vv.inc(v.tensor_add(rB[0:1, 0:64], rB[0:1, 0:64], rC[0:1, 0:64]))
        kb.W(v, c_tok)
        c_tok = vv.inc(v.tensor_scalar(rB[0:1, 0:64], rB[0:1, 0:64], -1.0, None, ALU.add))
        kb.W(v, c_tok)
        ebf_tok = vv.inc(v.tensor_copy(rBf[0:1, 0:64], rB[0:1, 0:64]))
        kb.W(t, ebf_tok)
        ii = t.matmul(mA[:, 384:448], env["ones_row"][:], rBf[0:1, 0:64], start=True, stop=True)
        ec_tok = mm.inc(ii)
        kb.W(v, ec_tok)
        ecall_tok = vv.inc(v.tensor_copy(ecall[:], mA[:, 384:448]))
        kb.W(v, ecall_tok)
        for j in range(NJ):
            ii = v.scalar_tensor_tensor(t1[:, 0, 0, j:j + 1], ecall[:, j:j + 1], 128.0,
                                        env["iota128c"][:], ALU.mult, ALU.add)
        c_tok = vv.inc(ii)
        kb.W(v, c_tok)
        wi_tok = vv.inc(v.tensor_copy(widx[:, 0:NJ], t1[:, 0, 0, 0:NJ]))
        kb.W(t, ecall_tok)
        ii = t.transpose(mA[0:64, 257:258], rB[0:1, 0:64], env["ident_f"][0:1, 0:1])
        ejt_tok = mm.inc(ii)
        kb.W(v, ejt_tok)
        ej_tok = vv.inc(v.tensor_copy(ej32[:], mA[0:64, 257:258]))
        kb.W(g, ej_tok, toks["eb2d"])
        g.indirect_dma_start(
            out=eb2all[:], out_offset=None, in_=wbf["eb2d"],
            in_offset=IndirectOffsetOnAxis(ap=ej32[:], axis=0)).then_inc(kb.dsc.sem, 16)
        kb.dsc.n += 16
        kb.W(q, kb.dsc.tok())
        ej_store = kb.dsen.inc(q.dma_start(out=wbf["eb2j"], in_=eb2all[:]), 16)
        kb.W(v, ms_tok)
        c_tok = vv.inc(v.memset(prodf[:], 0.0))
        kb.W(v, c_tok)
        c_tok = vv.inc(v.tensor_tensor_scan(rankf[:], msum[:], prodf[:], 0.0, ALU.add, ALU.add))
        kb.W(v, c_tok)
        c_tok = vv.inc(v.tensor_sub(rankf[:], rankf[:], msum[:]))
        kb.W(v, c_tok, sbc_tok)
        slot_tok = vv.inc(v.tensor_scalar(rankf[:], rankf[:], r15c[:, 1:2], None, ALU.add))
        interlude_mm = mm.tok()

        if DEBUG_DUMP:
            with nc.sbuf_tensor("dbgs", [128, 132], F32) as dbgs:
                kb.W(v, wi_tok, ecall_tok, ej_tok)
                v.tensor_copy(dbgs[:, 0:64], t1[:, 0, 0, 0:64])
                v.tensor_copy(dbgs[:, 64:128], widx[:, 0:64])
                d_tok = vv.inc(v.tensor_copy(dbgs[0:64, 129:130], ej32[:]))
                kb.W(q, d_tok)
                dmp = kb.dq.inc(q.dma_start(out=wbf["dbg"], in_=dbgs[:]), 16)
                kb.W(q, dmp)

        kb.W(g, kb.vv.tok())
        for ki in range(3):
            for sb in range(NSLAB):
                ii = g.tensor_copy(mcall[:, ki, sb, 0:1], ccols[:, sb, ki:ki + 1])
        cw_fill = kb.pp.inc(ii)
        scat_mm = interlude_mm
        sc32_tok = [None, None]
        way_scat = [None, None]
        mAcol_rd = None
        for ki in range(3):
            kp = ki % 2
            kb.W(v, slot_tok, scat_mm)
            pr_tok = vv.inc(v.tensor_mul(prodf[:], rankf[:], wmfm[:, ki, :]))
            sl_tok = None
            for pc in range(T // 512):
                kb.W(t, pr_tok, interlude_mm, mB_free)
                ii = t.matmul(mB[0:1, 0:512], env["ones15f"][:],
                              prodf[:, pc * 512:(pc + 1) * 512], start=True, stop=True)
                pm_tok = mm.inc(ii)
                kb.W(v, pm_tok)
                ii = v.tensor_copy(slotw[0:1, pc * 512:(pc + 1) * 512], mB[0:1, 0:512])
                sl_tok = vv.inc(ii)
                mB_free = sl_tok
            kb.W(t, sl_tok, ecall_tok, mAcol_rd)
            for sb in range(NSLAB):
                ii = t.transpose(mA[:, 448 + sb:449 + sb],
                                 slotw[0:1, sb * 128:(sb + 1) * 128], env["ident_f"][0:1, 0:1])
            st_tok = mm.inc(ii)
            kb.W(v, st_tok, way_scat[kp])
            sc32_tok[kp] = vv.inc(v.tensor_copy(slc32[:, kp], mA[:, 448:480]))
            mAcol_rd = sc32_tok[kp]
            kb.W(g, sc32_tok[kp], cw_fill, static_fill, ej_tok, kb.dsen.tok())
            ms2_tok = None
            for sb in range(NSLAB):
                ii = g.indirect_dma_start(
                    out=wbf["metad"],
                    out_offset=IndirectOffsetOnAxis(ap=slc32[:, kp, sb:sb + 1], axis=0),
                    in_=mcall[:, ki, sb], in_offset=None)
                ms2_tok = kb.dmcp[kp].inc(ii, 16)
            way_scat[kp] = ms2_tok
            scat_mm = mm.tok()
        scat_all = (kb.dmcp[0].tok(), kb.dmcp[1].tok())

        ctx.close()
        joinA = tuple(c.tok() for c in (
            kb.vv, kb.ev, kb.mm, kb.dsp[0], kb.dsp[1], kb.dxp[0],
            kb.dxp[1], kb.dbp[0], kb.dbp[1], kb.dg, kb.dU[0], kb.dU[1], kb.dug[0],
            kb.dug[1], kb.dsen, kb.dsc, *kb.dscp, *kb.dmcp, *kb.ecast) if c.n > 0)

        if STAGE == "scat":
            ctx.close()
            octx.close()
            return
        with ExitStack() as ctxb:
            enb = ctxb.enter_context
            wg = enb(nc.sbuf_tensor(None, [128, 2, 2, NK, D], BF16))
            ufm = enb(nc.sbuf_tensor(None, [128, 2, NK, CAP], BF16))
            u_tm2 = enb(nc.sbuf_tensor(None, [128, 2, 2, D], BF16))
            h2b = enb(nc.sbuf_tensor(None, [128, NK, CAP], BF16))
            eb1g = enb(nc.sbuf_tensor(None, [128, 2, NK], F32))
            eb2g = enb(nc.sbuf_tensor(None, [1, 2, D], BF16))
            mt = enb(nc.sbuf_tensor(None, [128, 2, 2, 16], F32))
            iy = enb(nc.sbuf_tensor(None, [128, 2, 2], mybir.dt.int32))
            iu = enb(nc.sbuf_tensor(None, [128, 2, 2], mybir.dt.int32))
            ysc = enb(nc.sbuf_tensor(None, [128, 2, 1024], BF16))
            hpsB = enb(nc.psum_tensor(None, [128, 1024], F32))
            yps = enb(nc.psum_tensor(None, [128, 2, 2, 512], F32))
            utpB = enb(nc.psum_tensor(None, [128, 2, 1024], BF16))

            wg_rd = [None, None]
            mt_rd = [None, None]
            h2_rd2 = {}
            hps_rd2 = [None, None]
            ysc_rd = {}
            iy_w = [None, None]
            iu_rd = [None, None]
            u_rd2 = [None, None]
            utk_arr = [None, None]
            cp_arr = [None, None]
            utp_rdB = [None]

            def load_job(j):
                pj = j % 2
                kb.W(g, wg_rd[pj], wi_tok, *scat_all, *joinA)
                g.indirect_dma_start(
                    out=wg[:, pj].rearrange("p t c n -> p (t c n)"), out_offset=None, in_=w12f,
                    in_offset=IndirectOffsetOnAxis(ap=widx[:, j:j + 1], axis=0)).then_inc(kb.dwg[pj].sem, 16)
                kb.dwg[pj].n += 16
                g.indirect_dma_start(
                    out=eb1g[:, pj], out_offset=None, in_=eb1f,
                    in_offset=IndirectOffsetOnAxis(ap=widx[:, j:j + 1], axis=0)).then_inc(kb.dwg[pj].sem, 16)
                kb.dwg[pj].n += 16
                wtk = kb.dwg[pj].tok()
                kb.W(q, mt_rd[pj], wg_rd[pj], *scat_all, ej_store, *joinA)
                q.dma_start(out=mt[:, pj], in_=wbf["metad"][j * CAP:(j + 1) * CAP, :]
                            .rearrange("(c p) n -> p c n", p=128)).then_inc(kb.dmt[pj].sem, 16)
                kb.dmt[pj].n += 16
                mtk = kb.dmt[pj].inc(
                    q.dma_start(out=eb2g[0:1, pj], in_=wbf["eb2j"][j:j + 1, :]), 16)
                return wtk, mtk

            def issue_u_gather(j):
                pj = j % 2
                kb.W(v, jl[j][1], iu_rd[pj])
                v.tensor_copy(iu[:, pj, 0:1], mt[:, pj, 0, 2:3])
                iu_tok = vv.inc(v.tensor_copy(iu[:, pj, 1:2], mt[:, pj, 1, 2:3]))
                kb.W(g, iu_tok, u_rd2[pj])
                utk = None
                for cc in range(2):
                    utk = kb.dut[pj].inc(g.indirect_dma_start(
                        out=u_tm2[:, pj, cc], out_offset=None, in_=wbf["Ud"],
                        in_offset=IndirectOffsetOnAxis(ap=iu[:, pj, cc:cc + 1], axis=0)), 16)
                utk_arr[pj] = utk
                iu_rd[pj] = utk

            jl = {0: load_job(0), 1: load_job(1)}
            if STAGE == "B0":
                ctxb.close()
                octx.close()
                return
            issue_u_gather(0)
            issue_u_gather(1)
            for j in range(NJ):
                pj = j % 2
                wtk, mtk = jl[j]
                if STAGE == "BL":
                    if j + 2 <= NJ - 1:
                        jl[j + 2] = load_job(j + 2)
                    continue
                if 2 <= j + 1 < NJ:
                    issue_u_gather(j + 1)
                kb.W(t, utk_arr[pj], utp_rdB[0])
                cp = None
                for cc in range(2):
                    for dk in range(NK):
                        ii = t.transpose(utpB[:, cc, dk * 128:(dk + 1) * 128],
                                         u_tm2[:, pj, cc, dk * 128:(dk + 1) * 128],
                                         env["ident_bf"][:])
                    tr_tok = mm.inc(ii)
                    kb.W(v, tr_tok)
                    cp = vv.inc(v.tensor_copy(
                        ufm[:, pj, :, cc * 128:(cc + 1) * 128],
                        utpB[:, cc].rearrange("p (k f) -> p k f", f=128)))
                u_rd2[pj] = tr_tok
                utp_rdB[0] = cp
                cp_arr[pj] = cp
                kb.W(v, mtk, iy_w[pj])
                c_tok = vv.inc(v.tensor_copy(iy[:, pj, 0:1], mt[:, pj, 0, 1:2]))
                kb.W(v, c_tok)
                c_tok = vv.inc(v.tensor_scalar(mt[:, pj, 1, 2:3], mt[:, pj, 1, 1:2],
                                               SENT, None, ALU.is_ge))
                kb.W(v, c_tok)
                c_tok = vv.inc(v.tensor_add(mt[:, pj, 1, 2:3], mt[:, pj, 1, 2:3],
                                            mt[:, pj, 1, 1:2]))
                kb.W(v, c_tok)
                iy_tok = vv.inc(v.tensor_copy(iy[:, pj, 1:2], mt[:, pj, 1, 2:3]))
                for hk in range(NK):
                    sl2 = hk % 2
                    kb.W(t, wtk, cp_arr[pj], hps_rd2[sl2])
                    for dk in range(NK):
                        ii = t.matmul(hpsB[:, sl2 * 512:sl2 * 512 + CAP],
                                      wg[:, pj, 0, dk, hk * 128:(hk + 1) * 128],
                                      ufm[:, pj, dk], start=(dk == 0), stop=(dk == NK - 1))
                    w1tok = mm.inc(ii)
                    kb.W(s, w1tok, h2_rd2.get(hk % 4))
                    ii = s.activation(h2b[:, hk], hpsB[:, sl2 * 512:sl2 * 512 + CAP],
                                      AF.Relu, bias=eb1g[:, pj, hk:hk + 1])
                    h2tok = ev.inc(ii)
                    hps_rd2[sl2] = h2tok
                    h2_rd2[hk % 4] = h2tok
                for sc2 in (() if STAGE == "BW1" else (0, 1)):
                    for hf in range(2):
                        kb.W(t, h2_rd2[3], mtk, ysc_rd.get((sc2, hf)))
                        for hk in range(NK):
                            kb.W(t, h2_rd2[hk % 4])
                            ii = t.matmul(yps[:, sc2, hf],
                                          h2b[:, hk, sc2 * 128:(sc2 + 1) * 128],
                                          wg[:, pj, 1, hk, hf * 512:(hf + 1) * 512],
                                          start=(hk == 0), stop=False)
                        ii = t.matmul(yps[:, sc2, hf], env["ones_row"][:],
                                      eb2g[0:1, pj, hf * 512:(hf + 1) * 512],
                                      start=False, stop=True)
                        w2tok = mm.inc(ii)
                        h2_rd2[3] = w2tok
                        kb.W(v, w2tok, iy_tok)
                        ii = v.tensor_scalar(ysc[:, sc2, hf * 512:(hf + 1) * 512],
                                             yps[:, sc2, hf], mt[:, pj, sc2, 0:1], None, ALU.mult)
                        ysck = vv.inc(ii)
                        ysc_rd[(sc2, hf)] = None
                        if hf == 1:
                            kb.W(g, ysck)
                            yst = kb.dy[sc2].inc(g.indirect_dma_start(
                                out=wbf["R3d"],
                                out_offset=IndirectOffsetOnAxis(ap=iy[:, pj, sc2:sc2 + 1], axis=0),
                                in_=ysc[:, sc2], in_offset=None), 16)
                            ysc_rd[(sc2, 0)] = yst
                            ysc_rd[(sc2, 1)] = yst
                iy_w[pj] = mm.tok()
                wg_rd[pj] = mm.tok()
                mt_rd[pj] = kb.vv.tok()
                if j + 2 <= NJ - 1:
                    jl[j + 2] = load_job(j + 2)

        if STAGE == "B":
            octx.close()
            return
        with ExitStack() as ctxc:
            enc = ctxc.enter_context
            NPC = 4
            r3s = enc(nc.sbuf_tensor(None, [128, NPC, 3, D], BF16))
            rsum = enc(nc.sbuf_tensor(None, [128, 2, D], F32))
            xc = enc(nc.sbuf_tensor(None, [128, NPC, NK, 128], F32))
            rtp = enc(nc.psum_tensor(None, [128, 2, D], F32))
            r3_rd = [None] * NPC
            rs_rd = [None, None]
            xc_rd = [None] * NPC
            rtp_rd = [None, None]
            ydone = (kb.dy[0].tok(), kb.dy[1].tok())

            def loadC(ch):
                par = ch % NPC
                p2 = ch % 2
                eng = q if p2 == 0 else s
                kb.W(eng, r3_rd[par], xc_rd[par], *ydone)
                rt = None
                for ki in range(3):
                    rt = kb.dR[p2].inc(
                        eng.dma_start(out=r3s[:, par, ki],
                                      in_=wbf["R3d"][ki * T + ch * 128: ki * T + (ch + 1) * 128, :]), 16)
                xl = kb.dxp[p2].inc(
                    eng.dma_start(out=xc[:, par], in_=dst3[:, :, ch * 128:(ch + 1) * 128]), 16)
                return rt, xl

            pre = {c: loadC(c) for c in range(min(NPC, NSLAB))}
            for ch in range(NSLAB):
                par = ch % NPC
                p2 = ch % 2
                rt, xl = pre[ch]
                kb.W(v, rt, rs_rd[p2])
                c_tok = vv.inc(v.tensor_add(rsum[:, p2], r3s[:, par, 0], r3s[:, par, 1]))
                kb.W(v, c_tok)
                rs_tok = vv.inc(v.tensor_add(rsum[:, p2], rsum[:, p2], r3s[:, par, 2]))
                r3_rd[par] = rs_tok
                kb.W(t, rs_tok, rtp_rd[p2])
                for dk in range(NK):
                    ii = t.transpose(rtp[:, p2, dk * 128:(dk + 1) * 128],
                                     rsum[:, p2, dk * 128:(dk + 1) * 128], env["ident_f"][:])
                rtr_tok = mm.inc(ii)
                rs_rd[p2] = rtr_tok
                kb.W(v, rtr_tok, xl)
                for dk in range(NK):
                    ii = v.tensor_add(xc[:, par, dk], xc[:, par, dk],
                                      rtp[:, p2, dk * 128:(dk + 1) * 128])
                xa_tok = vv.inc(ii)
                rtp_rd[p2] = xa_tok
                seng = q if p2 == 0 else s
                kb.W(seng, xa_tok)
                kb.dst.n += 16
                xc_rd[par] = kb.dsp[p2].inc(
                    seng.dma_start(out=dst3[:, :, ch * 128:(ch + 1) * 128], in_=xc[:, par]), 16)
                if ch + NPC < NSLAB:
                    pre[ch + NPC] = loadC(ch + NPC)
        kb.W(q, kb.dsp[0].tok(), kb.dsp[1].tok())
    octx.close()


def emit_ffn(nc, kb, ap, wbf, src, dst, env, toks):
    t, v, s, g, q = kb.t, kb.v, kb.s, kb.g, kb.q
    mm, ev, vv = kb.mm, kb.ev, kb.vv
    bias = env["bias"]
    src3 = src.rearrange("(k p) n -> p k n", p=128)
    dst3 = dst.rearrange("(k p) n -> p k n", p=128)

    entry = [kb.dsp[0].tok(), kb.dsp[1].tok()]
    with ExitStack() as ctx:
        xt = ctx.enter_context(nc.sbuf_tensor(None, [128, 2, NK, FT], F32))
        xbf = ctx.enter_context(nc.sbuf_tensor(None, [128, 2, NK, FT], BF16))
        x2bf = ctx.enter_context(nc.sbuf_tensor(None, [128, NK, FT], BF16))
        u = ctx.enter_context(nc.sbuf_tensor(None, [128, NK, FT], BF16))
        t1 = ctx.enter_context(nc.sbuf_tensor(None, [128, NK, FT], F32))
        bcast = ctx.enter_context(nc.sbuf_tensor(None, [128, 2 * FT], F32))
        srow = ctx.enter_context(nc.sbuf_tensor(None, [1, 3 * FT], F32))
        srbf = ctx.enter_context(nc.sbuf_tensor(None, [1, 2 * FT], BF16))
        Lrt = ctx.enter_context(nc.sbuf_tensor(None, [NEXP, FT], F32))
        Ltm = ctx.enter_context(nc.sbuf_tensor(None, [128, 2, NEXP], F32))
        smx = ctx.enter_context(nc.sbuf_tensor(None, [128, 2, 4, NEXP], F32))
        smk = ctx.enter_context(nc.sbuf_tensor(None, [128, 2, NEXP], mybir.dt.int32))
        smr = ctx.enter_context(nc.sbuf_tensor(None, [128, 2, 8], F32))
        cw_tm = ctx.enter_context(nc.sbuf_tensor(None, [128, 2, NEXP], F32))
        cw_fm = ctx.enter_context(nc.sbuf_tensor(None, [NEXP, FT], BF16))
        cwb = ctx.enter_context(nc.sbuf_tensor(None, [128, NEXP, FT], BF16))
        h_bf = ctx.enter_context(nc.sbuf_tensor(None, [128, 4, FT], BF16))
        h2_bf = ctx.enter_context(nc.sbuf_tensor(None, [128, 4, FT], BF16))
        sw1r = ctx.enter_context(nc.sbuf_tensor(None, [128, NK, D], BF16))
        sw2r = ctx.enter_context(nc.sbuf_tensor(None, [128, NK, D], BF16))
        w1b = ctx.enter_context(nc.sbuf_tensor(None, [128, 2, NK, D], BF16))
        w2b = ctx.enter_context(nc.sbuf_tensor(None, [128, 2, NK, D], BF16))
        acc = ctx.enter_context(nc.psum_tensor(None, [128, NK * FT], F32))
        hps = ctx.enter_context(nc.psum_tensor(None, [128, 1024], F32))
        mA = ctx.enter_context(nc.psum_tensor(None, [128, 512], F32))
        mB = ctx.enter_context(nc.psum_tensor(None, [128, 512], F32))
        for eng in (t, v, s, g, q):
            kb.W(eng, *entry)
        kb.W(s, toks["init"], toks["bias"])
        sw_tok = None
        g.dma_start(out=sw1r[:], in_=ap["sW1"].rearrange("(c p) n -> p c n", p=128)).then_inc(kb.dg.sem, 16)
        kb.dg.n += 16
        sw_tok = kb.dg.inc(g.dma_start(out=sw2r[:], in_=ap["sW2"].rearrange("(c p) n -> p c n", p=128)), 16)

        accs = lambda i: acc[:, i * FT:(i + 1) * FT]
        hsl = lambda b: hps[:, b * 512:b * 512 + FT]
        w1t = lambda pe, dk, hk: w1b[:, pe, dk, hk * 128:(hk + 1) * 128]
        w2t = lambda pe, hk, ik: w2b[:, pe, hk, ik * 128:(ik + 1) * 128]
        s1t = lambda dk, hk: sw1r[:, dk, hk * 128:(hk + 1) * 128]
        s2t = lambda hk, ik: sw2r[:, hk, ik * 128:(ik + 1) * 128]
        m_row = srow[0:1, 0:FT]
        msq_row = srow[0:1, FT:2 * FT]
        var_row = srow[0:1, 2 * FT:3 * FT]
        rstd_bf = srbf[0:1, 0:FT]
        mrstd_bf = srbf[0:1, FT:2 * FT]

        wtok = [None, None]
        xtok = [None, None]
        xbtok = [None, None]
        x2tok = None
        srow_w = None
        srbf_w = None
        bc_rd = None
        u_rd = None
        ltm_w = None
        cw_rd = None
        cwtm_w = None
        cwb_rd = [None] * NEXP
        mA_free = None
        mB_free = None
        acc_free = None
        t1_rd = None
        h_rd = [None] * 4
        h2_rd = [None] * 4
        hps_rd = [None, None]

        def load_x(ti):
            p = ti % 2
            kb.W(q, xtok[p])
            xt_tok = kb.dxp[p].inc(q.dma_start(out=xt[:, p], in_=src3[:, :, ti * FT:(ti + 1) * FT]), 16)
            kb.W(g, xbtok[p])
            xb_tok = kb.dbp[p].inc(g.dma_start(out=xbf[:, p], in_=src3[:, :, ti * FT:(ti + 1) * FT]), 16)
            return xt_tok, xb_tok

        def load_exp(e, ti):
            pe = e % 2
            if ti == 0:
                kb.W(q, toks["cast"][e])
            kb.W(q, wtok[pe])
            kb.dwp[pe].inc(q.dma_start(out=w1b[:, pe], in_=wbf["eW1"][e]), 16)
            return kb.dwp[pe].inc(q.dma_start(out=w2b[:, pe], in_=wbf["eW2"][e]), 16)

        x_next = load_x(0)
        for ti in range(NFT):
            p = ti % 2
            x_tok, xb_tok = x_next
            wld = {0: load_exp(0, ti), 1: load_exp(1, ti)}

            kb.W(s, x_tok, x2tok)
            for k in range(NK):
                ii = s.activation(x2bf[:, k], xt[:, p, k], AF.Square)
            sq_tok = ev.inc(ii)

            kb.W(t, xb_tok, sq_tok, mA_free)
            for k in range(NK):
                t.matmul(mA[0:1, 0:FT], env["ones_col"][:], xbf[:, p, k],
                         start=(k == 0), stop=(k == NK - 1))
            for k in range(NK):
                ii = t.matmul(mA[0:1, FT:2 * FT], env["ones_col"][:], x2bf[:, k],
                              start=(k == 0), stop=(k == NK - 1))
            stats_tok = mm.inc(ii)
            xbtok[p] = stats_tok
            x2tok = stats_tok

            kb.W(v, stats_tok, srow_w)
            c_tok = vv.inc(v.tensor_scalar(m_row, mA[0:1, 0:FT], 1.0 / 1024, None, ALU.mult))
            kb.W(v, c_tok)
            c_tok = vv.inc(v.tensor_mul(msq_row, m_row, m_row))
            kb.W(v, c_tok)
            c_tok = vv.inc(v.scalar_tensor_tensor(var_row, mA[0:1, FT:2 * FT], 1.0 / 1024,
                                                  msq_row, ALU.mult, ALU.subtract))
            kb.W(s, c_tok)
            e_tok = ev.inc(s.activation(var_row, var_row, AF.Ln, bias=1e-5))
            kb.W(s, e_tok, srbf_w)
            e_tok = ev.inc(s.activation(rstd_bf, var_row, AF.Exp, scale=-0.5))
            kb.W(v, e_tok)
            c_tok = vv.inc(v.tensor_mul(mrstd_bf, m_row, rstd_bf))
            srow_w = c_tok

            kb.W(t, c_tok)
            t.matmul(mA[:, 0:FT], env["ones_row"][:], rstd_bf, start=True, stop=True)
            ii = t.matmul(mA[:, FT:2 * FT], env["ones_row"][:], mrstd_bf, start=True, stop=True)
            bc_tok = mm.inc(ii)
            srbf_w = bc_tok
            kb.W(s, bc_tok, bc_rd)
            s.copy(bcast[:, 0:FT], mA[:, 0:FT])
            bce_tok = ev.inc(s.copy(bcast[:, FT:2 * FT], mA[:, FT:2 * FT]))

            kb.W(v, bce_tok, u_rd[p], t1_rd[p])
            for k in range(NK):
                ii = v.tensor_mul(t1[:, p, k], xt[:, p, k], bcast[:, 0:FT])
            mul_tok = vv.inc(ii)
            kb.W(v, mul_tok)
            for k in range(NK):
                ii = v.tensor_sub(u[:, p, k], t1[:, p, k], bcast[:, FT:2 * FT])
            u_tok = vv.inc(ii)
            bc_rd = u_tok

            kb.W(t, u_tok, toks["rw"], mB_free)
            for k in range(NK):
                ii = t.matmul(mB[0:NEXP, 0:FT], env["rw_bf"][:, k * NEXP:(k + 1) * NEXP],
                              u[:, p, k], start=(k == 0), stop=(k == NK - 1))
            rt_tok = mm.inc(ii)
            kb.W(v, rt_tok)
            lrt_tok = vv.inc(v.tensor_scalar(Lrt[:], mB[0:NEXP, 0:FT], env["rb_s"][:], None, ALU.add))
            kb.W(t, lrt_tok, ltm_w)
            t.transpose(mB[:, FT:FT + NEXP], Lrt[:, 0:128], env["ident_f"][0:NEXP, 0:NEXP])
            ii = t.transpose(mB[:, FT + NEXP:FT + 2 * NEXP], Lrt[:, 128:256], env["ident_f"][0:NEXP, 0:NEXP])
            ltr_tok = mm.inc(ii)
            kb.W(v, ltr_tok)
            v.tensor_copy(Ltm[:, 0], mB[:, FT:FT + NEXP])
            ltm_tok = vv.inc(v.tensor_copy(Ltm[:, 1], mB[:, FT + NEXP:FT + 2 * NEXP]))
            ltm_w = ltm_tok

            P = lambda sl: smx[:, sl, 0]
            P1 = lambda sl: smx[:, sl, 1]
            P2 = lambda sl: smx[:, sl, 2]
            MK = lambda sl: smk[:, sl]
            m1n = lambda sl: smr[:, sl, 0:1]
            m1p = lambda sl: smr[:, sl, 1:2]
            m2r = lambda sl: smr[:, sl, 2:3]
            m3r = lambda sl: smr[:, sl, 3:4]
            rr = lambda sl: smr[:, sl, 4:5]

            kb.W(v, ltm_tok, toks["init"])
            for sl in (0, 1):
                ii = v.reduce_max(m1n(sl), Ltm[:, sl], axis=AX.X, negate=True)
            c_tok = vv.inc(ii)
            kb.W(s, c_tok)
            for sl in (0, 1):
                ii = s.activation(P(sl), Ltm[:, sl], AF.Exp, bias=m1n(sl))
            e_tok = ev.inc(ii)
            kb.W(v, e_tok)
            for sl in (0, 1):
                v.reduce_sum(rr(sl), P(sl), axis=AX.X)
            for sl in (0, 1):
                ii = v.reduce_max(m1p(sl), P(sl), axis=AX.X)
            c_tok = vv.inc(ii)
            kb.W(v, c_tok)
            for sl in (0, 1):
                ii = v.reciprocal(rr(sl), rr(sl))
            c_tok = vv.inc(ii)
            kb.W(v, c_tok)
            for sl in (0, 1):
                ii = v.tensor_scalar(MK(sl), P(sl), m1p(sl), None, ALU.is_lt)
            c_tok = vv.inc(ii)
            kb.W(v, c_tok)
            for sl in (0, 1):
                ii = v.select(P1(sl), MK(sl), P(sl), env["negone"][:, 0:NEXP])
            c_tok = vv.inc(ii)
            kb.W(v, c_tok)
            for sl in (0, 1):
                ii = v.reduce_max(m2r(sl), P1(sl), axis=AX.X)
            c_tok = vv.inc(ii)
            kb.W(v, c_tok)
            for sl in (0, 1):
                ii = v.tensor_scalar(MK(sl), P1(sl), m2r(sl), None, ALU.is_lt)
            c_tok = vv.inc(ii)
            kb.W(v, c_tok)
            for sl in (0, 1):
                ii = v.select(P2(sl), MK(sl), P1(sl), env["negone"][:, 0:NEXP])
            c_tok = vv.inc(ii)
            kb.W(v, c_tok)
            for sl in (0, 1):
                ii = v.reduce_max(m3r(sl), P2(sl), axis=AX.X)
            c_tok = vv.inc(ii)
            kb.W(v, c_tok)
            for sl in (0, 1):
                ii = v.tensor_scalar(MK(sl), P(sl), m3r(sl), None, ALU.is_ge)
            c_tok = vv.inc(ii)
            kb.W(v, c_tok)
            for sl in (0, 1):
                ii = v.tensor_mul(P1(sl), P(sl), MK(sl))
            c_tok = vv.inc(ii)
            kb.W(v, c_tok, cwtm_w)
            for sl in (0, 1):
                ii = v.tensor_scalar(cw_tm[:, sl], P1(sl), rr(sl), None, ALU.mult)
            cw_tok = vv.inc(ii)

            kb.W(t, cw_tok)
            t.transpose(mB[0:NEXP, 0:128], cw_tm[:, 0], env["ident_f"][:])
            ii = t.transpose(mB[0:NEXP, 128:256], cw_tm[:, 1], env["ident_f"][:])
            cwt_tok = mm.inc(ii)
            cwtm_w = cwt_tok
            kb.W(v, cwt_tok, cw_rd)
            v.tensor_copy(cw_fm[:, 0:128], mB[0:NEXP, 0:128])
            cwf_tok = vv.inc(v.tensor_copy(cw_fm[:, 128:256], mB[0:NEXP, 128:256]))

            cwb_ev = [None] * NEXP
            for e in range(NEXP):
                dst_ps = mA if e % 2 == 0 else mB
                if e == 0:
                    kb.W(t, cwf_tok, bce_tok)
                elif e == 1:
                    kb.W(t, cwf_tok)
                else:
                    kb.W(t, cwb_ev[e - 2])
                ii = t.matmul(dst_ps[:, 0:FT], env["sel"][:, e * 128:(e + 1) * 128],
                              cw_fm[:], start=True, stop=True)
                cb_tok = mm.inc(ii)
                kb.W(s, cb_tok, cwb_rd[e])
                cwb_ev[e] = ev.inc(s.copy(cwb[:, e], dst_ps[:, 0:FT]))
            mA_free = cwb_ev[14]
            mB_free = cwb_ev[13]

            relu_ev = {}
            h2_vv = {}
            w2_mm = {}
            order = [NEXP] + list(range(NEXP))
            seq = [(e, j) for e in order for j in range(NK)]

            def w1_group(idx):
                e, j = seq[idx]
                ps_slot = idx % 2
                h_slot = idx % 4
                pe = e % 2
                if e < NEXP:
                    kb.W(t, wld[e], hps_rd[ps_slot])
                else:
                    kb.W(t, sw_tok, hps_rd[ps_slot])
                for dk in range(NK):
                    lt = w1t(pe, dk, j) if e < NEXP else s1t(dk, j)
                    ii = t.matmul(hsl(ps_slot), lt, u[:, dk], start=(dk == 0), stop=(dk == NK - 1))
                tok = mm.inc(ii)
                kb.W(s, tok, h_rd[h_slot])
                b = bias["eb1"](e, j) if e < NEXP else bias["sb1"](j)
                ii = s.activation(h_bf[:, h_slot], hsl(ps_slot), AF.Relu, bias=b)
                relu_ev[idx] = ev.inc(ii)
                hps_rd[ps_slot] = relu_ev[idx]
                if e < NEXP:
                    kb.W(v, relu_ev[idx], h2_rd[h_slot])
                    ii = v.tensor_mul(h2_bf[:, h_slot], h_bf[:, h_slot], cwb[:, e])
                    h2_vv[idx] = vv.inc(ii)
                    h_rd[h_slot] = h2_vv[idx]
                    cwb_rd[e] = h2_vv[idx]

            def w2_group(idx, first):
                e, j = seq[idx]
                h_slot = idx % 4
                pe = e % 2
                if e < NEXP:
                    kb.W(t, h2_vv[idx])
                    rhs = h2_bf[:, h_slot]
                else:
                    kb.W(t, relu_ev[idx])
                    rhs = h_bf[:, h_slot]
                if first:
                    kb.W(t, acc_free)
                for ik in range(NK):
                    lt = w2t(pe, j, ik) if e < NEXP else s2t(j, ik)
                    ii = t.matmul(accs(ik), lt, rhs, start=(first and ik % 2 == 0), stop=False)
                tok = mm.inc(ii)
                w2_mm[idx] = tok
                if e < NEXP:
                    h2_rd[h_slot] = tok
                else:
                    h_rd[h_slot] = tok

            for idx in range(len(seq)):
                w1_group(idx)
                if idx >= 2:
                    w2_group(idx - 2, first=(idx == 2))
                e, j = seq[idx]
                if j == NK - 1 and e < NEXP and e + 2 < NEXP:
                    wtok[(e + 2) % 2] = mm.tok()
                    wld[e + 2] = load_exp(e + 2, ti)
                if idx == len(seq) - 4 and ti + 1 < NFT:
                    x_next2 = load_x(ti + 1)
            w2_group(len(seq) - 2, first=False)
            w2_group(len(seq) - 1, first=False)
            wtok[0] = w2_mm[len(seq) - 1]
            wtok[1] = w2_mm[(1 + 13) * NK + NK - 1]

            kb.W(t, toks["rw"], cwf_tok)
            for ik in range(NK):
                ii = t.matmul(accs(ik), env["eb2_bf"][:, ik * 128:(ik + 1) * 128],
                              cw_fm[:], start=False, stop=(ik % 2 == 1))
            acc_tok = mm.inc(ii)
            cw_rd = acc_tok
            u_rd = acc_tok

            kb.W(v, acc_tok, t1_rd)
            for ik in range(NK):
                ii = v.tensor_scalar(t1[:, ik], accs(ik), bias["sb2"](ik), 0.5, ALU.add, ALU.mult)
            fe_tok = vv.inc(ii)
            acc_free = fe_tok
            kb.W(v, fe_tok)
            for ik in range(NK):
                ii = v.tensor_add(xt[:, p, ik], xt[:, p, ik], t1[:, ik])
            add_tok = vv.inc(ii)
            t1_rd = add_tok
            kb.W(q, add_tok)
            kb.dst.n += 16
            xtok[p] = kb.dsp[p].inc(q.dma_start(out=dst3[:, :, ti * FT:(ti + 1) * FT], in_=xt[:, p]), 16)
            if ti + 1 < NFT:
                x_next = x_next2

        kb.W(q, kb.dsp[0].tok(), kb.dsp[1].tok())


def emit_attn(nc, kb, ap, wbf, src, dst, env, toks):
    t, v, s, g, q = kb.t, kb.v, kb.s, kb.g, kb.q
    mm, ev, vv, pp = kb.mm, kb.ev, kb.vv, kb.pp
    bias = env["bias"]
    src3 = src.rearrange("(k p) n -> p k n", p=128)
    dst3 = dst.rearrange("(k p) n -> p k n", p=128)
    A = 512
    ST = 256
    NG = T // A

    entry = [kb.dsp[0].tok(), kb.dsp[1].tok()]
    with ExitStack() as ctx:
        NXP = 3
        xt = ctx.enter_context(nc.sbuf_tensor(None, [128, NXP, NK, ST], F32))
        xbf = ctx.enter_context(nc.sbuf_tensor(None, [128, 2, NK, ST], BF16))
        x2b = ctx.enter_context(nc.sbuf_tensor(None, [128, NK, ST], BF16))
        ug = ctx.enter_context(nc.sbuf_tensor(None, [128, NK, A], BF16))
        t1 = ctx.enter_context(nc.sbuf_tensor(None, [128, NK, ST], BF16))
        bcg = ctx.enter_context(nc.sbuf_tensor(None, [128, 2 * ST], F32))
        srow = ctx.enter_context(nc.sbuf_tensor(None, [1, 3 * ST], F32))
        srbf = ctx.enter_context(nc.sbuf_tensor(None, [1, 2 * ST], BF16))
        Qp = ctx.enter_context(nc.sbuf_tensor(None, [128, NK, A], BF16))
        Kp = ctx.enter_context(nc.sbuf_tensor(None, [128, NK, A], BF16))
        Vb = ctx.enter_context(nc.sbuf_tensor(None, [128, NK, A], BF16))
        Ob = ctx.enter_context(nc.sbuf_tensor(None, [128, NK, A], BF16))
        phia = ctx.enter_context(nc.sbuf_tensor(None, [128, 2, A], F32))
        phir = ctx.enter_context(nc.sbuf_tensor(None, [128, 2, A], F32))
        Ktm = ctx.enter_context(nc.sbuf_tensor(None, [128, 4, D], BF16))
        Vtm = ctx.enter_context(nc.sbuf_tensor(None, [128, 4, D], BF16))
        Ab = ctx.enter_context(nc.sbuf_tensor(None, [128, 2, 128], BF16))
        Ssb = ctx.enter_context(nc.sbuf_tensor(None, [128, NK, D], F32))
        Sbf = ctx.enter_context(nc.sbuf_tensor(None, [128, NK, D], BF16))
        Zc = ctx.enter_context(nc.sbuf_tensor(None, [128, NK], F32))
        Zs = ctx.enter_context(nc.sbuf_tensor(None, [128, NK], F32))
        Zbf = ctx.enter_context(nc.sbuf_tensor(None, [128, NK], BF16))
        dens = ctx.enter_context(nc.sbuf_tensor(None, [1, 2, ST], F32))
        rden = ctx.enter_context(nc.sbuf_tensor(None, [1, 2, ST], F32))
        rdbf = ctx.enter_context(nc.sbuf_tensor(None, [1, 2, ST], BF16))
        rdb = ctx.enter_context(nc.sbuf_tensor(None, [128, 2, ST], F32))
        wst = ctx.enter_context(nc.sbuf_tensor(None, [128, 2, NK, D], BF16))
        pA = ctx.enter_context(nc.psum_tensor(None, [128, 512], F32))
        pB = ctx.enter_context(nc.psum_tensor(None, [128, 512], F32))
        nump = ctx.enter_context(nc.psum_tensor(None, [128, NK, ST], F32))

        for eng in (t, v, s, g, q):
            kb.W(eng, *entry)
        kb.W(s, toks["init"], toks["bias"])

        smem = vv.inc(v.memset(Ssb[:], 0.0))
        zmem = vv.inc(v.memset(Zs[:], 0.0))

        wmat = lambda pe, ck, ok: wst[:, pe, ck, ok * 128:(ok + 1) * 128]
        wseq = ("wqT", "wkT", "wvT", "woT")
        wtok = [None, None]

        def load_w(gi, wi):
            pe = (gi * 4 + wi) % 2
            nm = wseq[wi]
            if gi == 0:
                kb.W(q, toks["cast"]["woT"])
            kb.W(q, wtok[pe])
            return pe, kb.dwp[pe].inc(q.dma_start(
                out=wst[:, pe], in_=wbf[nm].rearrange("(c p) n -> p c n", p=128)), 16)

        xbtok = [None, None]
        x2tok = None
        srow_w = None
        srbf_w = None
        bc_rd = None
        u_rd = None
        t1_rd = None
        pa_rd = None
        pb_rd = None
        num_rd = None
        qkv_rd = None
        Ab_rd = [None, None]
        O_rd = None
        tm_rd = None
        Sbf_w = None
        zbf_rd = None
        phi_rds = [None, None]
        rdb_rd = [None, None]
        ptr_kv_rd = None

        xtok = [None] * 4
        def load_x(sti):
            p = sti % NXP
            pb2 = sti % 2
            kb.W(q, xtok[p])
            a_tok = kb.dxp[pb2].inc(q.dma_start(out=xt[:, p], in_=src3[:, :, sti * ST:(sti + 1) * ST]), 16)
            kb.W(g, xbtok[pb2])
            b_tok = kb.dbp[pb2].inc(g.dma_start(out=xgb_in(pb2), in_=src3[:, :, sti * ST:(sti + 1) * ST]), 16)
            return a_tok, b_tok

        def xgb_in(p):
            return xbf[:, p]

        x_next = load_x(0)
        x_next2 = None
        for gi in range(NG):
            wld = {0: load_w(gi, 0), 1: load_w(gi, 1)}

            u_toks = []
            for sti2 in range(2):
                sti = gi * 2 + sti2
                p = sti % NXP
                pb2 = sti % 2
                if sti2 == 0:
                    x_tok, xb_tok = x_next
                else:
                    x_tok, xb_tok = x_next2
                if sti2 == 0:
                    x_next2 = load_x(sti + 1)
                kb.W(s, x_tok, x2tok)
                for k in range(NK):
                    ii = s.activation(x2b[:, k], xt[:, p, k], AF.Square)
                sq_tok = ev.inc(ii)
                kb.W(t, xb_tok, sq_tok, pa_rd, pb_rd)
                for k in range(NK):
                    t.matmul(pA[0:1, 0:ST], env["ones_col"][:], xbf[:, pb2, k],
                             start=(k == 0), stop=(k == NK - 1))
                for k in range(NK):
                    ii = t.matmul(pA[0:1, ST:2 * ST], env["ones_col"][:], x2b[:, k],
                                  start=(k == 0), stop=(k == NK - 1))
                stats_tok = mm.inc(ii)
                xbtok[pb2] = stats_tok
                x2tok = stats_tok

                m_row = srow[0:1, 0:ST]
                msq_row = srow[0:1, ST:2 * ST]
                var_row = srow[0:1, 2 * ST:3 * ST]
                rstd_bf = srbf[0:1, 0:ST]
                mrstd_bf = srbf[0:1, ST:2 * ST]
                kb.W(v, stats_tok, srow_w)
                c_tok = vv.inc(v.tensor_scalar(m_row, pA[0:1, 0:ST], 1.0 / 1024, None, ALU.mult))
                kb.W(v, c_tok)
                c_tok = vv.inc(v.tensor_mul(msq_row, m_row, m_row))
                kb.W(v, c_tok)
                c_tok = vv.inc(v.scalar_tensor_tensor(var_row, pA[0:1, ST:2 * ST], 1.0 / 1024,
                                                      msq_row, ALU.mult, ALU.subtract))
                kb.W(s, c_tok)
                e_tok = ev.inc(s.activation(var_row, var_row, AF.Ln, bias=1e-5))
                kb.W(s, e_tok, srbf_w)
                e_tok = ev.inc(s.activation(rstd_bf, var_row, AF.Exp, scale=-0.5))
                kb.W(v, e_tok)
                c_tok = vv.inc(v.tensor_mul(mrstd_bf, m_row, rstd_bf))
                srow_w = c_tok
                kb.W(t, c_tok)
                t.matmul(pB[:, 0:ST], env["ones_row"][:], rstd_bf, start=True, stop=True)
                ii = t.matmul(pB[:, ST:2 * ST], env["ones_row"][:], mrstd_bf, start=True, stop=True)
                bc_tok = mm.inc(ii)
                srbf_w = bc_tok
                pa_rd = c_tok
                kb.W(s, bc_tok, bc_rd)
                s.copy(bcg[:, 0:ST], pB[:, 0:ST])
                bce_tok = ev.inc(s.copy(bcg[:, ST:2 * ST], pB[:, ST:2 * ST]))
                pb_rd = bce_tok
                kb.W(v, bce_tok, u_rd, t1_rd)
                for k in range(NK):
                    ii = v.tensor_mul(t1[:, k], xt[:, p, k], bcg[:, 0:ST])
                mul_tok = vv.inc(ii)
                kb.W(v, mul_tok)
                for k in range(NK):
                    ii = v.tensor_sub(ug[:, k, sti2 * ST:(sti2 + 1) * ST], t1[:, k],
                                      bcg[:, ST:2 * ST])
                u_tok = vv.inc(ii)
                bc_rd = u_tok
                t1_rd = u_tok
                u_toks.append(u_tok)

            rot = [pA[:, 0:A], pB[:, 0:A], nump[:, 0:2, :], nump[:, 2:4, :]]
            rfree = [pa_rd, pb_rd, num_rd, num_rd]
            np_i = [0]
            NROT = 4

            def proj(wi, dst_sb, bias_fn, nbias_fn, is_v):
                nonlocal qkv_rd
                pe, wl = wld[wi]
                for ok in range(NK):
                    sl = np_i[0] % NROT
                    ps = rot[sl]
                    pp2 = np_i[0] % 2
                    kb.W(t, u_toks[0], u_toks[1], wl, rfree[sl])
                    for dk in range(NK):
                        ii = t.matmul(ps, wmat(pe, dk, ok), ug[:, dk],
                                      start=(dk == 0), stop=(dk == NK - 1))
                    ptok = mm.inc(ii)
                    if is_v:
                        kb.W(v, ptok, qkv_rd)
                        eetok = vv.inc(v.tensor_scalar(dst_sb[:, ok], ps,
                                                       bias_fn(ok), None, ALU.add))
                    else:
                        kb.W(s, ptok, phi_rds[pp2])
                        a1 = ev.inc(s.activation(phia[:, pp2], ps, AF.Relu,
                                                 scale=-1.0, bias=nbias_fn(ok)))
                        kb.W(s, a1)
                        ev.inc(s.activation(phia[:, pp2], phia[:, pp2], AF.Exp, scale=-1.0))
                        rtok = ev.inc(s.activation(phir[:, pp2], ps, AF.Relu, bias=bias_fn(ok)))
                        kb.W(v, rtok, qkv_rd)
                        eetok = vv.inc(v.tensor_add(dst_sb[:, ok], phia[:, pp2], phir[:, pp2]))
                        phi_rds[pp2] = eetok
                    rfree[sl] = eetok
                    np_i[0] += 1
                return eetok

            qdone = proj(0, Qp, bias["bq"], bias["nbq"], False)
            wtok[wld[0][0]] = mm.tok()
            wld[2] = load_w(gi, 2)
            kdone = proj(1, Kp, bias["bk"], bias["nbk"], False)
            wtok[wld[1][0]] = mm.tok()
            wld[3] = load_w(gi, 3)
            vdone = proj(2, Vb, bias["bv"], None, True)
            wtok[wld[2][0]] = mm.tok()
            pa_rd, pb_rd = rfree[0], rfree[1]
            num_rd = rfree[3] if rfree[3] is not None else num_rd
            kb.W(t, rfree[2])

            with nc.psum_tensor("ptr_%d" % gi, [128, D], BF16, allow_name_mangling=True) as ptr:
                kb.W(t, kdone, vdone, ptr_kv_rd, tm_rd)
                ttoks = []
                for ci in range(4):
                    csl = slice(ci * CH, (ci + 1) * CH)
                    for srcb, dstb in ((Kp, Ktm), (Vb, Vtm)):
                        if ttoks:
                            kb.W(t, ttoks[-1])
                        for dk in range(NK):
                            ii = t.transpose(ptr[:, dk * 128:(dk + 1) * 128],
                                             srcb[:, dk, csl], env["ident_bf"][:])
                        tr_tok = mm.inc(ii)
                        kb.W(v, tr_tok)
                        ttoks.append(vv.inc(v.tensor_copy(dstb[:, ci], ptr[:])))
                tm_done = ttoks[-1]

            numsl = lambda ek, lo, w: nump[:, ek, lo:lo + w]
            den_ps = lambda sti2: pB[0:1, sti2 * ST:(sti2 + 1) * ST]
            abctx = ExitStack()
            _ab0 = abctx.enter_context(nc.psum_tensor("abps0_%d" % gi, [128, 512], F32, allow_name_mangling=True))
            _ab1 = abctx.enter_context(nc.psum_tensor("abps1_%d" % gi, [128, 512], F32, allow_name_mangling=True))
            abps = [_ab0, _ab1]
            den_defer = []
            o_toks = [None, None]
            for sti2 in range(2):
                sti = gi * 2 + sti2
                ssl = slice(sti2 * 2 * CH, (sti2 + 1) * 2 * CH)

                kb.W(t, num_rd)
                if gi > 0:
                    kb.W(t, Sbf_w)
                    for dk in range(NK):
                        for ek in range(NK):
                            ii = t.matmul(numsl(ek, 0, ST),
                                          Sbf[:, dk, ek * 128:(ek + 1) * 128],
                                          Qp[:, dk, ssl],
                                          start=(dk == 0 and ek % 2 == 0), stop=False)

                kb.W(t, tm_done, qdone)
                abm_toks = {}
                den_started = [False]
                first_den_wait = [True]
                for li, c in enumerate((sti2 * 2, sti2 * 2 + 1)):
                    for m in range(c + 1):
                        slot = len(abm_toks) % 2
                        abm_toks[len(abm_toks)] = 1
                        csl_c = slice(c * CH, (c + 1) * CH)
                        csl_m = slice(m * CH, (m + 1) * CH)
                        kb.W(t, Ab_rd[slot])
                        for dk in range(NK):
                            ii = t.matmul(abps[slot][:, 0:128], Kp[:, dk, csl_m], Qp[:, dk, csl_c],
                                          start=(dk == 0), stop=(dk == NK - 1))
                        ablk = mm.inc(ii)
                        kb.W(v, ablk, toks["init"])
                        if m == c:
                            abe = vv.inc(v.tensor_mul(Ab[:, slot], abps[slot][:, 0:128], env["triu"][:]))
                        else:
                            abe = vv.inc(v.tensor_copy(Ab[:, slot], abps[slot][:, 0:128]))
                        lo = (c % 2) * 128
                        if first_den_wait[0]:
                            kb.W(t, abe, pb_rd, zbf_rd)
                            first_den_wait[0] = False
                        else:
                            kb.W(t, abe)
                        dst_flag = not den_started[0]
                        den_started[0] = True
                        t.matmul(den_ps(sti2)[:, lo:lo + 128], env["ones_col"][:], Ab[:, slot],
                                 start=dst_flag, stop=(gi == 0 and li == 1 and m == c))
                        av_start = (gi == 0) and (m == 0)
                        av_stop = (m == c) and (gi == 0 or li == 1)
                        for ek in range(NK):
                            ii = t.matmul(numsl(ek, lo, 128), Vtm[:, m, ek * 128:(ek + 1) * 128],
                                          Ab[:, slot], start=(av_start and ek % 2 == 0),
                                          stop=(av_stop and ek % 2 == 1))
                        av_tok = mm.inc(ii)
                        Ab_rd[slot] = av_tok
                if gi > 0:
                    kb.W(t, zbf_rd)
                    for dk in range(NK):
                        ii = t.matmul(den_ps(sti2)[:], Zbf[:, dk:dk + 1], Qp[:, dk, ssl],
                                      start=False, stop=(dk == NK - 1))
                    den_tok = mm.inc(ii)
                else:
                    den_tok = mm.tok()
                num_done = mm.tok()

                kb.W(s, num_done, O_rd)
                ev_ii = None
                for ek in range(NK):
                    ev_ii = s.copy(Ob[:, ek, ssl], nump[:, ek, :])
                evac_tok = ev.inc(ev_ii)
                num_rd = evac_tok
                den_defer.append((sti2, ssl, den_tok, evac_tok))

            abctx.close()
            if gi + 1 < NG:
                x_next = load_x((gi + 1) * 2)
            scan_mm = mm.tok()
            with nc.psum_tensor("kvq_%d" % gi, [128, 2, 512], F32, allow_name_mangling=True) as kvq:
                kvfree = [None, None]
                inter_last = num_done if gi > 0 else None
                for dk in range(NK):
                    for hf in range(2):
                        slot = (dk * 2 + hf) % 2
                        kb.W(t, tm_done, kvfree[slot])
                        for ci in range(4):
                            ii = t.matmul(kvq[:, slot],
                                          Ktm[:, ci, dk * 128:(dk + 1) * 128],
                                          Vtm[:, ci, hf * 512:(hf + 1) * 512],
                                          start=(ci == 0), stop=(ci == 3))
                        kv_tok = mm.inc(ii)
                        kb.W(v, kv_tok, smem)
                        ii = v.tensor_add(Ssb[:, dk, hf * 512:(hf + 1) * 512],
                                          Ssb[:, dk, hf * 512:(hf + 1) * 512],
                                          kvq[:, slot])
                        sadd = vv.inc(ii)
                        kvfree[slot] = sadd
                        kb.W(s, sadd)
                        ii = s.copy(Sbf[:, dk, hf * 512:(hf + 1) * 512],
                                    Ssb[:, dk, hf * 512:(hf + 1) * 512])
                        ev.inc(ii)
                Sbf_w = kb.ev.tok()
                ptr_kv_rd = kvfree[0]
                ptr_kv_rd = kb.vv.tok()
            tm_rd = mm.tok()

            for sti2, ssl, den_tok, evac_tok in den_defer:
                kb.W(s, den_tok)
                de_tok = ev.inc(s.activation(dens[0:1, sti2], den_ps(sti2)[:],
                                             AF.Copy, bias=1e-6))
                pb_rd = de_tok
                kb.W(v, de_tok)
                r_tok = vv.inc(v.reciprocal(rden[0:1, sti2], dens[0:1, sti2]))
                kb.W(v, r_tok)
                r_tok = vv.inc(v.tensor_copy(rdbf[0:1, sti2], rden[0:1, sti2]))
                kb.W(t, r_tok, pa_rd)
                ii = t.matmul(pA[:, sti2 * ST:(sti2 + 1) * ST], env["ones_row"][:],
                              rdbf[0:1, sti2], start=True, stop=True)
                rb_tok = mm.inc(ii)
                kb.W(s, rb_tok, rdb_rd[sti2])
                rdb_tok = ev.inc(s.copy(rdb[:, sti2], pA[:, sti2 * ST:(sti2 + 1) * ST]))
                pa_rd = rdb_tok
                kb.W(v, rdb_tok, evac_tok)
                for ek in range(NK):
                    ii = v.tensor_mul(Ob[:, ek, ssl], Ob[:, ek, ssl], rdb[:, sti2])
                o_toks[sti2] = vv.inc(ii)
                rdb_rd[sti2] = o_toks[sti2]

            kb.W(v, kdone, zmem)
            for dk in range(NK):
                ii = v.reduce_sum(Zc[:, dk:dk + 1], Kp[:, dk, :], axis=AX.X)
            zr_tok = vv.inc(ii)
            kb.W(v, zr_tok)
            za_tok = vv.inc(v.tensor_add(Zs[:], Zs[:], Zc[:]))
            zmem = za_tok
            kb.W(s, za_tok, zbf_rd, scan_mm)
            zb_tok = ev.inc(s.copy(Zbf[:], Zs[:]))
            zbf_rd = zb_tok

            pe3, wl3 = wld[3]
            for sti2 in range(2):
                sti = gi * 2 + sti2
                p = sti % NXP
                ssl = slice(sti2 * 2 * CH, (sti2 + 1) * 2 * CH)
                wo_ev = [None] * NK
                for jk in range(NK):
                    sl = jk % 2
                    ps = rot[sl]
                    kb.W(t, wl3, o_toks[sti2], pa_rd if sl == 0 else pb_rd)
                    if jk >= 2:
                        kb.W(t, wo_ev[jk - 2])
                    for ek in range(NK):
                        ii = t.matmul(ps[:, 0:ST], wmat(pe3, ek, jk), Ob[:, ek, ssl],
                                      start=(ek == 0), stop=(ek == NK - 1))
                    wotok = mm.inc(ii)
                    kb.W(v, wotok)
                    ii = v.scalar_tensor_tensor(xt[:, p, jk], ps[:, 0:ST], bias["bo"](jk),
                                                xt[:, p, jk], ALU.add, ALU.add)
                    wo_ev[jk] = vv.inc(ii)
                pa_rd = wo_ev[NK - 2]
                pb_rd = wo_ev[NK - 1]
                add_tok = wo_ev[NK - 1]
                kb.W(q, add_tok)
                kb.dst.n += 16
                xtok[p] = kb.dsp[sti % 2].inc(
                    q.dma_start(out=dst3[:, :, sti * ST:(sti + 1) * ST], in_=xt[:, p]), 16)
            womm = mm.tok()
            O_rd = womm
            u_rd = womm
            qkv_rd = womm
            wtok[pe3] = womm

        kb.W(q, kb.dsp[0].tok(), kb.dsp[1].tok())


def _stage(inputs):
    c = lambda a: np.ascontiguousarray(np.asarray(a, dtype=np.float32))
    base = {
        "wqT": c(np.asarray(inputs["L0_attn_Wq"]).T), "wkT": c(np.asarray(inputs["L0_attn_Wk"]).T),
        "wvT": c(np.asarray(inputs["L0_attn_Wv"]).T), "woT": c(np.asarray(inputs["L0_attn_Wo"]).T),
        "sW1": c(inputs["L0_ffn_sW1"]), "sW2": c(inputs["L0_ffn_sW2"]),
        "eW1": c(np.asarray(inputs["L0_ffn_eW1"]).reshape(NEXP, NK, 128, D)
                 .transpose(0, 2, 1, 3)),
        "eW2": c(np.asarray(inputs["L0_ffn_eW2"]).reshape(NEXP, NK, 128, D)
                 .transpose(0, 2, 1, 3)),
        "rW": c(np.asarray(inputs["L0_ffn_router_W"]).reshape(NK, 128, NEXP)
                .transpose(1, 0, 2).reshape(128, NK * NEXP)),
        "rb": c(np.asarray(inputs["L0_ffn_router_b"]).reshape(NEXP, 1)),
        "eb1": c(np.asarray(inputs["L0_ffn_eb1"]).reshape(NEXP, NK, 128)
                 .transpose(2, 0, 1).reshape(128, NEXP * NK)),
        "eb1sp": c(np.asarray(inputs["L0_ffn_eb1"]).reshape(NEXP, NK, 128)
                   .transpose(0, 2, 1).reshape(NEXP * 128, NK)),
        "eb2": c(inputs["L0_ffn_eb2"]),
    }
    for nm, key in (("bq", "L0_attn_bq"), ("bk", "L0_attn_bk"), ("bv", "L0_attn_bv"),
                    ("bo", "L0_attn_bo"), ("sb1", "L0_ffn_sb1"), ("sb2", "L0_ffn_sb2")):
        base[nm] = c(np.asarray(inputs[key]).reshape(NK, 128).T)
    xs = c(np.asarray(inputs["L0_attn_x"]).transpose(0, 2, 1))
    return base, xs


def run_device(inputs, phases=("f1", "at", "f2")):
    base, xs = _stage(inputs)
    nc = build_program(phases)
    in_maps = [dict(base, X=xs[i]) for i in range(8)]
    res = run_bass_kernel_spmd(nc, in_maps, core_ids=list(range(8)))
    Y = np.stack([np.asarray(r["Y"]) for r in res.results])
    return np.ascontiguousarray(Y.transpose(0, 2, 1))


def kernel(**inputs):
    return run_device(inputs)


# version 12
# speedup vs baseline: 1.0036x; 1.0036x over previous
from contextlib import ExitStack
import numpy as np
import ml_dtypes
import concourse.bass as bass
import concourse.mybir as mybir
from concourse.bass_utils import run_bass_kernel_spmd
from concourse.bass_core_rust import IndirectOffsetOnAxis

F32 = mybir.dt.float32
BF16 = mybir.dt.bfloat16
AF = mybir.ActivationFunctionType
ALU = mybir.AluOpType
AX = mybir.AxisListType

D = 1024
T = 4096
NK = 8
NEXP = 15
FT = 256
NFT = T // FT
CH = 128
NCH = T // CH
AG = 512
NAG = T // AG
BF = ml_dtypes.bfloat16
SPARSE = True
STAGE = "full"
DEBUG_DUMP = False


class Ctr:
    def __init__(self, nc, name):
        self.sem = nc.alloc_semaphore(name=name)
        self.n = 0

    def inc(self, inst, k=1):
        inst.then_inc(self.sem, k)
        self.n += k
        return (self, self.n)

    def tok(self):
        return (self, self.n)


class KB:
    def __init__(self, nc):
        self.nc = nc
        self.t = nc.tensor
        self.v = nc.vector
        self.s = nc.scalar
        self.g = nc.gpsimd
        self.q = nc.sync
        self.seen = {}
        self.mm = Ctr(nc, "mm")
        self.ev = Ctr(nc, "ev")
        self.vv = Ctr(nc, "vv")
        self.pp = Ctr(nc, "pp")
        self.dq = Ctr(nc, "dq")
        self.dg = Ctr(nc, "dg")
        self.dxp = [Ctr(nc, "dx0"), Ctr(nc, "dx1")]
        self.dbp = [Ctr(nc, "db0"), Ctr(nc, "db1")]
        self.dwp = [Ctr(nc, "dw0"), Ctr(nc, "dw1")]
        self.dsp = [Ctr(nc, "ds0"), Ctr(nc, "ds1")]
        self.dst = Ctr(nc, "dst")
        self.ecast = [Ctr(nc, f"ec{i}") for i in range(NEXP)]
        self.acast = Ctr(nc, "acast")
        self.dU = [Ctr(nc, "dUa"), Ctr(nc, "dUb")]
        self.dug = [Ctr(nc, "dug0"), Ctr(nc, "dug1")]
        self.dsc = Ctr(nc, "dsc")
        self.dscp = [Ctr(nc, "dscp0"), Ctr(nc, "dscp1")]
        self.dmcp = [Ctr(nc, "dmcp0"), Ctr(nc, "dmcp1")]
        self.dwg = [Ctr(nc, "dwg0"), Ctr(nc, "dwg1")]
        self.dbg = [Ctr(nc, "dbg0"), Ctr(nc, "dbg1")]
        self.dmt = [Ctr(nc, "dmt0"), Ctr(nc, "dmt1")]
        self.dut = [Ctr(nc, "dut0"), Ctr(nc, "dut1")]
        self.dy = [Ctr(nc, "dya"), Ctr(nc, "dyb")]
        self.dR = [Ctr(nc, "dR0"), Ctr(nc, "dR1")]
        self.dsen = Ctr(nc, "dsen")

    def W(self, eng, *toks):
        for tk in toks:
            if tk is None:
                continue
            ctr, val = tk
            if val <= 0:
                continue
            key = (id(eng), id(ctr))
            if self.seen.get(key, -1) >= val:
                continue
            self.seen[key] = val
            eng.wait_ge(ctr.sem, val)


def _sel_const():
    sel = np.zeros((NEXP, NEXP * 128), dtype=BF)
    for e in range(NEXP):
        sel[e, e * 128:(e + 1) * 128] = 1
    return sel


def build_program(phases=("f1", "at", "f2")):
    nc = bass.Bass("TRN2", target_bir_lowering=False, debug=False)
    kb = KB(nc)
    t, v, s, g, q = kb.t, kb.v, kb.s, kb.g, kb.q

    ap = {}
    def din(name, shape, dtype=F32):
        ap[name] = nc.dram_tensor(name, list(shape), dtype, kind="ExternalInput").ap()
        return ap[name]

    X = din("X", (D, T))
    for nm in ("wqT", "wkT", "wvT", "woT", "sW1", "sW2"):
        din(nm, (D, D))
    din("eW1", (NEXP, 128, NK, D))
    din("eW2", (NEXP, 128, NK, D))
    din("eb1sp", (NEXP * 128, NK))
    din("rW", (128, NK * NEXP))
    din("rb", (NEXP, 1))
    for nm in ("bq", "bk", "bv", "bo", "sb1", "sb2"):
        din(nm, (128, NK))
    din("eb1", (128, NEXP * NK))
    din("eb2", (NEXP, D))
    Y = nc.dram_tensor("Y", [D, T], F32, kind="ExternalOutput").ap()

    X1 = nc.dram_tensor("X1", [D, T], F32, kind="Internal").ap()
    X2 = nc.dram_tensor("X2", [D, T], F32, kind="Internal").ap()
    wbf = {}
    for nm in ("wqT", "wkT", "wvT", "woT"):
        wbf[nm] = nc.dram_tensor(nm + "_bf", [D, D], BF16, kind="Internal").ap()
    eW12 = nc.dram_tensor("eW12_bf", [NEXP, 128, 2, NK, D], BF16, kind="Internal").ap()
    wbf["eW12"] = eW12
    wbf["eW1"] = eW12[:, :, 0]
    wbf["eW2"] = eW12[:, :, 1]
    wbf["eb2d"] = nc.dram_tensor("eb2_bfd", [NEXP, D], BF16, kind="Internal").ap()
    NSLOT = 63 * 256
    wbf["Ud"] = nc.dram_tensor("Ud", [T, D], BF16, kind="Internal").ap()
    wbf["metad"] = nc.dram_tensor("metad", [NSLOT, 16], F32, kind="Internal").ap()
    wbf["R3d"] = nc.dram_tensor("R3d", [3 * T + 2, D], BF16, kind="Internal").ap()
    wbf["eb2j"] = nc.dram_tensor("eb2j", [64, D], BF16, kind="Internal").ap()

    ident_bf_d = nc.inline_tensor(np.eye(128, dtype=BF), name="c_ident_bf").ap()
    ident_f_d = nc.inline_tensor(np.eye(128, dtype=np.float32), name="c_ident_f").ap()
    triu_d = nc.inline_tensor(np.triu(np.ones((128, 128), dtype=np.float32)).astype(BF),
                              name="c_triu").ap()
    sel_d = nc.inline_tensor(_sel_const(), name="c_sel").ap()
    ones_col_d = nc.inline_tensor(np.ones((128, 1), dtype=BF), name="c_ones_col").ap()
    ones_row_d = nc.inline_tensor(np.ones((1, 128), dtype=BF), name="c_ones_row").ap()
    iota128c_d = nc.inline_tensor(np.arange(128, dtype=np.float32).reshape(128, 1), name="c_iota128c").ap()
    iota15b_d = nc.inline_tensor(np.tile(np.arange(NEXP, dtype=np.float32), (128, 1)), name="c_iota15b").ap()
    iota64r_d = nc.inline_tensor(np.arange(64, dtype=np.float32).reshape(1, 64), name="c_iota64r").ap()
    ones15f_d = nc.inline_tensor(np.ones((NEXP, 1), dtype=np.float32), name="c_ones15f").ap()
    ident15f_d = nc.inline_tensor(np.eye(NEXP, dtype=np.float32), name="c_ident15f").ap()

    if DEBUG_DUMP:
        wbf["dbg"] = nc.dram_tensor("DBG", [128, 132], F32, kind="ExternalOutput").ap()
    pio = {"f1": [X, X1], "at": [X1, X2], "f2": [X2, Y]}
    pio[phases[-1]][1] = Y

    with (
        nc.sbuf_tensor("consts", [128, 8], F32) as consts,
        nc.sbuf_tensor("negone", [128, 16], F32) as negone,
        nc.sbuf_tensor("ones_col", [128, 1], BF16) as ones_col,
        nc.sbuf_tensor("ones_row", [1, 128], BF16) as ones_row,
        nc.sbuf_tensor("ident_bf", [128, 128], BF16) as ident_bf,
        nc.sbuf_tensor("ident_f", [128, 128], F32) as ident_f,
        nc.sbuf_tensor("triu", [128, 128], BF16) as triu,
        nc.sbuf_tensor("sel", [NEXP, NEXP * 128], BF16) as sel,
        nc.sbuf_tensor("rw_bf", [128, NK * NEXP], BF16) as rw_bf,
        nc.sbuf_tensor("rb_s", [NEXP, 1], F32) as rb_s,
        nc.sbuf_tensor("eb2_bf", [NEXP, D], BF16) as eb2_bf,
        nc.sbuf_tensor("bcols", [128, 8 * NK], F32) as bcols,
        nc.sbuf_tensor("eb1_s", [128, NEXP * NK], F32) as eb1_s,
        nc.sbuf_tensor("iota128c", [128, 1], F32) as iota128c,
        nc.sbuf_tensor("iota15b", [128, NEXP], F32) as iota15b,
        nc.sbuf_tensor("iota64r", [1, 64], F32) as iota64r,
        nc.sbuf_tensor("ones15f", [NEXP, 1], F32) as ones15f,
        nc.sbuf_tensor("ident15f", [NEXP, NEXP], F32) as ident15f,
    ):
        for val, idx in ((0.5, 0), (-0.5, 1), (1.0 / 1024, 2), (1e-5, 3), (1e-6, 4)):
            nc.const_aps.register(F32, val, consts[:, idx:idx + 1])

        col = lambda buf, i, k: buf[:, i * NK + k:i * NK + k + 1]
        bias = dict(
            bq=lambda k: col(bcols, 0, k), bk=lambda k: col(bcols, 1, k),
            bv=lambda k: col(bcols, 2, k), bo=lambda k: col(bcols, 3, k),
            sb1=lambda k: col(bcols, 4, k), sb2=lambda k: col(bcols, 5, k),
            nbq=lambda k: col(bcols, 6, k), nbk=lambda k: col(bcols, 7, k),
            eb1=lambda e, k: eb1_s[:, e * NK + k:e * NK + k + 1],
        )

        for val, idx in ((0.5, 0), (-0.5, 1), (1.0 / 1024, 2), (1e-5, 3), (1e-6, 4)):
            v.memset(consts[:, idx:idx + 1], float(val))
        init_vv = kb.vv.inc(v.memset(negone[:], -1.0))

        ld_tok = None
        for dst_t, src_ap in (
            (ones_col[:], ones_col_d), (ones_row[:], ones_row_d),
            (ident_bf[:], ident_bf_d), (ident_f[:], ident_f_d), (triu[:], triu_d),
            (sel[:], sel_d), (rb_s[:], ap["rb"]), (eb1_s[:], ap["eb1"]),
            (iota128c[:], iota128c_d), (iota15b[:], iota15b_d),
            (iota64r[:], iota64r_d), (ones15f[:], ones15f_d), (ident15f[:], ident15f_d),
        ):
            ld_tok = kb.dq.inc(q.dma_start(out=dst_t, in_=src_ap), 16)
        for i, nm in enumerate(("bq", "bk", "bv", "bo", "sb1", "sb2")):
            ld_tok = kb.dq.inc(q.dma_start(out=bcols[:, i * NK:(i + 1) * NK], in_=ap[nm]), 16)

        g.dma_start(out=rw_bf[:], in_=ap["rW"]).then_inc(kb.dbg[0].sem, 16)
        kb.dbg[0].n += 16
        rw_tok = kb.dbg[0].inc(g.dma_start(out=eb2_bf[:], in_=ap["eb2"]), 16)

        kb.W(v, ld_tok)
        v.tensor_scalar(bcols[:, 6 * NK:7 * NK], bcols[:, 0 * NK:1 * NK], -1.0, None, ALU.mult)
        bias_tok = kb.vv.inc(
            v.tensor_scalar(bcols[:, 7 * NK:8 * NK], bcols[:, 1 * NK:2 * NK], -1.0, None, ALU.mult))

        cast_tok = {}
        cast_jobs = []
        eb2d_tok = kb.dbg[1].inc(g.dma_start(out=wbf["eb2d"], in_=ap["eb2"]), 16)

        def mk_ecast(e):
            def run():
                kb.ecast[e].inc(g.dma_start(out=wbf["eW1"][e], in_=ap["eW1"][e]), 16)
                cast_tok[e] = kb.ecast[e].inc(g.dma_start(out=wbf["eW2"][e], in_=ap["eW2"][e]), 16)
            return run

        def mk_acast(nm):
            def run():
                cast_tok[nm] = kb.acast.inc(g.dma_start(out=wbf[nm], in_=ap[nm]), 16)
            return run
        for e in range(NEXP):
            cast_jobs.append(mk_ecast(e))
        for nm in ("wqT", "wkT", "wvT", "woT"):
            cast_jobs.append(mk_acast(nm))

        toks = dict(bias=bias_tok, rw=rw_tok, init=init_vv, cast=cast_tok,
                    eb2d=eb2d_tok, cast_jobs=cast_jobs)
        env = dict(consts=consts, negone=negone, ones_col=ones_col,
                   ones_row=ones_row, ident_bf=ident_bf, ident_f=ident_f,
                   triu=triu, sel=sel, rw_bf=rw_bf, rb_s=rb_s, eb2_bf=eb2_bf,
                   bias=bias, iota128c=iota128c, iota15b=iota15b,
                   iota64r=iota64r, ones15f=ones15f, ident15f=ident15f)

        for ph in phases:
            src, dstp = pio[ph]
            if ph in ("f1", "f2"):
                (emit_ffn_sparse if SPARSE else emit_ffn)(nc, kb, ap, wbf, src, dstp, env, toks)
            else:
                while cast_jobs:
                    cast_jobs.pop(0)()
                emit_attn(nc, kb, ap, wbf, src, dstp, env, toks)

        for ctr in (kb.dq, kb.dg, kb.acast, *kb.ecast, *kb.dxp, *kb.dbp,
                    *kb.dwp, *kb.dsp, *kb.dU, *kb.dug, kb.dsc, *kb.dscp, *kb.dmcp, *kb.dwg,
                    *kb.dbg, *kb.dmt, *kb.dut, *kb.dy, *kb.dR, kb.dsen):
            kb.W(q, ctr.tok())
    return nc




def emit_ffn_sparse(nc, kb, ap, wbf, src, dst, env, toks):
    t, v, s, g, q = kb.t, kb.v, kb.s, kb.g, kb.q
    mm, ev, vv = kb.mm, kb.ev, kb.vv
    bias = env["bias"]
    src3 = src.rearrange("(k p) n -> p k n", p=128)
    dst3 = dst.rearrange("(k p) n -> p k n", p=128)
    NJ = 63
    CAP = 256
    NSLOT = NJ * CAP
    SENT = float(3 * T)
    NSLAB = T // 128
    w12f = wbf["eW12"].rearrange("e p t c n -> (e p) (t c n)")
    eb1f = ap["eb1sp"]
    entry = [kb.dsp[0].tok(), kb.dsp[1].tok(), kb.dy[0].tok(), kb.dy[1].tok(), kb.dR[0].tok(), kb.dR[1].tok()]
    octx = ExitStack()
    widx = octx.enter_context(nc.sbuf_tensor(None, [128, 64], mybir.dt.int32))
    with ExitStack() as ctx:
        en = ctx.enter_context
        xt = en(nc.sbuf_tensor(None, [128, 2, NK, FT], F32))
        xbf = en(nc.sbuf_tensor(None, [128, 2, NK, FT], BF16))
        x2b = en(nc.sbuf_tensor(None, [128, NK, FT], BF16))
        u = en(nc.sbuf_tensor(None, [128, 2, NK, FT], BF16))
        t1 = en(nc.sbuf_tensor(None, [128, 2, NK, FT], F32))
        bcast = en(nc.sbuf_tensor(None, [128, 2 * FT], F32))
        srow = en(nc.sbuf_tensor(None, [1, 3 * FT], F32))
        srbf = en(nc.sbuf_tensor(None, [1, 2 * FT], BF16))
        Lrt = en(nc.sbuf_tensor(None, [NEXP, FT], F32))
        Ltm = en(nc.sbuf_tensor(None, [128, 2, NEXP], F32))
        smx = en(nc.sbuf_tensor(None, [128, 2, 4, NEXP], F32))
        smk = en(nc.sbuf_tensor(None, [128, 2, NEXP], mybir.dt.int32))
        smr = en(nc.sbuf_tensor(None, [128, 2, 8], F32))
        keepm = en(nc.sbuf_tensor(None, [128, 2, NEXP], F32))
        wmtm = en(nc.sbuf_tensor(None, [128, 2, NEXP], F32))
        u_tm = en(nc.sbuf_tensor(None, [128, 2, D], BF16))
        sw1r = en(nc.sbuf_tensor(None, [128, NK, D], BF16))
        sw2r = en(nc.sbuf_tensor(None, [128, NK, D], BF16))
        wmfm = en(nc.sbuf_tensor(None, [NEXP, 3, T], BF16))
        ecols = en(nc.sbuf_tensor(None, [128, NSLAB, 4], F32))
        ccols = en(nc.sbuf_tensor(None, [128, NSLAB, 4], F32))
        msum = en(nc.sbuf_tensor(None, [15, T], F32))
        rankf = en(nc.sbuf_tensor(None, [15, T], F32))
        prodf = en(nc.sbuf_tensor(None, [15, T], F32))
        slotw = en(nc.sbuf_tensor(None, [1, T], F32))
        r15c = en(nc.sbuf_tensor(None, [15, 4], F32))
        rA = en(nc.sbuf_tensor(None, [1, 64], F32))
        rB = en(nc.sbuf_tensor(None, [1, 64], F32))
        rC = en(nc.sbuf_tensor(None, [1, 64], F32))
        rI = en(nc.sbuf_tensor(None, [1, 64], mybir.dt.int32))
        rBf = en(nc.sbuf_tensor(None, [1, 64], BF16))
        ecall = en(nc.sbuf_tensor(None, [128, 64], F32))
        ej32 = en(nc.sbuf_tensor(None, [64, 1], mybir.dt.int32))
        eb2all = en(nc.sbuf_tensor(None, [64, D], BF16))
        sentf = en(nc.sbuf_tensor(None, [128, 1024], F32))
        mcall = en(nc.sbuf_tensor(None, [128, 3, 32, 16], F32))
        slc32 = en(nc.sbuf_tensor(None, [128, 2, 32], mybir.dt.int32))
        acc = en(nc.psum_tensor(None, [128, NK * FT], F32))
        mA = en(nc.psum_tensor(None, [128, 512], F32))
        mB = en(nc.psum_tensor(None, [128, 512], F32))

        for eng in (t, v, s, g, q):
            kb.W(eng, *entry)
        kb.W(s, toks["init"], toks["bias"])

        g.dma_start(out=sw1r[:], in_=ap["sW1"].rearrange("(c p) n -> p c n", p=128)).then_inc(kb.dg.sem, 16)
        kb.dg.n += 16
        sw_tok = kb.dg.inc(g.dma_start(out=sw2r[:], in_=ap["sW2"].rearrange("(c p) n -> p c n", p=128)), 16)

        accs = lambda i: acc[:, i * FT:(i + 1) * FT]
        s1t = lambda dk, hk: sw1r[:, dk, hk * 128:(hk + 1) * 128]
        s2t = lambda hk, ik: sw2r[:, hk, ik * 128:(ik + 1) * 128]
        m_row = srow[0:1, 0:FT]
        msq_row = srow[0:1, FT:2 * FT]
        var_row = srow[0:1, 2 * FT:3 * FT]
        rstd_bf = srbf[0:1, 0:FT]
        mrstd_bf = srbf[0:1, FT:2 * FT]

        xtok = [None, None]
        xbtok = [None, None]
        x2tok = None
        srow_w = None
        srbf_w = None
        bc_rd = None
        u_rd = [None, None]
        t1_rd = [None, None]
        mA_free = None
        mB_free = None
        acc_free = None
        hps_rd = [None, None]
        utp_rd = None
        utm_rd = [None, None]
        ltm_w = None
        cwtm_w = None
        wm_rd = None

        def mk_fill(ki3, sb3):
            def run():
                if (ki3, sb3) == (0, 0):
                    kb.W(g, kb.dq.tok(), kb.dmcp[0].tok(), kb.dmcp[1].tok())
                g.tensor_scalar(mcall[:, ki3, sb3, 1:2], env["iota128c"][:],
                                float(ki3 * T + sb3 * 128), None, ALU.add)
                kb.pp.inc(g.tensor_scalar(mcall[:, ki3, sb3, 2:3], env["iota128c"][:],
                                          float(sb3 * 128), None, ALU.add))
            return run
        fill_jobs = [mk_fill(ki3, sb3) for ki3 in range(3) for sb3 in range(NSLAB)]
        smem0 = kb.vv.inc(v.memset(sentf[:], SENT))
        kb.W(q, smem0)
        sen_tok = None
        meta3 = wbf["metad"].rearrange("(a p) n -> p a n", p=128)
        for i in range(2):
            sen_tok = kb.dsen.inc(
                q.dma_start(out=meta3[:, i * 63:(i + 1) * 63, :],
                            in_=sentf[:, 0:63 * 16].rearrange("p (a n) -> p a n", n=16)), 16)

        def load_x(ti):
            p = ti % 2
            kb.W(q, xtok[p])
            a_tok = kb.dxp[p].inc(q.dma_start(out=xt[:, p], in_=src3[:, :, ti * FT:(ti + 1) * FT]), 16)
            kb.W(g, xbtok[p])
            b_tok = kb.dbp[p].inc(g.dma_start(out=xbf[:, p], in_=src3[:, :, ti * FT:(ti + 1) * FT]), 16)
            return a_tok, b_tok

        x_next = load_x(0)
        for ti in range(NFT):
            p = ti % 2
            x_tok, xb_tok = x_next
            if ti + 1 < NFT:
                x_next = load_x(ti + 1)
            if toks["cast_jobs"]:
                toks["cast_jobs"].pop(0)()
            for _ in range(7):
                if fill_jobs:
                    fill_jobs.pop(0)()

            kb.W(s, x_tok, x2tok)
            for k in range(NK):
                ii = s.activation(x2b[:, k], xt[:, p, k], AF.Square)
            sq_tok = ev.inc(ii)
            kb.W(t, xb_tok, sq_tok, mA_free)
            for k in range(NK):
                t.matmul(mA[0:1, 0:FT], env["ones_col"][:], xbf[:, p, k],
                         start=(k == 0), stop=(k == NK - 1))
            for k in range(NK):
                ii = t.matmul(mA[0:1, FT:2 * FT], env["ones_col"][:], x2b[:, k],
                              start=(k == 0), stop=(k == NK - 1))
            stats_tok = mm.inc(ii)
            xbtok[p] = stats_tok
            kb.W(v, stats_tok, srow_w)
            c_tok = vv.inc(v.tensor_scalar(m_row, mA[0:1, 0:FT], 1.0 / 1024, None, ALU.mult))
            kb.W(v, c_tok)
            c_tok = vv.inc(v.tensor_mul(msq_row, m_row, m_row))
            kb.W(v, c_tok)
            c_tok = vv.inc(v.scalar_tensor_tensor(var_row, mA[0:1, FT:2 * FT], 1.0 / 1024,
                                                  msq_row, ALU.mult, ALU.subtract))
            kb.W(s, c_tok)
            e_tok = ev.inc(s.activation(var_row, var_row, AF.Ln, bias=1e-5))
            kb.W(s, e_tok, srbf_w)
            e_tok = ev.inc(s.activation(rstd_bf, var_row, AF.Exp, scale=-0.5))
            kb.W(v, e_tok)
            c_tok = vv.inc(v.tensor_mul(mrstd_bf, m_row, rstd_bf))
            srow_w = c_tok
            kb.W(t, c_tok)
            t.matmul(mA[:, 0:FT], env["ones_row"][:], rstd_bf, start=True, stop=True)
            ii = t.matmul(mA[:, FT:2 * FT], env["ones_row"][:], mrstd_bf, start=True, stop=True)
            bc_tok = mm.inc(ii)
            srbf_w = bc_tok
            kb.W(s, bc_tok, bc_rd)
            s.copy(bcast[:, 0:FT], mA[:, 0:FT])
            bce_tok = ev.inc(s.copy(bcast[:, FT:2 * FT], mA[:, FT:2 * FT]))
            mA_free = bce_tok
            kb.W(v, bce_tok, u_rd[p], t1_rd[p])
            for k in range(NK):
                ii = v.tensor_mul(t1[:, p, k], xt[:, p, k], bcast[:, 0:FT])
            mul_tok = vv.inc(ii)
            kb.W(v, mul_tok)
            for k in range(NK):
                ii = v.tensor_sub(u[:, p, k], t1[:, p, k], bcast[:, FT:2 * FT])
            u_tok = vv.inc(ii)
            bc_rd = u_tok

            with nc.psum_tensor("utp%d" % (ti % 2), [128, D], BF16, allow_name_mangling=True) as utp:
                for cc in range(2):
                    kb.W(t, u_tok, utp_rd)
                    for dk in range(NK):
                        ii = t.transpose(utp[:, dk * 128:(dk + 1) * 128],
                                         u[:, p, dk, cc * 128:(cc + 1) * 128], env["ident_bf"][:])
                    tr_tok = mm.inc(ii)
                    kb.W(v, tr_tok, utm_rd[cc])
                    uc_tok = vv.inc(v.tensor_copy(u_tm[:, cc], utp[:]))
                    utp_rd = uc_tok
                    kb.W(q, uc_tok)
                    utm_rd[cc] = kb.dU[cc].inc(
                        q.dma_start(out=wbf["Ud"][ti * FT + cc * 128: ti * FT + (cc + 1) * 128, :],
                                    in_=u_tm[:, cc]), 16)

            kb.W(t, u_tok, toks["rw"], mB_free)
            for k in range(NK):
                ii = t.matmul(mB[0:NEXP, 0:FT], env["rw_bf"][:, k * NEXP:(k + 1) * NEXP],
                              u[:, p, k], start=(k == 0), stop=(k == NK - 1))
            rt_tok = mm.inc(ii)
            kb.W(v, rt_tok)
            lrt_tok = vv.inc(v.tensor_scalar(Lrt[:], mB[0:NEXP, 0:FT], env["rb_s"][:], None, ALU.add))
            kb.W(t, lrt_tok, ltm_w)
            t.transpose(mB[:, FT:FT + NEXP], Lrt[:, 0:128], env["ident_f"][0:NEXP, 0:NEXP])
            ii = t.transpose(mB[:, FT + NEXP:FT + 2 * NEXP], Lrt[:, 128:256],
                             env["ident_f"][0:NEXP, 0:NEXP])
            ltr_tok = mm.inc(ii)
            kb.W(v, ltr_tok)
            v.tensor_copy(Ltm[:, 0], mB[:, FT:FT + NEXP])
            ltm_tok = vv.inc(v.tensor_copy(Ltm[:, 1], mB[:, FT + NEXP:FT + 2 * NEXP]))
            ltm_w = ltm_tok

            P = lambda sl: smx[:, sl, 0]
            P1 = lambda sl: smx[:, sl, 1]
            P2 = lambda sl: smx[:, sl, 2]
            WMT = lambda sl: smx[:, sl, 3]
            MK = lambda sl: smk[:, sl]
            m1n = lambda sl: smr[:, sl, 0:1]
            m1p = lambda sl: smr[:, sl, 1:2]
            m2r = lambda sl: smr[:, sl, 2:3]
            m3r = lambda sl: smr[:, sl, 3:4]
            rr = lambda sl: smr[:, sl, 4:5]
            ek_c = lambda sl, k: ecols[:, ti * 2 + sl, k:k + 1]
            cw_c = lambda sl, k: ccols[:, ti * 2 + sl, k:k + 1]

            kb.W(v, ltm_tok, toks["init"])
            for sl in (0, 1):
                ii = v.reduce_max(m1n(sl), Ltm[:, sl], axis=AX.X, negate=True)
            c_tok = vv.inc(ii)
            kb.W(s, c_tok)
            for sl in (0, 1):
                ii = s.activation(P(sl), Ltm[:, sl], AF.Exp, bias=m1n(sl))
            e_tok = ev.inc(ii)
            kb.W(v, e_tok)
            for sl in (0, 1):
                v.reduce_sum(rr(sl), P(sl), axis=AX.X)
            for sl in (0, 1):
                ii = v.reduce_max(m1p(sl), P(sl), axis=AX.X)
            c_tok = vv.inc(ii)
            kb.W(v, c_tok)
            for sl in (0, 1):
                ii = v.reciprocal(rr(sl), rr(sl))
            c_tok = vv.inc(ii)
            kb.W(v, c_tok)
            for sl in (0, 1):
                ii = v.tensor_scalar(WMT(sl), P(sl), m1p(sl), None, ALU.is_ge)
            c_tok = vv.inc(ii)
            kb.W(v, c_tok)
            for sl in (0, 1):
                ii = v.scalar_tensor_tensor(P1(sl), WMT(sl), -2.0, P(sl), ALU.mult, ALU.add)
            c_tok = vv.inc(ii)
            kb.W(v, c_tok)
            for sl in (0, 1):
                ii = v.reduce_max(m2r(sl), P1(sl), axis=AX.X)
            c_tok = vv.inc(ii)
            kb.W(v, c_tok)
            for sl in (0, 1):
                ii = v.tensor_scalar(WMT(sl), P1(sl), m2r(sl), None, ALU.is_ge)
            c_tok = vv.inc(ii)
            kb.W(v, c_tok)
            for sl in (0, 1):
                ii = v.scalar_tensor_tensor(P2(sl), WMT(sl), -2.0, P1(sl), ALU.mult, ALU.add)
            c_tok = vv.inc(ii)
            kb.W(v, c_tok)
            for sl in (0, 1):
                ii = v.reduce_max(m3r(sl), P2(sl), axis=AX.X)
            top_tok = vv.inc(ii)

            way_tok = top_tok
            for ki, mk in ((0, m1p), (1, m2r), (2, m3r)):
                kb.W(v, way_tok)
                for sl in (0, 1):
                    ii = v.tensor_scalar(wmtm[:, sl], P(sl), mk(sl), None, ALU.is_equal)
                wt1 = vv.inc(ii)
                kb.W(v, wt1)
                for sl in (0, 1):
                    ii = v.tensor_mul(WMT(sl), wmtm[:, sl], env["iota15b"][:])
                wt2 = vv.inc(ii)
                kb.W(v, wt2)
                for sl in (0, 1):
                    ii = v.reduce_sum(ek_c(sl, ki), WMT(sl), axis=AX.X)
                vv.inc(ii)
                for sl in (0, 1):
                    ii = v.tensor_scalar(cw_c(sl, ki), mk(sl), rr(sl), 0.5, ALU.mult, ALU.mult)
                wt3 = vv.inc(ii)
                kb.W(t, wt1, cwtm_w)
                for sl in (0, 1):
                    ii = t.transpose(mB[0:NEXP, 256 + sl * 128: 256 + (sl + 1) * 128],
                                     wmtm[:, sl], env["ident_f"][:])
                wt4 = mm.inc(ii)
                kb.W(v, wt4)
                for sl in (0, 1):
                    stile = ti * 2 + sl
                    ii = v.tensor_copy(wmfm[:, ki, stile * 128:(stile + 1) * 128],
                                       mB[0:NEXP, 256 + sl * 128:256 + (sl + 1) * 128])
                way_tok = vv.inc(ii)
                cwtm_w = way_tok
            wm_rd = way_tok
            mB_free = way_tok

            with nc.psum_tensor("hps%d" % (ti % 2), [128, 1024], F32, allow_name_mangling=True) as hps:
                hsl = lambda b: hps[:, b * 512:b * 512 + FT]
                h_rd = mm.tok()
                for j in range(NK):
                    slot = j % 2
                    kb.W(t, sw_tok, u_tok, hps_rd[slot], utp_rd)
                    for dk in range(NK):
                        ii = t.matmul(hsl(slot), s1t(dk, j), u[:, p, dk],
                                      start=(dk == 0), stop=(dk == NK - 1))
                    w1tok = mm.inc(ii)
                    kb.W(s, w1tok)
                    ii = s.activation(x2b[:, j], hsl(slot), AF.Relu, bias=bias["sb1"](j))
                    rl_tok = ev.inc(ii)
                    hps_rd[slot] = rl_tok
                    kb.W(t, rl_tok)
                    if j == 0:
                        kb.W(t, acc_free)
                    for ik in range(NK):
                        ii = t.matmul(accs(ik), s2t(j, ik), x2b[:, j],
                                      start=(j == 0 and ik % 2 == 0),
                                      stop=(j == NK - 1 and ik % 2 == 1))
                    mm.inc(ii)
            x2tok = mm.tok()

            acc_tok = mm.tok()
            kb.W(v, acc_tok, t1_rd[p])
            for ik in range(NK):
                ii = v.tensor_scalar(t1[:, p, ik], accs(ik), bias["sb2"](ik), 0.5, ALU.add, ALU.mult)
            fe_tok = vv.inc(ii)
            acc_free = fe_tok
            kb.W(v, fe_tok)
            for ik in range(NK):
                ii = v.tensor_add(xt[:, p, ik], xt[:, p, ik], t1[:, p, ik])
            add_tok = vv.inc(ii)
            t1_rd[p] = add_tok
            u_rd[p] = mm.tok()
            kb.W(q, add_tok)
            kb.dst.n += 16
            xtok[p] = kb.dsp[p].inc(q.dma_start(out=dst3[:, :, ti * FT:(ti + 1) * FT], in_=xt[:, p]), 16)

        while toks["cast_jobs"]:
            toks["cast_jobs"].pop(0)()
        while fill_jobs:
            fill_jobs.pop(0)()
        static_fill = kb.pp.tok()

        if STAGE == "A":
            ctx.close()
            octx.close()
            return
        kb.W(v, wm_rd)
        c_tok = vv.inc(v.tensor_add(msum[:], wmfm[:, 0, :], wmfm[:, 1, :]))
        kb.W(v, c_tok)
        ms_tok = vv.inc(v.tensor_add(msum[:], msum[:], wmfm[:, 2, :]))
        kb.W(v, ms_tok)
        cnt_tok = vv.inc(v.reduce_sum(r15c[:, 0:1], msum[:], axis=AX.X))
        kb.W(t, cnt_tok, mA_free)
        ii = t.transpose(mA[0:1, 0:NEXP], r15c[:, 0:1], env["ident15f"][:])
        ct_tok = mm.inc(ii)
        kb.W(v, ct_tok)
        c_tok = vv.inc(v.tensor_copy(rA[0:1, 0:NEXP], mA[0:1, 0:NEXP]))
        kb.W(v, c_tok)
        c_tok = vv.inc(v.memset(rB[0:1, 0:NEXP], 0.0))
        for m2 in range(16):
            kb.W(v, c_tok)
            c_tok = vv.inc(v.tensor_scalar(rC[0:1, 0:NEXP], rA[0:1, 0:NEXP],
                                           float(256 * m2), None, ALU.is_gt))
            kb.W(v, c_tok)
            c_tok = vv.inc(v.tensor_add(rB[0:1, 0:NEXP], rB[0:1, 0:NEXP], rC[0:1, 0:NEXP]))
        kb.W(v, c_tok)
        c_tok = vv.inc(v.memset(rC[0:1, 0:NEXP], 0.0))
        kb.W(v, c_tok)
        c_tok = vv.inc(v.tensor_tensor_scan(rA[0:1, 0:NEXP], rB[0:1, 0:NEXP],
                                            rC[0:1, 0:NEXP], 0.0, ALU.add, ALU.add))
        kb.W(v, c_tok)
        jb_tok = vv.inc(v.tensor_sub(rA[0:1, 0:NEXP], rA[0:1, 0:NEXP], rB[0:1, 0:NEXP]))
        kb.W(v, jb_tok)
        c_tok = vv.inc(v.tensor_scalar(rC[0:1, 0:NEXP], rA[0:1, 0:NEXP], 256.0, None, ALU.mult))
        kb.W(t, c_tok)
        ii = t.transpose(mA[0:NEXP, 256:257], rC[0:1, 0:NEXP], env["ident_f"][0:1, 0:1])
        sb_tok = mm.inc(ii)
        kb.W(v, sb_tok)
        sbc_tok = vv.inc(v.tensor_copy(r15c[:, 1:2], mA[0:NEXP, 256:257]))
        kb.W(v, sbc_tok)
        c_tok = vv.inc(v.memset(rB[0:1, 0:64], 0.0))
        for e in range(NEXP):
            kb.W(v, c_tok)
            c_tok = vv.inc(v.tensor_scalar(rC[0:1, 0:64], env["iota64r"][:],
                                           rA[0:1, e:e + 1], None, ALU.is_ge))
            kb.W(v, c_tok)
            c_tok = vv.inc(v.tensor_add(rB[0:1, 0:64], rB[0:1, 0:64], rC[0:1, 0:64]))
        kb.W(v, c_tok)
        c_tok = vv.inc(v.tensor_scalar(rB[0:1, 0:64], rB[0:1, 0:64], -1.0, None, ALU.add))
        kb.W(v, c_tok)
        ebf_tok = vv.inc(v.tensor_copy(rBf[0:1, 0:64], rB[0:1, 0:64]))
        kb.W(t, ebf_tok)
        ii = t.matmul(mA[:, 384:448], env["ones_row"][:], rBf[0:1, 0:64], start=True, stop=True)
        ec_tok = mm.inc(ii)
        kb.W(v, ec_tok)
        ecall_tok = vv.inc(v.tensor_copy(ecall[:], mA[:, 384:448]))
        kb.W(v, ecall_tok)
        for j in range(NJ):
            ii = v.scalar_tensor_tensor(t1[:, 0, 0, j:j + 1], ecall[:, j:j + 1], 128.0,
                                        env["iota128c"][:], ALU.mult, ALU.add)
        c_tok = vv.inc(ii)
        kb.W(v, c_tok)
        wi_tok = vv.inc(v.tensor_copy(widx[:, 0:NJ], t1[:, 0, 0, 0:NJ]))
        kb.W(t, ecall_tok)
        ii = t.transpose(mA[0:64, 257:258], rB[0:1, 0:64], env["ident_f"][0:1, 0:1])
        ejt_tok = mm.inc(ii)
        kb.W(v, ejt_tok)
        ej_tok = vv.inc(v.tensor_copy(ej32[:], mA[0:64, 257:258]))
        kb.W(g, ej_tok, toks["eb2d"])
        g.indirect_dma_start(
            out=eb2all[:], out_offset=None, in_=wbf["eb2d"],
            in_offset=IndirectOffsetOnAxis(ap=ej32[:], axis=0)).then_inc(kb.dsc.sem, 16)
        kb.dsc.n += 16
        kb.W(q, kb.dsc.tok())
        ej_store = kb.dsen.inc(q.dma_start(out=wbf["eb2j"], in_=eb2all[:]), 16)
        kb.W(v, ms_tok)
        c_tok = vv.inc(v.memset(prodf[:], 0.0))
        kb.W(v, c_tok)
        c_tok = vv.inc(v.tensor_tensor_scan(rankf[:], msum[:], prodf[:], 0.0, ALU.add, ALU.add))
        kb.W(v, c_tok)
        c_tok = vv.inc(v.tensor_sub(rankf[:], rankf[:], msum[:]))
        kb.W(v, c_tok, sbc_tok)
        slot_tok = vv.inc(v.tensor_scalar(rankf[:], rankf[:], r15c[:, 1:2], None, ALU.add))
        interlude_mm = mm.tok()

        if DEBUG_DUMP:
            with nc.sbuf_tensor("dbgs", [128, 132], F32) as dbgs:
                kb.W(v, wi_tok, ecall_tok, ej_tok)
                v.tensor_copy(dbgs[:, 0:64], t1[:, 0, 0, 0:64])
                v.tensor_copy(dbgs[:, 64:128], widx[:, 0:64])
                d_tok = vv.inc(v.tensor_copy(dbgs[0:64, 129:130], ej32[:]))
                kb.W(q, d_tok)
                dmp = kb.dq.inc(q.dma_start(out=wbf["dbg"], in_=dbgs[:]), 16)
                kb.W(q, dmp)

        kb.W(g, kb.vv.tok())
        for ki in range(3):
            for sb in range(NSLAB):
                ii = g.tensor_copy(mcall[:, ki, sb, 0:1], ccols[:, sb, ki:ki + 1])
        cw_fill = kb.pp.inc(ii)
        scat_mm = interlude_mm
        sc32_tok = [None, None]
        way_scat = [None, None]
        mAcol_rd = None
        for ki in range(3):
            kp = ki % 2
            kb.W(v, slot_tok, scat_mm)
            pr_tok = vv.inc(v.tensor_mul(prodf[:], rankf[:], wmfm[:, ki, :]))
            sl_tok = None
            for pc in range(T // 512):
                kb.W(t, pr_tok, interlude_mm, mB_free)
                ii = t.matmul(mB[0:1, 0:512], env["ones15f"][:],
                              prodf[:, pc * 512:(pc + 1) * 512], start=True, stop=True)
                pm_tok = mm.inc(ii)
                kb.W(v, pm_tok)
                ii = v.tensor_copy(slotw[0:1, pc * 512:(pc + 1) * 512], mB[0:1, 0:512])
                sl_tok = vv.inc(ii)
                mB_free = sl_tok
            kb.W(t, sl_tok, ecall_tok, mAcol_rd)
            for sb in range(NSLAB):
                ii = t.transpose(mA[:, 448 + sb:449 + sb],
                                 slotw[0:1, sb * 128:(sb + 1) * 128], env["ident_f"][0:1, 0:1])
            st_tok = mm.inc(ii)
            kb.W(v, st_tok, way_scat[kp])
            sc32_tok[kp] = vv.inc(v.tensor_copy(slc32[:, kp], mA[:, 448:480]))
            mAcol_rd = sc32_tok[kp]
            kb.W(g, sc32_tok[kp], cw_fill, static_fill, ej_tok, kb.dsen.tok())
            ms2_tok = None
            for sb in range(NSLAB):
                ii = g.indirect_dma_start(
                    out=wbf["metad"],
                    out_offset=IndirectOffsetOnAxis(ap=slc32[:, kp, sb:sb + 1], axis=0),
                    in_=mcall[:, ki, sb], in_offset=None)
                ms2_tok = kb.dmcp[kp].inc(ii, 16)
            way_scat[kp] = ms2_tok
            scat_mm = mm.tok()
        scat_all = (kb.dmcp[0].tok(), kb.dmcp[1].tok())

        ctx.close()
        joinA = tuple(c.tok() for c in (
            kb.vv, kb.ev, kb.mm, kb.dsp[0], kb.dsp[1], kb.dxp[0],
            kb.dxp[1], kb.dbp[0], kb.dbp[1], kb.dg, kb.dU[0], kb.dU[1], kb.dug[0],
            kb.dug[1], kb.dsen, kb.dsc, *kb.dscp, *kb.dmcp, *kb.ecast) if c.n > 0)

        if STAGE == "scat":
            ctx.close()
            octx.close()
            return
        with ExitStack() as ctxb:
            enb = ctxb.enter_context
            wg = enb(nc.sbuf_tensor(None, [128, 2, 2, NK, D], BF16))
            ufm = enb(nc.sbuf_tensor(None, [128, 2, NK, CAP], BF16))
            u_tm2 = enb(nc.sbuf_tensor(None, [128, 2, 2, D], BF16))
            h2b = enb(nc.sbuf_tensor(None, [128, NK, CAP], BF16))
            eb1g = enb(nc.sbuf_tensor(None, [128, 2, NK], F32))
            eb2g = enb(nc.sbuf_tensor(None, [1, 2, D], BF16))
            mt = enb(nc.sbuf_tensor(None, [128, 2, 2, 16], F32))
            iy = enb(nc.sbuf_tensor(None, [128, 2, 2], mybir.dt.int32))
            iu = enb(nc.sbuf_tensor(None, [128, 2, 2], mybir.dt.int32))
            ysc = enb(nc.sbuf_tensor(None, [128, 2, 1024], BF16))
            hpsB = enb(nc.psum_tensor(None, [128, 1024], F32))
            yps = enb(nc.psum_tensor(None, [128, 2, 2, 512], F32))
            utpB = enb(nc.psum_tensor(None, [128, 2, 1024], BF16))

            wg_rd = [None, None]
            mt_rd = [None, None]
            h2_rd2 = {}
            hps_rd2 = [None, None]
            ysc_rd = {}
            iy_w = [None, None]
            iu_rd = [None, None]
            u_rd2 = [None, None]
            utk_arr = [None, None]
            cp_arr = [None, None]
            utp_rdB = [None]

            def load_job(j):
                pj = j % 2
                kb.W(g, wg_rd[pj], wi_tok, *scat_all, *joinA)
                g.indirect_dma_start(
                    out=wg[:, pj].rearrange("p t c n -> p (t c n)"), out_offset=None, in_=w12f,
                    in_offset=IndirectOffsetOnAxis(ap=widx[:, j:j + 1], axis=0)).then_inc(kb.dwg[pj].sem, 16)
                kb.dwg[pj].n += 16
                g.indirect_dma_start(
                    out=eb1g[:, pj], out_offset=None, in_=eb1f,
                    in_offset=IndirectOffsetOnAxis(ap=widx[:, j:j + 1], axis=0)).then_inc(kb.dwg[pj].sem, 16)
                kb.dwg[pj].n += 16
                wtk = kb.dwg[pj].tok()
                kb.W(q, mt_rd[pj], wg_rd[pj], *scat_all, ej_store, *joinA)
                q.dma_start(out=mt[:, pj], in_=wbf["metad"][j * CAP:(j + 1) * CAP, :]
                            .rearrange("(c p) n -> p c n", p=128)).then_inc(kb.dmt[pj].sem, 16)
                kb.dmt[pj].n += 16
                mtk = kb.dmt[pj].inc(
                    q.dma_start(out=eb2g[0:1, pj], in_=wbf["eb2j"][j:j + 1, :]), 16)
                return wtk, mtk

            def issue_u_gather(j):
                pj = j % 2
                kb.W(v, jl[j][1], iu_rd[pj])
                v.tensor_copy(iu[:, pj, 0:1], mt[:, pj, 0, 2:3])
                iu_tok = vv.inc(v.tensor_copy(iu[:, pj, 1:2], mt[:, pj, 1, 2:3]))
                kb.W(g, iu_tok, u_rd2[pj])
                utk = None
                for cc in range(2):
                    utk = kb.dut[pj].inc(g.indirect_dma_start(
                        out=u_tm2[:, pj, cc], out_offset=None, in_=wbf["Ud"],
                        in_offset=IndirectOffsetOnAxis(ap=iu[:, pj, cc:cc + 1], axis=0)), 16)
                utk_arr[pj] = utk
                iu_rd[pj] = utk

            jl = {0: load_job(0), 1: load_job(1)}
            if STAGE == "B0":
                ctxb.close()
                octx.close()
                return
            issue_u_gather(0)
            issue_u_gather(1)
            for j in range(NJ):
                pj = j % 2
                wtk, mtk = jl[j]
                if STAGE == "BL":
                    if j + 2 <= NJ - 1:
                        jl[j + 2] = load_job(j + 2)
                    continue
                if 2 <= j + 1 < NJ:
                    issue_u_gather(j + 1)
                kb.W(t, utk_arr[pj], utp_rdB[0])
                cp = None
                for cc in range(2):
                    for dk in range(NK):
                        ii = t.transpose(utpB[:, cc, dk * 128:(dk + 1) * 128],
                                         u_tm2[:, pj, cc, dk * 128:(dk + 1) * 128],
                                         env["ident_bf"][:])
                    tr_tok = mm.inc(ii)
                    kb.W(v, tr_tok)
                    cp = vv.inc(v.tensor_copy(
                        ufm[:, pj, :, cc * 128:(cc + 1) * 128],
                        utpB[:, cc].rearrange("p (k f) -> p k f", f=128)))
                u_rd2[pj] = tr_tok
                utp_rdB[0] = cp
                cp_arr[pj] = cp
                kb.W(v, mtk, iy_w[pj])
                c_tok = vv.inc(v.tensor_copy(iy[:, pj, 0:1], mt[:, pj, 0, 1:2]))
                kb.W(v, c_tok)
                c_tok = vv.inc(v.tensor_scalar(mt[:, pj, 1, 2:3], mt[:, pj, 1, 1:2],
                                               SENT, None, ALU.is_ge))
                kb.W(v, c_tok)
                c_tok = vv.inc(v.tensor_add(mt[:, pj, 1, 2:3], mt[:, pj, 1, 2:3],
                                            mt[:, pj, 1, 1:2]))
                kb.W(v, c_tok)
                iy_tok = vv.inc(v.tensor_copy(iy[:, pj, 1:2], mt[:, pj, 1, 2:3]))
                for hk in range(NK):
                    sl2 = hk % 2
                    kb.W(t, wtk, cp_arr[pj], hps_rd2[sl2])
                    for dk in range(NK):
                        ii = t.matmul(hpsB[:, sl2 * 512:sl2 * 512 + CAP],
                                      wg[:, pj, 0, dk, hk * 128:(hk + 1) * 128],
                                      ufm[:, pj, dk], start=(dk == 0), stop=(dk == NK - 1))
                    w1tok = mm.inc(ii)
                    kb.W(s, w1tok, h2_rd2.get(hk % 4))
                    ii = s.activation(h2b[:, hk], hpsB[:, sl2 * 512:sl2 * 512 + CAP],
                                      AF.Relu, bias=eb1g[:, pj, hk:hk + 1])
                    h2tok = ev.inc(ii)
                    hps_rd2[sl2] = h2tok
                    h2_rd2[hk % 4] = h2tok
                for sc2 in (() if STAGE == "BW1" else (0, 1)):
                    for hf in range(2):
                        kb.W(t, h2_rd2[3], mtk, ysc_rd.get((sc2, hf)))
                        for hk in range(NK):
                            kb.W(t, h2_rd2[hk % 4])
                            ii = t.matmul(yps[:, sc2, hf],
                                          h2b[:, hk, sc2 * 128:(sc2 + 1) * 128],
                                          wg[:, pj, 1, hk, hf * 512:(hf + 1) * 512],
                                          start=(hk == 0), stop=False)
                        ii = t.matmul(yps[:, sc2, hf], env["ones_row"][:],
                                      eb2g[0:1, pj, hf * 512:(hf + 1) * 512],
                                      start=False, stop=True)
                        w2tok = mm.inc(ii)
                        h2_rd2[3] = w2tok
                        kb.W(v, w2tok, iy_tok)
                        ii = v.tensor_scalar(ysc[:, sc2, hf * 512:(hf + 1) * 512],
                                             yps[:, sc2, hf], mt[:, pj, sc2, 0:1], None, ALU.mult)
                        ysck = vv.inc(ii)
                        ysc_rd[(sc2, hf)] = None
                        if hf == 1:
                            kb.W(g, ysck)
                            yst = kb.dy[sc2].inc(g.indirect_dma_start(
                                out=wbf["R3d"],
                                out_offset=IndirectOffsetOnAxis(ap=iy[:, pj, sc2:sc2 + 1], axis=0),
                                in_=ysc[:, sc2], in_offset=None), 16)
                            ysc_rd[(sc2, 0)] = yst
                            ysc_rd[(sc2, 1)] = yst
                iy_w[pj] = mm.tok()
                wg_rd[pj] = mm.tok()
                mt_rd[pj] = kb.vv.tok()
                if j + 2 <= NJ - 1:
                    jl[j + 2] = load_job(j + 2)

        if STAGE == "B":
            octx.close()
            return
        with ExitStack() as ctxc:
            enc = ctxc.enter_context
            NPC = 4
            r3s = enc(nc.sbuf_tensor(None, [128, NPC, 3, D], BF16))
            rsum = enc(nc.sbuf_tensor(None, [128, 2, D], F32))
            xc = enc(nc.sbuf_tensor(None, [128, NPC, NK, 128], F32))
            rtp = enc(nc.psum_tensor(None, [128, 2, D], F32))
            r3_rd = [None] * NPC
            rs_rd = [None, None]
            xc_rd = [None] * NPC
            rtp_rd = [None, None]
            ydone = (kb.dy[0].tok(), kb.dy[1].tok())

            def loadC(ch):
                par = ch % NPC
                p2 = ch % 2
                eng = q if p2 == 0 else s
                kb.W(eng, r3_rd[par], xc_rd[par], *ydone)
                rt = None
                for ki in range(3):
                    rt = kb.dR[p2].inc(
                        eng.dma_start(out=r3s[:, par, ki],
                                      in_=wbf["R3d"][ki * T + ch * 128: ki * T + (ch + 1) * 128, :]), 16)
                xl = kb.dxp[p2].inc(
                    eng.dma_start(out=xc[:, par], in_=dst3[:, :, ch * 128:(ch + 1) * 128]), 16)
                return rt, xl

            pre = {c: loadC(c) for c in range(min(NPC, NSLAB))}
            for ch in range(NSLAB):
                par = ch % NPC
                p2 = ch % 2
                rt, xl = pre[ch]
                kb.W(v, rt, rs_rd[p2])
                c_tok = vv.inc(v.tensor_add(rsum[:, p2], r3s[:, par, 0], r3s[:, par, 1]))
                kb.W(v, c_tok)
                rs_tok = vv.inc(v.tensor_add(rsum[:, p2], rsum[:, p2], r3s[:, par, 2]))
                r3_rd[par] = rs_tok
                kb.W(t, rs_tok, rtp_rd[p2])
                for dk in range(NK):
                    ii = t.transpose(rtp[:, p2, dk * 128:(dk + 1) * 128],
                                     rsum[:, p2, dk * 128:(dk + 1) * 128], env["ident_f"][:])
                rtr_tok = mm.inc(ii)
                rs_rd[p2] = rtr_tok
                kb.W(v, rtr_tok, xl)
                for dk in range(NK):
                    ii = v.tensor_add(xc[:, par, dk], xc[:, par, dk],
                                      rtp[:, p2, dk * 128:(dk + 1) * 128])
                xa_tok = vv.inc(ii)
                rtp_rd[p2] = xa_tok
                seng = q if p2 == 0 else s
                kb.W(seng, xa_tok)
                kb.dst.n += 16
                xc_rd[par] = kb.dsp[p2].inc(
                    seng.dma_start(out=dst3[:, :, ch * 128:(ch + 1) * 128], in_=xc[:, par]), 16)
                if ch + NPC < NSLAB:
                    pre[ch + NPC] = loadC(ch + NPC)
        kb.W(q, kb.dsp[0].tok(), kb.dsp[1].tok())
    octx.close()


def emit_ffn(nc, kb, ap, wbf, src, dst, env, toks):
    t, v, s, g, q = kb.t, kb.v, kb.s, kb.g, kb.q
    mm, ev, vv = kb.mm, kb.ev, kb.vv
    bias = env["bias"]
    src3 = src.rearrange("(k p) n -> p k n", p=128)
    dst3 = dst.rearrange("(k p) n -> p k n", p=128)

    entry = [kb.dsp[0].tok(), kb.dsp[1].tok()]
    with ExitStack() as ctx:
        xt = ctx.enter_context(nc.sbuf_tensor(None, [128, 2, NK, FT], F32))
        xbf = ctx.enter_context(nc.sbuf_tensor(None, [128, 2, NK, FT], BF16))
        x2bf = ctx.enter_context(nc.sbuf_tensor(None, [128, NK, FT], BF16))
        u = ctx.enter_context(nc.sbuf_tensor(None, [128, NK, FT], BF16))
        t1 = ctx.enter_context(nc.sbuf_tensor(None, [128, NK, FT], F32))
        bcast = ctx.enter_context(nc.sbuf_tensor(None, [128, 2 * FT], F32))
        srow = ctx.enter_context(nc.sbuf_tensor(None, [1, 3 * FT], F32))
        srbf = ctx.enter_context(nc.sbuf_tensor(None, [1, 2 * FT], BF16))
        Lrt = ctx.enter_context(nc.sbuf_tensor(None, [NEXP, FT], F32))
        Ltm = ctx.enter_context(nc.sbuf_tensor(None, [128, 2, NEXP], F32))
        smx = ctx.enter_context(nc.sbuf_tensor(None, [128, 2, 4, NEXP], F32))
        smk = ctx.enter_context(nc.sbuf_tensor(None, [128, 2, NEXP], mybir.dt.int32))
        smr = ctx.enter_context(nc.sbuf_tensor(None, [128, 2, 8], F32))
        cw_tm = ctx.enter_context(nc.sbuf_tensor(None, [128, 2, NEXP], F32))
        cw_fm = ctx.enter_context(nc.sbuf_tensor(None, [NEXP, FT], BF16))
        cwb = ctx.enter_context(nc.sbuf_tensor(None, [128, NEXP, FT], BF16))
        h_bf = ctx.enter_context(nc.sbuf_tensor(None, [128, 4, FT], BF16))
        h2_bf = ctx.enter_context(nc.sbuf_tensor(None, [128, 4, FT], BF16))
        sw1r = ctx.enter_context(nc.sbuf_tensor(None, [128, NK, D], BF16))
        sw2r = ctx.enter_context(nc.sbuf_tensor(None, [128, NK, D], BF16))
        w1b = ctx.enter_context(nc.sbuf_tensor(None, [128, 2, NK, D], BF16))
        w2b = ctx.enter_context(nc.sbuf_tensor(None, [128, 2, NK, D], BF16))
        acc = ctx.enter_context(nc.psum_tensor(None, [128, NK * FT], F32))
        hps = ctx.enter_context(nc.psum_tensor(None, [128, 1024], F32))
        mA = ctx.enter_context(nc.psum_tensor(None, [128, 512], F32))
        mB = ctx.enter_context(nc.psum_tensor(None, [128, 512], F32))
        for eng in (t, v, s, g, q):
            kb.W(eng, *entry)
        kb.W(s, toks["init"], toks["bias"])
        sw_tok = None
        g.dma_start(out=sw1r[:], in_=ap["sW1"].rearrange("(c p) n -> p c n", p=128)).then_inc(kb.dg.sem, 16)
        kb.dg.n += 16
        sw_tok = kb.dg.inc(g.dma_start(out=sw2r[:], in_=ap["sW2"].rearrange("(c p) n -> p c n", p=128)), 16)

        accs = lambda i: acc[:, i * FT:(i + 1) * FT]
        hsl = lambda b: hps[:, b * 512:b * 512 + FT]
        w1t = lambda pe, dk, hk: w1b[:, pe, dk, hk * 128:(hk + 1) * 128]
        w2t = lambda pe, hk, ik: w2b[:, pe, hk, ik * 128:(ik + 1) * 128]
        s1t = lambda dk, hk: sw1r[:, dk, hk * 128:(hk + 1) * 128]
        s2t = lambda hk, ik: sw2r[:, hk, ik * 128:(ik + 1) * 128]
        m_row = srow[0:1, 0:FT]
        msq_row = srow[0:1, FT:2 * FT]
        var_row = srow[0:1, 2 * FT:3 * FT]
        rstd_bf = srbf[0:1, 0:FT]
        mrstd_bf = srbf[0:1, FT:2 * FT]

        wtok = [None, None]
        xtok = [None, None]
        xbtok = [None, None]
        x2tok = None
        srow_w = None
        srbf_w = None
        bc_rd = None
        u_rd = None
        ltm_w = None
        cw_rd = None
        cwtm_w = None
        cwb_rd = [None] * NEXP
        mA_free = None
        mB_free = None
        acc_free = None
        t1_rd = None
        h_rd = [None] * 4
        h2_rd = [None] * 4
        hps_rd = [None, None]

        def load_x(ti):
            p = ti % 2
            kb.W(q, xtok[p])
            xt_tok = kb.dxp[p].inc(q.dma_start(out=xt[:, p], in_=src3[:, :, ti * FT:(ti + 1) * FT]), 16)
            kb.W(g, xbtok[p])
            xb_tok = kb.dbp[p].inc(g.dma_start(out=xbf[:, p], in_=src3[:, :, ti * FT:(ti + 1) * FT]), 16)
            return xt_tok, xb_tok

        def load_exp(e, ti):
            pe = e % 2
            if ti == 0:
                kb.W(q, toks["cast"][e])
            kb.W(q, wtok[pe])
            kb.dwp[pe].inc(q.dma_start(out=w1b[:, pe], in_=wbf["eW1"][e]), 16)
            return kb.dwp[pe].inc(q.dma_start(out=w2b[:, pe], in_=wbf["eW2"][e]), 16)

        x_next = load_x(0)
        for ti in range(NFT):
            p = ti % 2
            x_tok, xb_tok = x_next
            wld = {0: load_exp(0, ti), 1: load_exp(1, ti)}

            kb.W(s, x_tok, x2tok)
            for k in range(NK):
                ii = s.activation(x2bf[:, k], xt[:, p, k], AF.Square)
            sq_tok = ev.inc(ii)

            kb.W(t, xb_tok, sq_tok, mA_free)
            for k in range(NK):
                t.matmul(mA[0:1, 0:FT], env["ones_col"][:], xbf[:, p, k],
                         start=(k == 0), stop=(k == NK - 1))
            for k in range(NK):
                ii = t.matmul(mA[0:1, FT:2 * FT], env["ones_col"][:], x2bf[:, k],
                              start=(k == 0), stop=(k == NK - 1))
            stats_tok = mm.inc(ii)
            xbtok[p] = stats_tok
            x2tok = stats_tok

            kb.W(v, stats_tok, srow_w)
            c_tok = vv.inc(v.tensor_scalar(m_row, mA[0:1, 0:FT], 1.0 / 1024, None, ALU.mult))
            kb.W(v, c_tok)
            c_tok = vv.inc(v.tensor_mul(msq_row, m_row, m_row))
            kb.W(v, c_tok)
            c_tok = vv.inc(v.scalar_tensor_tensor(var_row, mA[0:1, FT:2 * FT], 1.0 / 1024,
                                                  msq_row, ALU.mult, ALU.subtract))
            kb.W(s, c_tok)
            e_tok = ev.inc(s.activation(var_row, var_row, AF.Ln, bias=1e-5))
            kb.W(s, e_tok, srbf_w)
            e_tok = ev.inc(s.activation(rstd_bf, var_row, AF.Exp, scale=-0.5))
            kb.W(v, e_tok)
            c_tok = vv.inc(v.tensor_mul(mrstd_bf, m_row, rstd_bf))
            srow_w = c_tok

            kb.W(t, c_tok)
            t.matmul(mA[:, 0:FT], env["ones_row"][:], rstd_bf, start=True, stop=True)
            ii = t.matmul(mA[:, FT:2 * FT], env["ones_row"][:], mrstd_bf, start=True, stop=True)
            bc_tok = mm.inc(ii)
            srbf_w = bc_tok
            kb.W(s, bc_tok, bc_rd)
            s.copy(bcast[:, 0:FT], mA[:, 0:FT])
            bce_tok = ev.inc(s.copy(bcast[:, FT:2 * FT], mA[:, FT:2 * FT]))

            kb.W(v, bce_tok, u_rd[p], t1_rd[p])
            for k in range(NK):
                ii = v.tensor_mul(t1[:, p, k], xt[:, p, k], bcast[:, 0:FT])
            mul_tok = vv.inc(ii)
            kb.W(v, mul_tok)
            for k in range(NK):
                ii = v.tensor_sub(u[:, p, k], t1[:, p, k], bcast[:, FT:2 * FT])
            u_tok = vv.inc(ii)
            bc_rd = u_tok

            kb.W(t, u_tok, toks["rw"], mB_free)
            for k in range(NK):
                ii = t.matmul(mB[0:NEXP, 0:FT], env["rw_bf"][:, k * NEXP:(k + 1) * NEXP],
                              u[:, p, k], start=(k == 0), stop=(k == NK - 1))
            rt_tok = mm.inc(ii)
            kb.W(v, rt_tok)
            lrt_tok = vv.inc(v.tensor_scalar(Lrt[:], mB[0:NEXP, 0:FT], env["rb_s"][:], None, ALU.add))
            kb.W(t, lrt_tok, ltm_w)
            t.transpose(mB[:, FT:FT + NEXP], Lrt[:, 0:128], env["ident_f"][0:NEXP, 0:NEXP])
            ii = t.transpose(mB[:, FT + NEXP:FT + 2 * NEXP], Lrt[:, 128:256], env["ident_f"][0:NEXP, 0:NEXP])
            ltr_tok = mm.inc(ii)
            kb.W(v, ltr_tok)
            v.tensor_copy(Ltm[:, 0], mB[:, FT:FT + NEXP])
            ltm_tok = vv.inc(v.tensor_copy(Ltm[:, 1], mB[:, FT + NEXP:FT + 2 * NEXP]))
            ltm_w = ltm_tok

            P = lambda sl: smx[:, sl, 0]
            P1 = lambda sl: smx[:, sl, 1]
            P2 = lambda sl: smx[:, sl, 2]
            MK = lambda sl: smk[:, sl]
            m1n = lambda sl: smr[:, sl, 0:1]
            m1p = lambda sl: smr[:, sl, 1:2]
            m2r = lambda sl: smr[:, sl, 2:3]
            m3r = lambda sl: smr[:, sl, 3:4]
            rr = lambda sl: smr[:, sl, 4:5]

            kb.W(v, ltm_tok, toks["init"])
            for sl in (0, 1):
                ii = v.reduce_max(m1n(sl), Ltm[:, sl], axis=AX.X, negate=True)
            c_tok = vv.inc(ii)
            kb.W(s, c_tok)
            for sl in (0, 1):
                ii = s.activation(P(sl), Ltm[:, sl], AF.Exp, bias=m1n(sl))
            e_tok = ev.inc(ii)
            kb.W(v, e_tok)
            for sl in (0, 1):
                v.reduce_sum(rr(sl), P(sl), axis=AX.X)
            for sl in (0, 1):
                ii = v.reduce_max(m1p(sl), P(sl), axis=AX.X)
            c_tok = vv.inc(ii)
            kb.W(v, c_tok)
            for sl in (0, 1):
                ii = v.reciprocal(rr(sl), rr(sl))
            c_tok = vv.inc(ii)
            kb.W(v, c_tok)
            for sl in (0, 1):
                ii = v.tensor_scalar(MK(sl), P(sl), m1p(sl), None, ALU.is_lt)
            c_tok = vv.inc(ii)
            kb.W(v, c_tok)
            for sl in (0, 1):
                ii = v.select(P1(sl), MK(sl), P(sl), env["negone"][:, 0:NEXP])
            c_tok = vv.inc(ii)
            kb.W(v, c_tok)
            for sl in (0, 1):
                ii = v.reduce_max(m2r(sl), P1(sl), axis=AX.X)
            c_tok = vv.inc(ii)
            kb.W(v, c_tok)
            for sl in (0, 1):
                ii = v.tensor_scalar(MK(sl), P1(sl), m2r(sl), None, ALU.is_lt)
            c_tok = vv.inc(ii)
            kb.W(v, c_tok)
            for sl in (0, 1):
                ii = v.select(P2(sl), MK(sl), P1(sl), env["negone"][:, 0:NEXP])
            c_tok = vv.inc(ii)
            kb.W(v, c_tok)
            for sl in (0, 1):
                ii = v.reduce_max(m3r(sl), P2(sl), axis=AX.X)
            c_tok = vv.inc(ii)
            kb.W(v, c_tok)
            for sl in (0, 1):
                ii = v.tensor_scalar(MK(sl), P(sl), m3r(sl), None, ALU.is_ge)
            c_tok = vv.inc(ii)
            kb.W(v, c_tok)
            for sl in (0, 1):
                ii = v.tensor_mul(P1(sl), P(sl), MK(sl))
            c_tok = vv.inc(ii)
            kb.W(v, c_tok, cwtm_w)
            for sl in (0, 1):
                ii = v.tensor_scalar(cw_tm[:, sl], P1(sl), rr(sl), None, ALU.mult)
            cw_tok = vv.inc(ii)

            kb.W(t, cw_tok)
            t.transpose(mB[0:NEXP, 0:128], cw_tm[:, 0], env["ident_f"][:])
            ii = t.transpose(mB[0:NEXP, 128:256], cw_tm[:, 1], env["ident_f"][:])
            cwt_tok = mm.inc(ii)
            cwtm_w = cwt_tok
            kb.W(v, cwt_tok, cw_rd)
            v.tensor_copy(cw_fm[:, 0:128], mB[0:NEXP, 0:128])
            cwf_tok = vv.inc(v.tensor_copy(cw_fm[:, 128:256], mB[0:NEXP, 128:256]))

            cwb_ev = [None] * NEXP
            for e in range(NEXP):
                dst_ps = mA if e % 2 == 0 else mB
                if e == 0:
                    kb.W(t, cwf_tok, bce_tok)
                elif e == 1:
                    kb.W(t, cwf_tok)
                else:
                    kb.W(t, cwb_ev[e - 2])
                ii = t.matmul(dst_ps[:, 0:FT], env["sel"][:, e * 128:(e + 1) * 128],
                              cw_fm[:], start=True, stop=True)
                cb_tok = mm.inc(ii)
                kb.W(s, cb_tok, cwb_rd[e])
                cwb_ev[e] = ev.inc(s.copy(cwb[:, e], dst_ps[:, 0:FT]))
            mA_free = cwb_ev[14]
            mB_free = cwb_ev[13]

            relu_ev = {}
            h2_vv = {}
            w2_mm = {}
            order = [NEXP] + list(range(NEXP))
            seq = [(e, j) for e in order for j in range(NK)]

            def w1_group(idx):
                e, j = seq[idx]
                ps_slot = idx % 2
                h_slot = idx % 4
                pe = e % 2
                if e < NEXP:
                    kb.W(t, wld[e], hps_rd[ps_slot])
                else:
                    kb.W(t, sw_tok, hps_rd[ps_slot])
                for dk in range(NK):
                    lt = w1t(pe, dk, j) if e < NEXP else s1t(dk, j)
                    ii = t.matmul(hsl(ps_slot), lt, u[:, dk], start=(dk == 0), stop=(dk == NK - 1))
                tok = mm.inc(ii)
                kb.W(s, tok, h_rd[h_slot])
                b = bias["eb1"](e, j) if e < NEXP else bias["sb1"](j)
                ii = s.activation(h_bf[:, h_slot], hsl(ps_slot), AF.Relu, bias=b)
                relu_ev[idx] = ev.inc(ii)
                hps_rd[ps_slot] = relu_ev[idx]
                if e < NEXP:
                    kb.W(v, relu_ev[idx], h2_rd[h_slot])
                    ii = v.tensor_mul(h2_bf[:, h_slot], h_bf[:, h_slot], cwb[:, e])
                    h2_vv[idx] = vv.inc(ii)
                    h_rd[h_slot] = h2_vv[idx]
                    cwb_rd[e] = h2_vv[idx]

            def w2_group(idx, first):
                e, j = seq[idx]
                h_slot = idx % 4
                pe = e % 2
                if e < NEXP:
                    kb.W(t, h2_vv[idx])
                    rhs = h2_bf[:, h_slot]
                else:
                    kb.W(t, relu_ev[idx])
                    rhs = h_bf[:, h_slot]
                if first:
                    kb.W(t, acc_free)
                for ik in range(NK):
                    lt = w2t(pe, j, ik) if e < NEXP else s2t(j, ik)
                    ii = t.matmul(accs(ik), lt, rhs, start=(first and ik % 2 == 0), stop=False)
                tok = mm.inc(ii)
                w2_mm[idx] = tok
                if e < NEXP:
                    h2_rd[h_slot] = tok
                else:
                    h_rd[h_slot] = tok

            for idx in range(len(seq)):
                w1_group(idx)
                if idx >= 2:
                    w2_group(idx - 2, first=(idx == 2))
                e, j = seq[idx]
                if j == NK - 1 and e < NEXP and e + 2 < NEXP:
                    wtok[(e + 2) % 2] = mm.tok()
                    wld[e + 2] = load_exp(e + 2, ti)
                if idx == len(seq) - 4 and ti + 1 < NFT:
                    x_next2 = load_x(ti + 1)
            w2_group(len(seq) - 2, first=False)
            w2_group(len(seq) - 1, first=False)
            wtok[0] = w2_mm[len(seq) - 1]
            wtok[1] = w2_mm[(1 + 13) * NK + NK - 1]

            kb.W(t, toks["rw"], cwf_tok)
            for ik in range(NK):
                ii = t.matmul(accs(ik), env["eb2_bf"][:, ik * 128:(ik + 1) * 128],
                              cw_fm[:], start=False, stop=(ik % 2 == 1))
            acc_tok = mm.inc(ii)
            cw_rd = acc_tok
            u_rd = acc_tok

            kb.W(v, acc_tok, t1_rd)
            for ik in range(NK):
                ii = v.tensor_scalar(t1[:, ik], accs(ik), bias["sb2"](ik), 0.5, ALU.add, ALU.mult)
            fe_tok = vv.inc(ii)
            acc_free = fe_tok
            kb.W(v, fe_tok)
            for ik in range(NK):
                ii = v.tensor_add(xt[:, p, ik], xt[:, p, ik], t1[:, ik])
            add_tok = vv.inc(ii)
            t1_rd = add_tok
            kb.W(q, add_tok)
            kb.dst.n += 16
            xtok[p] = kb.dsp[p].inc(q.dma_start(out=dst3[:, :, ti * FT:(ti + 1) * FT], in_=xt[:, p]), 16)
            if ti + 1 < NFT:
                x_next = x_next2

        kb.W(q, kb.dsp[0].tok(), kb.dsp[1].tok())


def emit_attn(nc, kb, ap, wbf, src, dst, env, toks):
    t, v, s, g, q = kb.t, kb.v, kb.s, kb.g, kb.q
    mm, ev, vv, pp = kb.mm, kb.ev, kb.vv, kb.pp
    bias = env["bias"]
    src3 = src.rearrange("(k p) n -> p k n", p=128)
    dst3 = dst.rearrange("(k p) n -> p k n", p=128)
    A = 512
    ST = 256
    NG = T // A

    entry = [kb.dsp[0].tok(), kb.dsp[1].tok()]
    with ExitStack() as ctx:
        NXP = 2
        xt = ctx.enter_context(nc.sbuf_tensor(None, [128, NXP, NK, ST], F32))
        xbf = ctx.enter_context(nc.sbuf_tensor(None, [128, 2, NK, ST], BF16))
        x2b = ctx.enter_context(nc.sbuf_tensor(None, [128, NK, ST], BF16))
        ug = ctx.enter_context(nc.sbuf_tensor(None, [128, NK, A], BF16))
        t1 = ctx.enter_context(nc.sbuf_tensor(None, [128, NK, ST], BF16))
        bcg = ctx.enter_context(nc.sbuf_tensor(None, [128, 2 * ST], F32))
        srow = ctx.enter_context(nc.sbuf_tensor(None, [1, 3 * ST], F32))
        srbf = ctx.enter_context(nc.sbuf_tensor(None, [1, 2 * ST], BF16))
        Qp = ctx.enter_context(nc.sbuf_tensor(None, [128, NK, A], BF16))
        Kp = ctx.enter_context(nc.sbuf_tensor(None, [128, NK, A], BF16))
        Vb = ctx.enter_context(nc.sbuf_tensor(None, [128, NK, A], BF16))
        phia = ctx.enter_context(nc.sbuf_tensor(None, [128, 2, A], BF16))
        phir = ctx.enter_context(nc.sbuf_tensor(None, [128, 2, A], BF16))
        Ktm = ctx.enter_context(nc.sbuf_tensor(None, [128, 4, D], BF16))
        Vtm = ctx.enter_context(nc.sbuf_tensor(None, [128, 4, D], BF16))
        Ab = ctx.enter_context(nc.sbuf_tensor(None, [128, 2, 128], BF16))
        Ssb = ctx.enter_context(nc.sbuf_tensor(None, [128, NK, D], F32))
        Sbf = ctx.enter_context(nc.sbuf_tensor(None, [128, NK, D], BF16))
        Zc = ctx.enter_context(nc.sbuf_tensor(None, [128, NK], F32))
        Zs = ctx.enter_context(nc.sbuf_tensor(None, [128, NK], F32))
        Zbf = ctx.enter_context(nc.sbuf_tensor(None, [128, NK], BF16))
        dens = ctx.enter_context(nc.sbuf_tensor(None, [1, 2, ST], F32))
        rden = ctx.enter_context(nc.sbuf_tensor(None, [1, 2, ST], F32))
        rdbf = ctx.enter_context(nc.sbuf_tensor(None, [1, 2, ST], BF16))
        rdb = ctx.enter_context(nc.sbuf_tensor(None, [128, 2, ST], F32))
        wst = ctx.enter_context(nc.sbuf_tensor(None, [128, 3, NK, D], BF16))
        pA = ctx.enter_context(nc.psum_tensor(None, [128, 512], F32))
        pB = ctx.enter_context(nc.psum_tensor(None, [128, 512], F32))
        nump = ctx.enter_context(nc.psum_tensor(None, [128, NK, ST], F32))

        for eng in (t, v, s, g, q):
            kb.W(eng, *entry)
        kb.W(s, toks["init"], toks["bias"])

        smem = vv.inc(v.memset(Ssb[:], 0.0))
        zmem = vv.inc(v.memset(Zs[:], 0.0))

        wmat = lambda pe, ck, ok: wst[:, pe, ck, ok * 128:(ok + 1) * 128]
        wseq = ("wqT", "wkT", "wvT", "woT")
        wtok = [None, None, None]
        wpre = {}

        def load_w(gi, wi):
            pe = (gi * 4 + wi) % 3
            nm = wseq[wi]
            if gi == 0:
                kb.W(q, toks["cast"]["woT"])
            kb.W(q, wtok[pe])
            return pe, kb.dwp[wi % 2].inc(q.dma_start(
                out=wst[:, pe], in_=wbf[nm].rearrange("(c p) n -> p c n", p=128)), 16)

        xbtok = [None, None]
        x2tok = None
        srow_w = None
        srbf_w = None
        bc_rd = None
        u_rd = None
        t1_rd = None
        pa_rd = None
        pb_rd = None
        num_rd = None
        qkv_rd = None
        Ab_rd = [None, None]
        O_rd = None
        tm_rd = None
        Sbf_w = None
        zbf_rd = None
        phi_rds = [None, None]
        rdb_rd = [None, None]
        ptr_kv_rd = None

        xtok = [None] * NXP
        def load_x(sti):
            p = sti % NXP
            pb2 = sti % 2
            kb.W(q, xtok[p])
            a_tok = kb.dxp[pb2].inc(q.dma_start(out=xt[:, p], in_=src3[:, :, sti * ST:(sti + 1) * ST]), 16)
            kb.W(g, xbtok[pb2])
            b_tok = kb.dbp[pb2].inc(g.dma_start(out=xgb_in(pb2), in_=src3[:, :, sti * ST:(sti + 1) * ST]), 16)
            return a_tok, b_tok

        def xgb_in(p):
            return xbf[:, p]

        x_next = load_x(0)
        x_next2 = None
        for gi in range(NG):
            wld = {0: wpre.pop((gi, 0), None) or load_w(gi, 0),
                   1: wpre.pop((gi, 1), None) or load_w(gi, 1)}

            u_toks = []
            for sti2 in range(2):
                sti = gi * 2 + sti2
                p = sti % NXP
                pb2 = sti % 2
                if sti2 == 0:
                    x_tok, xb_tok = x_next
                else:
                    x_tok, xb_tok = x_next2
                if sti2 == 0:
                    x_next2 = load_x(sti + 1)
                kb.W(s, x_tok, x2tok)
                for k in range(NK):
                    ii = s.activation(x2b[:, k], xt[:, p, k], AF.Square)
                sq_tok = ev.inc(ii)
                kb.W(t, xb_tok, sq_tok, pa_rd, pb_rd)
                for k in range(NK):
                    t.matmul(pA[0:1, 0:ST], env["ones_col"][:], xbf[:, pb2, k],
                             start=(k == 0), stop=(k == NK - 1))
                for k in range(NK):
                    ii = t.matmul(pA[0:1, ST:2 * ST], env["ones_col"][:], x2b[:, k],
                                  start=(k == 0), stop=(k == NK - 1))
                stats_tok = mm.inc(ii)
                xbtok[pb2] = stats_tok
                x2tok = stats_tok

                m_row = srow[0:1, 0:ST]
                msq_row = srow[0:1, ST:2 * ST]
                var_row = srow[0:1, 2 * ST:3 * ST]
                rstd_bf = srbf[0:1, 0:ST]
                mrstd_bf = srbf[0:1, ST:2 * ST]
                kb.W(v, stats_tok, srow_w)
                c_tok = vv.inc(v.tensor_scalar(m_row, pA[0:1, 0:ST], 1.0 / 1024, None, ALU.mult))
                kb.W(v, c_tok)
                c_tok = vv.inc(v.tensor_mul(msq_row, m_row, m_row))
                kb.W(v, c_tok)
                c_tok = vv.inc(v.scalar_tensor_tensor(var_row, pA[0:1, ST:2 * ST], 1.0 / 1024,
                                                      msq_row, ALU.mult, ALU.subtract))
                kb.W(s, c_tok)
                e_tok = ev.inc(s.activation(var_row, var_row, AF.Ln, bias=1e-5))
                kb.W(s, e_tok, srbf_w)
                e_tok = ev.inc(s.activation(rstd_bf, var_row, AF.Exp, scale=-0.5))
                kb.W(v, e_tok)
                c_tok = vv.inc(v.tensor_mul(mrstd_bf, m_row, rstd_bf))
                srow_w = c_tok
                kb.W(t, c_tok)
                t.matmul(pB[:, 0:ST], env["ones_row"][:], rstd_bf, start=True, stop=True)
                ii = t.matmul(pB[:, ST:2 * ST], env["ones_row"][:], mrstd_bf, start=True, stop=True)
                bc_tok = mm.inc(ii)
                srbf_w = bc_tok
                pa_rd = c_tok
                kb.W(s, bc_tok, bc_rd)
                s.copy(bcg[:, 0:ST], pB[:, 0:ST])
                bce_tok = ev.inc(s.copy(bcg[:, ST:2 * ST], pB[:, ST:2 * ST]))
                pb_rd = bce_tok
                kb.W(v, bce_tok, u_rd, t1_rd)
                for k in range(NK):
                    ii = v.tensor_mul(t1[:, k], xt[:, p, k], bcg[:, 0:ST])
                mul_tok = vv.inc(ii)
                kb.W(v, mul_tok)
                for k in range(NK):
                    ii = v.tensor_sub(ug[:, k, sti2 * ST:(sti2 + 1) * ST], t1[:, k],
                                      bcg[:, ST:2 * ST])
                u_tok = vv.inc(ii)
                bc_rd = u_tok
                t1_rd = u_tok
                u_toks.append(u_tok)

            rot = [pA[:, 0:A], pB[:, 0:A], nump[:, 0:2, :], nump[:, 2:4, :]]
            rfree = [pa_rd, pb_rd, num_rd, num_rd]
            np_i = [0]
            NROT = 4

            def proj(wi, dst_sb, bias_fn, nbias_fn, is_v):
                nonlocal qkv_rd
                pe, wl = wld[wi]
                for ok in range(NK):
                    sl = np_i[0] % NROT
                    ps = rot[sl]
                    pp2 = np_i[0] % 2
                    kb.W(t, u_toks[0], u_toks[1], wl, rfree[sl])
                    for dk in range(NK):
                        ii = t.matmul(ps, wmat(pe, dk, ok), ug[:, dk],
                                      start=(dk == 0), stop=(dk == NK - 1))
                    ptok = mm.inc(ii)
                    if is_v:
                        kb.W(v, ptok, qkv_rd)
                        eetok = vv.inc(v.tensor_scalar(dst_sb[:, ok], ps,
                                                       bias_fn(ok), None, ALU.add))
                    else:
                        kb.W(s, ptok, phi_rds[pp2])
                        a1 = ev.inc(s.activation(phia[:, pp2], ps, AF.Relu,
                                                 scale=-1.0, bias=nbias_fn(ok)))
                        kb.W(s, a1)
                        ev.inc(s.activation(phia[:, pp2], phia[:, pp2], AF.Exp, scale=-1.0))
                        rtok = ev.inc(s.activation(phir[:, pp2], ps, AF.Relu, bias=bias_fn(ok)))
                        kb.W(v, rtok, qkv_rd)
                        eetok = vv.inc(v.tensor_add(dst_sb[:, ok], phia[:, pp2], phir[:, pp2]))
                        phi_rds[pp2] = eetok
                    rfree[sl] = eetok
                    np_i[0] += 1
                return eetok

            qdone = proj(0, Qp, bias["bq"], bias["nbq"], False)
            wtok[wld[0][0]] = mm.tok()
            wld[2] = load_w(gi, 2)
            kdone = proj(1, Kp, bias["bk"], bias["nbk"], False)
            wtok[wld[1][0]] = mm.tok()
            wld[3] = load_w(gi, 3)
            vdone = proj(2, Vb, bias["bv"], None, True)
            wtok[wld[2][0]] = mm.tok()
            if gi + 1 < NG:
                wpre[(gi + 1, 0)] = load_w(gi + 1, 0)
                wpre[(gi + 1, 1)] = load_w(gi + 1, 1)
            pa_rd, pb_rd = rfree[0], rfree[1]
            num_rd = rfree[3] if rfree[3] is not None else num_rd
            kb.W(t, rfree[2])

            with nc.psum_tensor("ptr_%d" % gi, [128, D], BF16, allow_name_mangling=True) as ptr:
                kb.W(t, kdone, vdone, ptr_kv_rd, tm_rd)
                ttoks = []
                for ci in range(4):
                    csl = slice(ci * CH, (ci + 1) * CH)
                    for srcb, dstb in ((Kp, Ktm), (Vb, Vtm)):
                        if ttoks:
                            kb.W(t, ttoks[-1])
                        for dk in range(NK):
                            ii = t.transpose(ptr[:, dk * 128:(dk + 1) * 128],
                                             srcb[:, dk, csl], env["ident_bf"][:])
                        tr_tok = mm.inc(ii)
                        kb.W(v, tr_tok)
                        ttoks.append(vv.inc(v.tensor_copy(dstb[:, ci], ptr[:])))
                tm_done = ttoks[-1]

            numsl = lambda ek, lo, w: nump[:, ek, lo:lo + w]
            den_ps = lambda sti2: pB[0:1, sti2 * ST:(sti2 + 1) * ST]
            abctx = ExitStack()
            _ab0 = abctx.enter_context(nc.psum_tensor("abps0_%d" % gi, [128, 512], F32, allow_name_mangling=True))
            _ab1 = abctx.enter_context(nc.psum_tensor("abps1_%d" % gi, [128, 512], F32, allow_name_mangling=True))
            abps = [_ab0, _ab1]
            den_defer = []
            o_toks = [None, None]
            for sti2 in range(2):
                sti = gi * 2 + sti2
                ssl = slice(sti2 * 2 * CH, (sti2 + 1) * 2 * CH)

                kb.W(t, num_rd)
                if gi > 0:
                    kb.W(t, Sbf_w)
                    for dk in range(NK):
                        for ek in range(NK):
                            ii = t.matmul(numsl(ek, 0, ST),
                                          Sbf[:, dk, ek * 128:(ek + 1) * 128],
                                          Qp[:, dk, ssl],
                                          start=(dk == 0 and ek % 2 == 0), stop=False)

                kb.W(t, tm_done, qdone)
                abm_toks = {}
                den_started = [False]
                first_den_wait = [True]
                for li, c in enumerate((sti2 * 2, sti2 * 2 + 1)):
                    for m in range(c + 1):
                        slot = len(abm_toks) % 2
                        abm_toks[len(abm_toks)] = 1
                        csl_c = slice(c * CH, (c + 1) * CH)
                        csl_m = slice(m * CH, (m + 1) * CH)
                        kb.W(t, Ab_rd[slot])
                        for dk in range(NK):
                            ii = t.matmul(abps[slot][:, 0:128], Kp[:, dk, csl_m], Qp[:, dk, csl_c],
                                          start=(dk == 0), stop=(dk == NK - 1))
                        ablk = mm.inc(ii)
                        kb.W(v, ablk, toks["init"])
                        if m == c:
                            abe = vv.inc(v.tensor_mul(Ab[:, slot], abps[slot][:, 0:128], env["triu"][:]))
                        else:
                            abe = vv.inc(v.tensor_copy(Ab[:, slot], abps[slot][:, 0:128]))
                        lo = (c % 2) * 128
                        if first_den_wait[0]:
                            kb.W(t, abe, pb_rd, zbf_rd)
                            first_den_wait[0] = False
                        else:
                            kb.W(t, abe)
                        dst_flag = not den_started[0]
                        den_started[0] = True
                        t.matmul(den_ps(sti2)[:, lo:lo + 128], env["ones_col"][:], Ab[:, slot],
                                 start=dst_flag, stop=(gi == 0 and li == 1 and m == c))
                        av_start = (gi == 0) and (m == 0)
                        av_stop = (m == c) and (gi == 0 or li == 1)
                        for ek in range(NK):
                            ii = t.matmul(numsl(ek, lo, 128), Vtm[:, m, ek * 128:(ek + 1) * 128],
                                          Ab[:, slot], start=(av_start and ek % 2 == 0),
                                          stop=(av_stop and ek % 2 == 1))
                        av_tok = mm.inc(ii)
                        Ab_rd[slot] = av_tok
                if gi > 0:
                    kb.W(t, zbf_rd)
                    for dk in range(NK):
                        ii = t.matmul(den_ps(sti2)[:], Zbf[:, dk:dk + 1], Qp[:, dk, ssl],
                                      start=False, stop=(dk == NK - 1))
                    den_tok = mm.inc(ii)
                else:
                    den_tok = mm.tok()
                num_done = mm.tok()

                kb.W(s, num_done, O_rd)
                ev_ii = None
                for ek in range(NK):
                    ev_ii = s.copy(ug[:, ek, ssl], nump[:, ek, :])
                evac_tok = ev.inc(ev_ii)
                num_rd = evac_tok
                den_defer.append((sti2, ssl, den_tok, evac_tok))

            abctx.close()
            scan_mm = mm.tok()
            with nc.psum_tensor("kvq_%d" % gi, [128, 2, 512], F32, allow_name_mangling=True) as kvq:
                kvfree = [None, None]
                inter_last = num_done if gi > 0 else None
                for dk in range(NK):
                    for hf in range(2):
                        slot = (dk * 2 + hf) % 2
                        kb.W(t, tm_done, kvfree[slot])
                        for ci in range(4):
                            ii = t.matmul(kvq[:, slot],
                                          Ktm[:, ci, dk * 128:(dk + 1) * 128],
                                          Vtm[:, ci, hf * 512:(hf + 1) * 512],
                                          start=(ci == 0), stop=(ci == 3))
                        kv_tok = mm.inc(ii)
                        kb.W(v, kv_tok, smem)
                        ii = v.tensor_add(Ssb[:, dk, hf * 512:(hf + 1) * 512],
                                          Ssb[:, dk, hf * 512:(hf + 1) * 512],
                                          kvq[:, slot])
                        sadd = vv.inc(ii)
                        kvfree[slot] = sadd
                        kb.W(s, sadd)
                        ii = s.copy(Sbf[:, dk, hf * 512:(hf + 1) * 512],
                                    Ssb[:, dk, hf * 512:(hf + 1) * 512])
                        ev.inc(ii)
                Sbf_w = kb.ev.tok()
                ptr_kv_rd = kvfree[0]
                ptr_kv_rd = kb.vv.tok()
            tm_rd = mm.tok()

            for sti2, ssl, den_tok, evac_tok in den_defer:
                kb.W(s, den_tok)
                de_tok = ev.inc(s.activation(dens[0:1, sti2], den_ps(sti2)[:],
                                             AF.Copy, bias=1e-6))
                pb_rd = de_tok
                kb.W(v, de_tok)
                r_tok = vv.inc(v.reciprocal(rden[0:1, sti2], dens[0:1, sti2]))
                kb.W(v, r_tok)
                r_tok = vv.inc(v.tensor_copy(rdbf[0:1, sti2], rden[0:1, sti2]))
                kb.W(t, r_tok, pa_rd)
                ii = t.matmul(pA[:, sti2 * ST:(sti2 + 1) * ST], env["ones_row"][:],
                              rdbf[0:1, sti2], start=True, stop=True)
                rb_tok = mm.inc(ii)
                kb.W(s, rb_tok, rdb_rd[sti2])
                rdb_tok = ev.inc(s.copy(rdb[:, sti2], pA[:, sti2 * ST:(sti2 + 1) * ST]))
                pa_rd = rdb_tok
                kb.W(v, rdb_tok, evac_tok)
                for ek in range(NK):
                    ii = v.tensor_mul(ug[:, ek, ssl], ug[:, ek, ssl], rdb[:, sti2])
                o_toks[sti2] = vv.inc(ii)
                rdb_rd[sti2] = o_toks[sti2]

            kb.W(v, kdone, zmem)
            for dk in range(NK):
                ii = v.reduce_sum(Zc[:, dk:dk + 1], Kp[:, dk, :], axis=AX.X)
            zr_tok = vv.inc(ii)
            kb.W(v, zr_tok)
            za_tok = vv.inc(v.tensor_add(Zs[:], Zs[:], Zc[:]))
            zmem = za_tok
            kb.W(s, za_tok, zbf_rd, scan_mm)
            zb_tok = ev.inc(s.copy(Zbf[:], Zs[:]))
            zbf_rd = zb_tok

            pe3, wl3 = wld[3]
            for sti2 in range(2):
                sti = gi * 2 + sti2
                p = sti % NXP
                ssl = slice(sti2 * 2 * CH, (sti2 + 1) * 2 * CH)
                wo_ev = [None] * NK
                for jk in range(NK):
                    sl = jk % 2
                    ps = rot[sl]
                    kb.W(t, wl3, o_toks[sti2], pa_rd if sl == 0 else pb_rd)
                    if jk >= 2:
                        kb.W(t, wo_ev[jk - 2])
                    for ek in range(NK):
                        ii = t.matmul(ps[:, 0:ST], wmat(pe3, ek, jk), ug[:, ek, ssl],
                                      start=(ek == 0), stop=(ek == NK - 1))
                    wotok = mm.inc(ii)
                    kb.W(v, wotok)
                    ii = v.scalar_tensor_tensor(xt[:, p, jk], ps[:, 0:ST], bias["bo"](jk),
                                                xt[:, p, jk], ALU.add, ALU.add)
                    wo_ev[jk] = vv.inc(ii)
                pa_rd = wo_ev[NK - 2]
                pb_rd = wo_ev[NK - 1]
                add_tok = wo_ev[NK - 1]
                kb.W(q, add_tok)
                kb.dst.n += 16
                xtok[p] = kb.dsp[sti % 2].inc(
                    q.dma_start(out=dst3[:, :, sti * ST:(sti + 1) * ST], in_=xt[:, p]), 16)
                if sti2 == 0 and gi + 1 < NG:
                    x_next = load_x((gi + 1) * 2)
            womm = mm.tok()
            O_rd = womm
            u_rd = womm
            qkv_rd = womm
            wtok[pe3] = womm

        kb.W(q, kb.dsp[0].tok(), kb.dsp[1].tok())


def _stage(inputs):
    c = lambda a: np.ascontiguousarray(np.asarray(a, dtype=np.float32))
    base = {
        "wqT": c(np.asarray(inputs["L0_attn_Wq"]).T), "wkT": c(np.asarray(inputs["L0_attn_Wk"]).T),
        "wvT": c(np.asarray(inputs["L0_attn_Wv"]).T), "woT": c(np.asarray(inputs["L0_attn_Wo"]).T),
        "sW1": c(inputs["L0_ffn_sW1"]), "sW2": c(inputs["L0_ffn_sW2"]),
        "eW1": c(np.asarray(inputs["L0_ffn_eW1"]).reshape(NEXP, NK, 128, D)
                 .transpose(0, 2, 1, 3)),
        "eW2": c(np.asarray(inputs["L0_ffn_eW2"]).reshape(NEXP, NK, 128, D)
                 .transpose(0, 2, 1, 3)),
        "rW": c(np.asarray(inputs["L0_ffn_router_W"]).reshape(NK, 128, NEXP)
                .transpose(1, 0, 2).reshape(128, NK * NEXP)),
        "rb": c(np.asarray(inputs["L0_ffn_router_b"]).reshape(NEXP, 1)),
        "eb1": c(np.asarray(inputs["L0_ffn_eb1"]).reshape(NEXP, NK, 128)
                 .transpose(2, 0, 1).reshape(128, NEXP * NK)),
        "eb1sp": c(np.asarray(inputs["L0_ffn_eb1"]).reshape(NEXP, NK, 128)
                   .transpose(0, 2, 1).reshape(NEXP * 128, NK)),
        "eb2": c(inputs["L0_ffn_eb2"]),
    }
    for nm, key in (("bq", "L0_attn_bq"), ("bk", "L0_attn_bk"), ("bv", "L0_attn_bv"),
                    ("bo", "L0_attn_bo"), ("sb1", "L0_ffn_sb1"), ("sb2", "L0_ffn_sb2")):
        base[nm] = c(np.asarray(inputs[key]).reshape(NK, 128).T)
    xs = c(np.asarray(inputs["L0_attn_x"]).transpose(0, 2, 1))
    return base, xs


def run_device(inputs, phases=("f1", "at", "f2")):
    base, xs = _stage(inputs)
    nc = build_program(phases)
    in_maps = [dict(base, X=xs[i]) for i in range(8)]
    res = run_bass_kernel_spmd(nc, in_maps, core_ids=list(range(8)))
    Y = np.stack([np.asarray(r["Y"]) for r in res.results])
    return np.ascontiguousarray(Y.transpose(0, 2, 1))


def kernel(**inputs):
    return run_device(inputs)


# version 13
# speedup vs baseline: 1.0823x; 1.0784x over previous
from contextlib import ExitStack
import numpy as np
import ml_dtypes
import concourse.bass as bass
import concourse.mybir as mybir
from concourse.bass_utils import run_bass_kernel_spmd
from concourse.bass_core_rust import IndirectOffsetOnAxis

F32 = mybir.dt.float32
BF16 = mybir.dt.bfloat16
AF = mybir.ActivationFunctionType
ALU = mybir.AluOpType
AX = mybir.AxisListType

D = 1024
T = 4096
NK = 8
NEXP = 15
FT = 256
NFT = T // FT
CH = 128
NCH = T // CH
AG = 512
NAG = T // AG
BF = ml_dtypes.bfloat16
SPARSE = True
STAGE = "full"
DEBUG_DUMP = False


class Ctr:
    def __init__(self, nc, name):
        self.sem = nc.alloc_semaphore(name=name)
        self.n = 0

    def inc(self, inst, k=1):
        inst.then_inc(self.sem, k)
        self.n += k
        return (self, self.n)

    def tok(self):
        return (self, self.n)


class KB:
    def __init__(self, nc):
        self.nc = nc
        self.t = nc.tensor
        self.v = nc.vector
        self.s = nc.scalar
        self.g = nc.gpsimd
        self.q = nc.sync
        self.seen = {}
        self.mm = Ctr(nc, "mm")
        self.ev = Ctr(nc, "ev")
        self.vv = Ctr(nc, "vv")
        self.pp = Ctr(nc, "pp")
        self.dq = Ctr(nc, "dq")
        self.dg = Ctr(nc, "dg")
        self.dxp = [Ctr(nc, "dx0"), Ctr(nc, "dx1")]
        self.dbp = [Ctr(nc, "db0"), Ctr(nc, "db1")]
        self.dwp = [Ctr(nc, "dw0"), Ctr(nc, "dw1")]
        self.dsp = [Ctr(nc, "ds0"), Ctr(nc, "ds1")]
        self.dst = Ctr(nc, "dst")
        self.ecast = [Ctr(nc, f"ec{i}") for i in range(NEXP)]
        self.acast = Ctr(nc, "acast")
        self.dU = [Ctr(nc, "dUa"), Ctr(nc, "dUb")]
        self.dug = [Ctr(nc, "dug0"), Ctr(nc, "dug1")]
        self.dsc = Ctr(nc, "dsc")
        self.dscp = [Ctr(nc, "dscp0"), Ctr(nc, "dscp1")]
        self.dmcp = [Ctr(nc, "dmcp0"), Ctr(nc, "dmcp1")]
        self.dwg = [Ctr(nc, "dwg0"), Ctr(nc, "dwg1")]
        self.dbg = [Ctr(nc, "dbg0"), Ctr(nc, "dbg1")]
        self.dmt = [Ctr(nc, "dmt0"), Ctr(nc, "dmt1")]
        self.dut = [Ctr(nc, "dut0"), Ctr(nc, "dut1")]
        self.dy = [Ctr(nc, "dya"), Ctr(nc, "dyb")]
        self.dR = [Ctr(nc, "dR0"), Ctr(nc, "dR1")]
        self.dsen = Ctr(nc, "dsen")

    def W(self, eng, *toks):
        for tk in toks:
            if tk is None:
                continue
            ctr, val = tk
            if val <= 0:
                continue
            key = (id(eng), id(ctr))
            if self.seen.get(key, -1) >= val:
                continue
            self.seen[key] = val
            eng.wait_ge(ctr.sem, val)


def _sel_const():
    sel = np.zeros((NEXP, NEXP * 128), dtype=BF)
    for e in range(NEXP):
        sel[e, e * 128:(e + 1) * 128] = 1
    return sel


def build_program(phases=("f1", "at", "f2")):
    nc = bass.Bass("TRN2", target_bir_lowering=False, debug=False)
    kb = KB(nc)
    t, v, s, g, q = kb.t, kb.v, kb.s, kb.g, kb.q

    ap = {}
    def din(name, shape, dtype=F32):
        ap[name] = nc.dram_tensor(name, list(shape), dtype, kind="ExternalInput").ap()
        return ap[name]

    X = din("X", (D, T))
    for nm in ("wqT", "wkT", "wvT", "woT", "sW1", "sW2"):
        din(nm, (D, D))
    din("eW1", (NEXP, 128, NK, D))
    din("eW2", (NEXP, 128, NK, D))
    din("eb1sp", (NEXP * 128, NK))
    din("rW", (128, NK * NEXP))
    din("rb", (NEXP, 1))
    for nm in ("bq", "bk", "bv", "bo", "sb1", "sb2"):
        din(nm, (128, NK))
    din("eb1", (128, NEXP * NK))
    din("eb2", (NEXP, D))
    Y = nc.dram_tensor("Y", [D, T], F32, kind="ExternalOutput").ap()

    X1 = nc.dram_tensor("X1", [D, T], F32, kind="Internal").ap()
    X2 = nc.dram_tensor("X2", [D, T], F32, kind="Internal").ap()
    wbf = {}
    for nm in ("wqT", "wkT", "wvT", "woT"):
        wbf[nm] = nc.dram_tensor(nm + "_bf", [D, D], BF16, kind="Internal").ap()
    eW12 = nc.dram_tensor("eW12_bf", [NEXP, 128, 2, NK, D], BF16, kind="Internal").ap()
    wbf["eW12"] = eW12
    wbf["eW1"] = eW12[:, :, 0]
    wbf["eW2"] = eW12[:, :, 1]
    wbf["eb2d"] = nc.dram_tensor("eb2_bfd", [NEXP, D], BF16, kind="Internal").ap()
    NSLOT = 63 * 256
    wbf["Ud"] = nc.dram_tensor("Ud", [T, D], BF16, kind="Internal").ap()
    wbf["metad"] = nc.dram_tensor("metad", [NSLOT, 16], F32, kind="Internal").ap()
    wbf["R3d"] = nc.dram_tensor("R3d", [3 * T + 2, D], BF16, kind="Internal").ap()
    wbf["eb2j"] = nc.dram_tensor("eb2j", [64, D], BF16, kind="Internal").ap()

    ident_bf_d = nc.inline_tensor(np.eye(128, dtype=BF), name="c_ident_bf").ap()
    ident_f_d = nc.inline_tensor(np.eye(128, dtype=np.float32), name="c_ident_f").ap()
    triu_d = nc.inline_tensor(np.triu(np.ones((128, 128), dtype=np.float32)).astype(BF),
                              name="c_triu").ap()
    sel_d = nc.inline_tensor(_sel_const(), name="c_sel").ap()
    ones_col_d = nc.inline_tensor(np.ones((128, 1), dtype=BF), name="c_ones_col").ap()
    ones_row_d = nc.inline_tensor(np.ones((1, 128), dtype=BF), name="c_ones_row").ap()
    iota128c_d = nc.inline_tensor(np.arange(128, dtype=np.float32).reshape(128, 1), name="c_iota128c").ap()
    iota15b_d = nc.inline_tensor(np.tile(np.arange(NEXP, dtype=np.float32), (128, 1)), name="c_iota15b").ap()
    iota64r_d = nc.inline_tensor(np.arange(64, dtype=np.float32).reshape(1, 64), name="c_iota64r").ap()
    ones15f_d = nc.inline_tensor(np.ones((NEXP, 1), dtype=np.float32), name="c_ones15f").ap()
    ident15f_d = nc.inline_tensor(np.eye(NEXP, dtype=np.float32), name="c_ident15f").ap()

    if DEBUG_DUMP:
        wbf["dbg"] = nc.dram_tensor("DBG", [128, 132], F32, kind="ExternalOutput").ap()
    pio = {"f1": [X, X1], "at": [X1, X2], "f2": [X2, Y]}
    pio[phases[-1]][1] = Y

    with (
        nc.sbuf_tensor("consts", [128, 8], F32) as consts,
        nc.sbuf_tensor("negone", [128, 16], F32) as negone,
        nc.sbuf_tensor("ones_col", [128, 1], BF16) as ones_col,
        nc.sbuf_tensor("ones_row", [1, 128], BF16) as ones_row,
        nc.sbuf_tensor("ident_bf", [128, 128], BF16) as ident_bf,
        nc.sbuf_tensor("ident_f", [128, 128], F32) as ident_f,
        nc.sbuf_tensor("triu", [128, 128], BF16) as triu,
        nc.sbuf_tensor("sel", [NEXP, NEXP * 128], BF16) as sel,
        nc.sbuf_tensor("rw_bf", [128, NK * NEXP], BF16) as rw_bf,
        nc.sbuf_tensor("rb_s", [NEXP, 1], F32) as rb_s,
        nc.sbuf_tensor("eb2_bf", [NEXP, D], BF16) as eb2_bf,
        nc.sbuf_tensor("bcols", [128, 8 * NK], F32) as bcols,
        nc.sbuf_tensor("eb1_s", [128, NEXP * NK], F32) as eb1_s,
        nc.sbuf_tensor("iota128c", [128, 1], F32) as iota128c,
        nc.sbuf_tensor("iota15b", [128, NEXP], F32) as iota15b,
        nc.sbuf_tensor("iota64r", [1, 64], F32) as iota64r,
        nc.sbuf_tensor("ones15f", [NEXP, 1], F32) as ones15f,
        nc.sbuf_tensor("ident15f", [NEXP, NEXP], F32) as ident15f,
    ):
        for val, idx in ((0.5, 0), (-0.5, 1), (1.0 / 1024, 2), (1e-5, 3), (1e-6, 4)):
            nc.const_aps.register(F32, val, consts[:, idx:idx + 1])

        col = lambda buf, i, k: buf[:, i * NK + k:i * NK + k + 1]
        bias = dict(
            bq=lambda k: col(bcols, 0, k), bk=lambda k: col(bcols, 1, k),
            bv=lambda k: col(bcols, 2, k), bo=lambda k: col(bcols, 3, k),
            sb1=lambda k: col(bcols, 4, k), sb2=lambda k: col(bcols, 5, k),
            nbq=lambda k: col(bcols, 6, k), nbk=lambda k: col(bcols, 7, k),
            eb1=lambda e, k: eb1_s[:, e * NK + k:e * NK + k + 1],
        )

        for val, idx in ((0.5, 0), (-0.5, 1), (1.0 / 1024, 2), (1e-5, 3), (1e-6, 4)):
            v.memset(consts[:, idx:idx + 1], float(val))
        init_vv = kb.vv.inc(v.memset(negone[:], -1.0))

        ld_tok = None
        for dst_t, src_ap in (
            (ones_col[:], ones_col_d), (ones_row[:], ones_row_d),
            (ident_bf[:], ident_bf_d), (ident_f[:], ident_f_d), (triu[:], triu_d),
            (sel[:], sel_d), (rb_s[:], ap["rb"]), (eb1_s[:], ap["eb1"]),
            (iota128c[:], iota128c_d), (iota15b[:], iota15b_d),
            (iota64r[:], iota64r_d), (ones15f[:], ones15f_d), (ident15f[:], ident15f_d),
        ):
            ld_tok = kb.dq.inc(q.dma_start(out=dst_t, in_=src_ap), 16)
        for i, nm in enumerate(("bq", "bk", "bv", "bo", "sb1", "sb2")):
            ld_tok = kb.dq.inc(q.dma_start(out=bcols[:, i * NK:(i + 1) * NK], in_=ap[nm]), 16)

        g.dma_start(out=rw_bf[:], in_=ap["rW"]).then_inc(kb.dbg[0].sem, 16)
        kb.dbg[0].n += 16
        rw_tok = kb.dbg[0].inc(g.dma_start(out=eb2_bf[:], in_=ap["eb2"]), 16)

        kb.W(v, ld_tok)
        v.tensor_scalar(bcols[:, 6 * NK:7 * NK], bcols[:, 0 * NK:1 * NK], -1.0, None, ALU.mult)
        bias_tok = kb.vv.inc(
            v.tensor_scalar(bcols[:, 7 * NK:8 * NK], bcols[:, 1 * NK:2 * NK], -1.0, None, ALU.mult))

        cast_tok = {}
        cast_jobs = []
        eb2d_tok = kb.dbg[1].inc(g.dma_start(out=wbf["eb2d"], in_=ap["eb2"]), 16)

        def mk_ecast(e):
            def run():
                kb.ecast[e].inc(g.dma_start(out=wbf["eW1"][e], in_=ap["eW1"][e]), 16)
                cast_tok[e] = kb.ecast[e].inc(g.dma_start(out=wbf["eW2"][e], in_=ap["eW2"][e]), 16)
            return run

        def mk_acast(nm):
            def run():
                cast_tok[nm] = kb.acast.inc(g.dma_start(out=wbf[nm], in_=ap[nm]), 16)
            return run
        for e in range(NEXP):
            cast_jobs.append(mk_ecast(e))
        for nm in ("wqT", "wkT", "wvT", "woT"):
            cast_jobs.append(mk_acast(nm))

        toks = dict(bias=bias_tok, rw=rw_tok, init=init_vv, cast=cast_tok,
                    eb2d=eb2d_tok, cast_jobs=cast_jobs)
        env = dict(consts=consts, negone=negone, ones_col=ones_col,
                   ones_row=ones_row, ident_bf=ident_bf, ident_f=ident_f,
                   triu=triu, sel=sel, rw_bf=rw_bf, rb_s=rb_s, eb2_bf=eb2_bf,
                   bias=bias, iota128c=iota128c, iota15b=iota15b,
                   iota64r=iota64r, ones15f=ones15f, ident15f=ident15f)

        for ph in phases:
            src, dstp = pio[ph]
            if ph in ("f1", "f2"):
                (emit_ffn_sparse if SPARSE else emit_ffn)(nc, kb, ap, wbf, src, dstp, env, toks)
            else:
                while cast_jobs:
                    cast_jobs.pop(0)()
                emit_attn(nc, kb, ap, wbf, src, dstp, env, toks)

        for ctr in (kb.dq, kb.dg, kb.acast, *kb.ecast, *kb.dxp, *kb.dbp,
                    *kb.dwp, *kb.dsp, *kb.dU, *kb.dug, kb.dsc, *kb.dscp, *kb.dmcp, *kb.dwg,
                    *kb.dbg, *kb.dmt, *kb.dut, *kb.dy, *kb.dR, kb.dsen):
            kb.W(q, ctr.tok())
    return nc




def emit_ffn_sparse(nc, kb, ap, wbf, src, dst, env, toks):
    t, v, s, g, q = kb.t, kb.v, kb.s, kb.g, kb.q
    mm, ev, vv = kb.mm, kb.ev, kb.vv
    bias = env["bias"]
    src3 = src.rearrange("(k p) n -> p k n", p=128)
    dst3 = dst.rearrange("(k p) n -> p k n", p=128)
    NJ = 63
    CAP = 256
    NSLOT = NJ * CAP
    SENT = float(3 * T)
    NSLAB = T // 128
    w12f = wbf["eW12"].rearrange("e p t c n -> (e p) (t c n)")
    eb1f = ap["eb1sp"]
    entry = [kb.dsp[0].tok(), kb.dsp[1].tok(), kb.dy[0].tok(), kb.dy[1].tok(), kb.dR[0].tok(), kb.dR[1].tok()]
    octx = ExitStack()
    widx = octx.enter_context(nc.sbuf_tensor(None, [128, 64], mybir.dt.int32))
    with ExitStack() as ctx:
        en = ctx.enter_context
        xt = en(nc.sbuf_tensor(None, [128, 2, NK, FT], F32))
        xbf = en(nc.sbuf_tensor(None, [128, 2, NK, FT], BF16))
        x2b = en(nc.sbuf_tensor(None, [128, NK, FT], BF16))
        u = en(nc.sbuf_tensor(None, [128, 2, NK, FT], BF16))
        t1 = en(nc.sbuf_tensor(None, [128, 2, NK, FT], F32))
        bcast = en(nc.sbuf_tensor(None, [128, 2 * FT], F32))
        srow = en(nc.sbuf_tensor(None, [1, 3 * FT], F32))
        srbf = en(nc.sbuf_tensor(None, [1, 2 * FT], BF16))
        Lrt = en(nc.sbuf_tensor(None, [NEXP, FT], F32))
        Ltm = en(nc.sbuf_tensor(None, [128, 2, NEXP], F32))
        smx = en(nc.sbuf_tensor(None, [128, 2, 4, NEXP], F32))
        smk = en(nc.sbuf_tensor(None, [128, 2, NEXP], mybir.dt.int32))
        smr = en(nc.sbuf_tensor(None, [128, 2, 8], F32))
        keepm = en(nc.sbuf_tensor(None, [128, 2, NEXP], F32))
        wmtm = en(nc.sbuf_tensor(None, [128, 2, NEXP], F32))
        u_tm = en(nc.sbuf_tensor(None, [128, 2, D], BF16))
        sw1r = en(nc.sbuf_tensor(None, [128, NK, D], BF16))
        sw2r = en(nc.sbuf_tensor(None, [128, NK, D], BF16))
        wmfm = en(nc.sbuf_tensor(None, [NEXP, 3, T], BF16))
        ecols = en(nc.sbuf_tensor(None, [128, NSLAB, 4], F32))
        ccols = en(nc.sbuf_tensor(None, [128, NSLAB, 4], F32))
        msum = en(nc.sbuf_tensor(None, [15, T], F32))
        rankf = en(nc.sbuf_tensor(None, [15, T], F32))
        prodf = en(nc.sbuf_tensor(None, [15, T], F32))
        slotw = en(nc.sbuf_tensor(None, [1, T], F32))
        r15c = en(nc.sbuf_tensor(None, [15, 4], F32))
        rA = en(nc.sbuf_tensor(None, [1, 64], F32))
        rB = en(nc.sbuf_tensor(None, [1, 64], F32))
        rC = en(nc.sbuf_tensor(None, [1, 64], F32))
        rI = en(nc.sbuf_tensor(None, [1, 64], mybir.dt.int32))
        rBf = en(nc.sbuf_tensor(None, [1, 64], BF16))
        ecall = en(nc.sbuf_tensor(None, [128, 64], F32))
        ej32 = en(nc.sbuf_tensor(None, [64, 1], mybir.dt.int32))
        eb2all = en(nc.sbuf_tensor(None, [64, D], BF16))
        sentf = en(nc.sbuf_tensor(None, [128, 1024], F32))
        mcall = en(nc.sbuf_tensor(None, [128, 3, 32, 16], F32))
        slc32 = en(nc.sbuf_tensor(None, [128, 2, 32], mybir.dt.int32))
        acc = en(nc.psum_tensor(None, [128, NK * FT], F32))
        mA = en(nc.psum_tensor(None, [128, 512], F32))
        mB = en(nc.psum_tensor(None, [128, 512], F32))

        for eng in (t, v, s, g, q):
            kb.W(eng, *entry)
        kb.W(s, toks["init"], toks["bias"])

        g.dma_start(out=sw1r[:], in_=ap["sW1"].rearrange("(c p) n -> p c n", p=128)).then_inc(kb.dg.sem, 16)
        kb.dg.n += 16
        sw_tok = kb.dg.inc(g.dma_start(out=sw2r[:], in_=ap["sW2"].rearrange("(c p) n -> p c n", p=128)), 16)

        accs = lambda i: acc[:, i * FT:(i + 1) * FT]
        s1t = lambda dk, hk: sw1r[:, dk, hk * 128:(hk + 1) * 128]
        s2t = lambda hk, ik: sw2r[:, hk, ik * 128:(ik + 1) * 128]
        m_row = srow[0:1, 0:FT]
        msq_row = srow[0:1, FT:2 * FT]
        var_row = srow[0:1, 2 * FT:3 * FT]
        rstd_bf = srbf[0:1, 0:FT]
        mrstd_bf = srbf[0:1, FT:2 * FT]

        xtok = [None, None]
        xbtok = [None, None]
        x2tok = None
        srow_w = None
        srbf_w = None
        bc_rd = None
        u_rd = [None, None]
        t1_rd = [None, None]
        mA_free = None
        mB_free = None
        acc_free = None
        hps_rd = [None, None]
        utp_rd = None
        utm_rd = [None, None]
        ltm_w = None
        cwtm_w = None
        wm_rd = None

        def mk_fill(ki3, sb3):
            def run():
                if (ki3, sb3) == (0, 0):
                    kb.W(g, kb.dq.tok(), kb.dmcp[0].tok(), kb.dmcp[1].tok())
                g.tensor_scalar(mcall[:, ki3, sb3, 1:2], env["iota128c"][:],
                                float(ki3 * T + sb3 * 128), None, ALU.add)
                kb.pp.inc(g.tensor_scalar(mcall[:, ki3, sb3, 2:3], env["iota128c"][:],
                                          float(sb3 * 128), None, ALU.add))
            return run
        fill_jobs = [mk_fill(ki3, sb3) for ki3 in range(3) for sb3 in range(NSLAB)]
        smem0 = kb.vv.inc(v.memset(sentf[:], SENT))
        kb.W(q, smem0)
        sen_tok = None
        meta3 = wbf["metad"].rearrange("(a p) n -> p a n", p=128)
        for i in range(2):
            sen_tok = kb.dsen.inc(
                q.dma_start(out=meta3[:, i * 63:(i + 1) * 63, :],
                            in_=sentf[:, 0:63 * 16].rearrange("p (a n) -> p a n", n=16)), 16)

        def load_x(ti):
            p = ti % 2
            kb.W(q, xtok[p])
            a_tok = kb.dxp[p].inc(q.dma_start(out=xt[:, p], in_=src3[:, :, ti * FT:(ti + 1) * FT]), 16)
            kb.W(g, xbtok[p])
            b_tok = kb.dbp[p].inc(g.dma_start(out=xbf[:, p], in_=src3[:, :, ti * FT:(ti + 1) * FT]), 16)
            return a_tok, b_tok

        x_next = load_x(0)
        for ti in range(NFT):
            p = ti % 2
            x_tok, xb_tok = x_next
            if ti + 1 < NFT:
                x_next = load_x(ti + 1)
            if toks["cast_jobs"]:
                toks["cast_jobs"].pop(0)()
            for _ in range(7):
                if fill_jobs:
                    fill_jobs.pop(0)()

            kb.W(s, x_tok, x2tok)
            for k in range(NK):
                ii = s.activation(x2b[:, k], xt[:, p, k], AF.Square)
            sq_tok = ev.inc(ii)
            kb.W(t, xb_tok, sq_tok, mA_free)
            for k in range(NK):
                t.matmul(mA[0:1, 0:FT], env["ones_col"][:], xbf[:, p, k],
                         start=(k == 0), stop=(k == NK - 1))
            for k in range(NK):
                ii = t.matmul(mA[0:1, FT:2 * FT], env["ones_col"][:], x2b[:, k],
                              start=(k == 0), stop=(k == NK - 1))
            stats_tok = mm.inc(ii)
            xbtok[p] = stats_tok
            kb.W(v, stats_tok, srow_w)
            c_tok = vv.inc(v.tensor_scalar(m_row, mA[0:1, 0:FT], 1.0 / 1024, None, ALU.mult))
            kb.W(v, c_tok)
            c_tok = vv.inc(v.tensor_mul(msq_row, m_row, m_row))
            kb.W(v, c_tok)
            c_tok = vv.inc(v.scalar_tensor_tensor(var_row, mA[0:1, FT:2 * FT], 1.0 / 1024,
                                                  msq_row, ALU.mult, ALU.subtract))
            kb.W(s, c_tok)
            e_tok = ev.inc(s.activation(var_row, var_row, AF.Ln, bias=1e-5))
            kb.W(s, e_tok, srbf_w)
            e_tok = ev.inc(s.activation(rstd_bf, var_row, AF.Exp, scale=-0.5))
            kb.W(v, e_tok)
            c_tok = vv.inc(v.tensor_mul(mrstd_bf, m_row, rstd_bf))
            srow_w = c_tok
            kb.W(t, c_tok)
            t.matmul(mA[:, 0:FT], env["ones_row"][:], rstd_bf, start=True, stop=True)
            ii = t.matmul(mA[:, FT:2 * FT], env["ones_row"][:], mrstd_bf, start=True, stop=True)
            bc_tok = mm.inc(ii)
            srbf_w = bc_tok
            kb.W(s, bc_tok, bc_rd)
            s.copy(bcast[:, 0:FT], mA[:, 0:FT])
            bce_tok = ev.inc(s.copy(bcast[:, FT:2 * FT], mA[:, FT:2 * FT]))
            mA_free = bce_tok
            kb.W(v, bce_tok, u_rd[p], t1_rd[p])
            for k in range(NK):
                ii = v.tensor_mul(t1[:, p, k], xt[:, p, k], bcast[:, 0:FT])
            mul_tok = vv.inc(ii)
            kb.W(v, mul_tok)
            for k in range(NK):
                ii = v.tensor_sub(u[:, p, k], t1[:, p, k], bcast[:, FT:2 * FT])
            u_tok = vv.inc(ii)
            bc_rd = u_tok

            with nc.psum_tensor("utp%d" % (ti % 2), [128, D], BF16, allow_name_mangling=True) as utp:
                for cc in range(2):
                    kb.W(t, u_tok, utp_rd)
                    for dk in range(NK):
                        ii = t.transpose(utp[:, dk * 128:(dk + 1) * 128],
                                         u[:, p, dk, cc * 128:(cc + 1) * 128], env["ident_bf"][:])
                    tr_tok = mm.inc(ii)
                    kb.W(v, tr_tok, utm_rd[cc])
                    uc_tok = vv.inc(v.tensor_copy(u_tm[:, cc], utp[:]))
                    utp_rd = uc_tok
                    kb.W(q, uc_tok)
                    utm_rd[cc] = kb.dU[cc].inc(
                        q.dma_start(out=wbf["Ud"][ti * FT + cc * 128: ti * FT + (cc + 1) * 128, :],
                                    in_=u_tm[:, cc]), 16)


            with nc.psum_tensor("hps%d" % (ti % 2), [128, 1024], F32, allow_name_mangling=True) as hps:
                hsl = lambda b: hps[:, b * 512:b * 512 + FT]
                h_rd = mm.tok()
                for j in range(NK):
                    slot = j % 2
                    kb.W(t, sw_tok, u_tok, hps_rd[slot], utp_rd)
                    for dk in range(NK):
                        ii = t.matmul(hsl(slot), s1t(dk, j), u[:, p, dk],
                                      start=(dk == 0), stop=(dk == NK - 1))
                    w1tok = mm.inc(ii)
                    kb.W(s, w1tok)
                    ii = s.activation(x2b[:, j], hsl(slot), AF.Relu, bias=bias["sb1"](j))
                    rl_tok = ev.inc(ii)
                    hps_rd[slot] = rl_tok
                    kb.W(t, rl_tok)
                    if j == 0:
                        kb.W(t, acc_free)
                    for ik in range(NK):
                        ii = t.matmul(accs(ik), s2t(j, ik), x2b[:, j],
                                      start=(j == 0 and ik % 2 == 0),
                                      stop=(j == NK - 1 and ik % 2 == 1))
                    mm.inc(ii)
            x2tok = mm.tok()

            kb.W(t, u_tok, toks["rw"], mB_free)
            for k in range(NK):
                ii = t.matmul(mB[0:NEXP, 0:FT], env["rw_bf"][:, k * NEXP:(k + 1) * NEXP],
                              u[:, p, k], start=(k == 0), stop=(k == NK - 1))
            rt_tok = mm.inc(ii)
            kb.W(v, rt_tok)
            lrt_tok = vv.inc(v.tensor_scalar(Lrt[:], mB[0:NEXP, 0:FT], env["rb_s"][:], None, ALU.add))
            kb.W(t, lrt_tok, ltm_w)
            t.transpose(mB[:, FT:FT + NEXP], Lrt[:, 0:128], env["ident_f"][0:NEXP, 0:NEXP])
            ii = t.transpose(mB[:, FT + NEXP:FT + 2 * NEXP], Lrt[:, 128:256],
                             env["ident_f"][0:NEXP, 0:NEXP])
            ltr_tok = mm.inc(ii)
            kb.W(v, ltr_tok)
            v.tensor_copy(Ltm[:, 0], mB[:, FT:FT + NEXP])
            ltm_tok = vv.inc(v.tensor_copy(Ltm[:, 1], mB[:, FT + NEXP:FT + 2 * NEXP]))
            ltm_w = ltm_tok

            P = lambda sl: smx[:, sl, 0]
            P1 = lambda sl: smx[:, sl, 1]
            P2 = lambda sl: smx[:, sl, 2]
            WMT = lambda sl: smx[:, sl, 3]
            MK = lambda sl: smk[:, sl]
            m1n = lambda sl: smr[:, sl, 0:1]
            m1p = lambda sl: smr[:, sl, 1:2]
            m2r = lambda sl: smr[:, sl, 2:3]
            m3r = lambda sl: smr[:, sl, 3:4]
            rr = lambda sl: smr[:, sl, 4:5]
            ek_c = lambda sl, k: ecols[:, ti * 2 + sl, k:k + 1]
            cw_c = lambda sl, k: ccols[:, ti * 2 + sl, k:k + 1]

            kb.W(v, ltm_tok, toks["init"])
            for sl in (0, 1):
                ii = v.reduce_max(m1n(sl), Ltm[:, sl], axis=AX.X, negate=True)
            c_tok = vv.inc(ii)
            kb.W(s, c_tok)
            for sl in (0, 1):
                ii = s.activation(P(sl), Ltm[:, sl], AF.Exp, bias=m1n(sl))
            e_tok = ev.inc(ii)
            kb.W(v, e_tok)
            for sl in (0, 1):
                v.reduce_sum(rr(sl), P(sl), axis=AX.X)
            for sl in (0, 1):
                ii = v.reduce_max(m1p(sl), P(sl), axis=AX.X)
            c_tok = vv.inc(ii)
            kb.W(v, c_tok)
            for sl in (0, 1):
                ii = v.reciprocal(rr(sl), rr(sl))
            c_tok = vv.inc(ii)
            kb.W(v, c_tok)
            for sl in (0, 1):
                ii = v.tensor_scalar(WMT(sl), P(sl), m1p(sl), None, ALU.is_ge)
            c_tok = vv.inc(ii)
            kb.W(v, c_tok)
            for sl in (0, 1):
                ii = v.scalar_tensor_tensor(P1(sl), WMT(sl), -2.0, P(sl), ALU.mult, ALU.add)
            c_tok = vv.inc(ii)
            kb.W(v, c_tok)
            for sl in (0, 1):
                ii = v.reduce_max(m2r(sl), P1(sl), axis=AX.X)
            c_tok = vv.inc(ii)
            kb.W(v, c_tok)
            for sl in (0, 1):
                ii = v.tensor_scalar(WMT(sl), P1(sl), m2r(sl), None, ALU.is_ge)
            c_tok = vv.inc(ii)
            kb.W(v, c_tok)
            for sl in (0, 1):
                ii = v.scalar_tensor_tensor(P2(sl), WMT(sl), -2.0, P1(sl), ALU.mult, ALU.add)
            c_tok = vv.inc(ii)
            kb.W(v, c_tok)
            for sl in (0, 1):
                ii = v.reduce_max(m3r(sl), P2(sl), axis=AX.X)
            top_tok = vv.inc(ii)

            way_tok = top_tok
            for ki, mk in ((0, m1p), (1, m2r), (2, m3r)):
                kb.W(v, way_tok)
                for sl in (0, 1):
                    ii = v.tensor_scalar(wmtm[:, sl], P(sl), mk(sl), None, ALU.is_equal)
                wt1 = vv.inc(ii)
                kb.W(v, wt1)
                for sl in (0, 1):
                    ii = v.tensor_mul(WMT(sl), wmtm[:, sl], env["iota15b"][:])
                wt2 = vv.inc(ii)
                kb.W(v, wt2)
                for sl in (0, 1):
                    ii = v.reduce_sum(ek_c(sl, ki), WMT(sl), axis=AX.X)
                vv.inc(ii)
                for sl in (0, 1):
                    ii = v.tensor_scalar(cw_c(sl, ki), mk(sl), rr(sl), 0.5, ALU.mult, ALU.mult)
                wt3 = vv.inc(ii)
                kb.W(t, wt1, cwtm_w)
                for sl in (0, 1):
                    ii = t.transpose(mB[0:NEXP, 256 + sl * 128: 256 + (sl + 1) * 128],
                                     wmtm[:, sl], env["ident_f"][:])
                wt4 = mm.inc(ii)
                kb.W(v, wt4)
                for sl in (0, 1):
                    stile = ti * 2 + sl
                    ii = v.tensor_copy(wmfm[:, ki, stile * 128:(stile + 1) * 128],
                                       mB[0:NEXP, 256 + sl * 128:256 + (sl + 1) * 128])
                way_tok = vv.inc(ii)
                cwtm_w = way_tok
            wm_rd = way_tok
            mB_free = way_tok


            acc_tok = mm.tok()
            kb.W(v, acc_tok, t1_rd[p])
            for ik in range(NK):
                ii = v.tensor_scalar(t1[:, p, ik], accs(ik), bias["sb2"](ik), 0.5, ALU.add, ALU.mult)
            fe_tok = vv.inc(ii)
            acc_free = fe_tok
            kb.W(v, fe_tok)
            for ik in range(NK):
                ii = v.tensor_add(xt[:, p, ik], xt[:, p, ik], t1[:, p, ik])
            add_tok = vv.inc(ii)
            t1_rd[p] = add_tok
            u_rd[p] = mm.tok()
            kb.W(q, add_tok)
            kb.dst.n += 16
            xtok[p] = kb.dsp[p].inc(q.dma_start(out=dst3[:, :, ti * FT:(ti + 1) * FT], in_=xt[:, p]), 16)

        while toks["cast_jobs"]:
            toks["cast_jobs"].pop(0)()
        while fill_jobs:
            fill_jobs.pop(0)()
        static_fill = kb.pp.tok()

        if STAGE == "A":
            ctx.close()
            octx.close()
            return
        kb.W(v, wm_rd)
        c_tok = vv.inc(v.tensor_add(msum[:], wmfm[:, 0, :], wmfm[:, 1, :]))
        kb.W(v, c_tok)
        ms_tok = vv.inc(v.tensor_add(msum[:], msum[:], wmfm[:, 2, :]))
        kb.W(v, ms_tok)
        cnt_tok = vv.inc(v.reduce_sum(r15c[:, 0:1], msum[:], axis=AX.X))
        kb.W(t, cnt_tok, mA_free)
        ii = t.transpose(mA[0:1, 0:NEXP], r15c[:, 0:1], env["ident15f"][:])
        ct_tok = mm.inc(ii)
        kb.W(v, ct_tok)
        c_tok = vv.inc(v.tensor_copy(rA[0:1, 0:NEXP], mA[0:1, 0:NEXP]))
        kb.W(v, c_tok)
        c_tok = vv.inc(v.memset(rB[0:1, 0:NEXP], 0.0))
        for m2 in range(16):
            kb.W(v, c_tok)
            c_tok = vv.inc(v.tensor_scalar(rC[0:1, 0:NEXP], rA[0:1, 0:NEXP],
                                           float(256 * m2), None, ALU.is_gt))
            kb.W(v, c_tok)
            c_tok = vv.inc(v.tensor_add(rB[0:1, 0:NEXP], rB[0:1, 0:NEXP], rC[0:1, 0:NEXP]))
        kb.W(v, c_tok)
        c_tok = vv.inc(v.memset(rC[0:1, 0:NEXP], 0.0))
        kb.W(v, c_tok)
        c_tok = vv.inc(v.tensor_tensor_scan(rA[0:1, 0:NEXP], rB[0:1, 0:NEXP],
                                            rC[0:1, 0:NEXP], 0.0, ALU.add, ALU.add))
        kb.W(v, c_tok)
        jb_tok = vv.inc(v.tensor_sub(rA[0:1, 0:NEXP], rA[0:1, 0:NEXP], rB[0:1, 0:NEXP]))
        kb.W(v, jb_tok)
        c_tok = vv.inc(v.tensor_scalar(rC[0:1, 0:NEXP], rA[0:1, 0:NEXP], 256.0, None, ALU.mult))
        kb.W(t, c_tok)
        ii = t.transpose(mA[0:NEXP, 256:257], rC[0:1, 0:NEXP], env["ident_f"][0:1, 0:1])
        sb_tok = mm.inc(ii)
        kb.W(v, sb_tok)
        sbc_tok = vv.inc(v.tensor_copy(r15c[:, 1:2], mA[0:NEXP, 256:257]))
        kb.W(v, sbc_tok)
        c_tok = vv.inc(v.memset(rB[0:1, 0:64], 0.0))
        for e in range(NEXP):
            kb.W(v, c_tok)
            c_tok = vv.inc(v.tensor_scalar(rC[0:1, 0:64], env["iota64r"][:],
                                           rA[0:1, e:e + 1], None, ALU.is_ge))
            kb.W(v, c_tok)
            c_tok = vv.inc(v.tensor_add(rB[0:1, 0:64], rB[0:1, 0:64], rC[0:1, 0:64]))
        kb.W(v, c_tok)
        c_tok = vv.inc(v.tensor_scalar(rB[0:1, 0:64], rB[0:1, 0:64], -1.0, None, ALU.add))
        kb.W(v, c_tok)
        ebf_tok = vv.inc(v.tensor_copy(rBf[0:1, 0:64], rB[0:1, 0:64]))
        kb.W(t, ebf_tok)
        ii = t.matmul(mA[:, 384:448], env["ones_row"][:], rBf[0:1, 0:64], start=True, stop=True)
        ec_tok = mm.inc(ii)
        kb.W(v, ec_tok)
        ecall_tok = vv.inc(v.tensor_copy(ecall[:], mA[:, 384:448]))
        kb.W(v, ecall_tok)
        for j in range(NJ):
            ii = v.scalar_tensor_tensor(t1[:, 0, 0, j:j + 1], ecall[:, j:j + 1], 128.0,
                                        env["iota128c"][:], ALU.mult, ALU.add)
        c_tok = vv.inc(ii)
        kb.W(v, c_tok)
        wi_tok = vv.inc(v.tensor_copy(widx[:, 0:NJ], t1[:, 0, 0, 0:NJ]))
        kb.W(t, ecall_tok)
        ii = t.transpose(mA[0:64, 257:258], rB[0:1, 0:64], env["ident_f"][0:1, 0:1])
        ejt_tok = mm.inc(ii)
        kb.W(v, ejt_tok)
        ej_tok = vv.inc(v.tensor_copy(ej32[:], mA[0:64, 257:258]))
        kb.W(g, ej_tok, toks["eb2d"])
        g.indirect_dma_start(
            out=eb2all[:], out_offset=None, in_=wbf["eb2d"],
            in_offset=IndirectOffsetOnAxis(ap=ej32[:], axis=0)).then_inc(kb.dsc.sem, 16)
        kb.dsc.n += 16
        kb.W(q, kb.dsc.tok())
        ej_store = kb.dsen.inc(q.dma_start(out=wbf["eb2j"], in_=eb2all[:]), 16)
        kb.W(v, ms_tok)
        c_tok = vv.inc(v.memset(prodf[:], 0.0))
        kb.W(v, c_tok)
        c_tok = vv.inc(v.tensor_tensor_scan(rankf[:], msum[:], prodf[:], 0.0, ALU.add, ALU.add))
        kb.W(v, c_tok)
        c_tok = vv.inc(v.tensor_sub(rankf[:], rankf[:], msum[:]))
        kb.W(v, c_tok, sbc_tok)
        slot_tok = vv.inc(v.tensor_scalar(rankf[:], rankf[:], r15c[:, 1:2], None, ALU.add))
        interlude_mm = mm.tok()

        if DEBUG_DUMP:
            with nc.sbuf_tensor("dbgs", [128, 132], F32) as dbgs:
                kb.W(v, wi_tok, ecall_tok, ej_tok)
                v.tensor_copy(dbgs[:, 0:64], t1[:, 0, 0, 0:64])
                v.tensor_copy(dbgs[:, 64:128], widx[:, 0:64])
                d_tok = vv.inc(v.tensor_copy(dbgs[0:64, 129:130], ej32[:]))
                kb.W(q, d_tok)
                dmp = kb.dq.inc(q.dma_start(out=wbf["dbg"], in_=dbgs[:]), 16)
                kb.W(q, dmp)

        kb.W(g, kb.vv.tok())
        for ki in range(3):
            for sb in range(NSLAB):
                ii = g.tensor_copy(mcall[:, ki, sb, 0:1], ccols[:, sb, ki:ki + 1])
        cw_fill = kb.pp.inc(ii)
        scat_mm = interlude_mm
        sc32_tok = [None, None]
        way_scat = [None, None]
        mAcol_rd = None
        for ki in range(3):
            kp = ki % 2
            kb.W(v, slot_tok, scat_mm)
            pr_tok = vv.inc(v.tensor_mul(prodf[:], rankf[:], wmfm[:, ki, :]))
            sl_tok = None
            for pc in range(T // 512):
                kb.W(t, pr_tok, interlude_mm, mB_free)
                ii = t.matmul(mB[0:1, 0:512], env["ones15f"][:],
                              prodf[:, pc * 512:(pc + 1) * 512], start=True, stop=True)
                pm_tok = mm.inc(ii)
                kb.W(v, pm_tok)
                ii = v.tensor_copy(slotw[0:1, pc * 512:(pc + 1) * 512], mB[0:1, 0:512])
                sl_tok = vv.inc(ii)
                mB_free = sl_tok
            kb.W(t, sl_tok, ecall_tok, mAcol_rd)
            for sb in range(NSLAB):
                ii = t.transpose(mA[:, 448 + sb:449 + sb],
                                 slotw[0:1, sb * 128:(sb + 1) * 128], env["ident_f"][0:1, 0:1])
            st_tok = mm.inc(ii)
            kb.W(v, st_tok, way_scat[kp])
            sc32_tok[kp] = vv.inc(v.tensor_copy(slc32[:, kp], mA[:, 448:480]))
            mAcol_rd = sc32_tok[kp]
            kb.W(g, sc32_tok[kp], cw_fill, static_fill, ej_tok, kb.dsen.tok())
            ms2_tok = None
            for sb in range(NSLAB):
                ii = g.indirect_dma_start(
                    out=wbf["metad"],
                    out_offset=IndirectOffsetOnAxis(ap=slc32[:, kp, sb:sb + 1], axis=0),
                    in_=mcall[:, ki, sb], in_offset=None)
                ms2_tok = kb.dmcp[kp].inc(ii, 16)
            way_scat[kp] = ms2_tok
            scat_mm = mm.tok()
        scat_all = (kb.dmcp[0].tok(), kb.dmcp[1].tok())

        ctx.close()
        joinA = tuple(c.tok() for c in (
            kb.vv, kb.ev, kb.mm, kb.dsp[0], kb.dsp[1], kb.dxp[0],
            kb.dxp[1], kb.dbp[0], kb.dbp[1], kb.dg, kb.dU[0], kb.dU[1], kb.dug[0],
            kb.dug[1], kb.dsen, kb.dsc, *kb.dscp, *kb.dmcp, *kb.ecast) if c.n > 0)

        if STAGE == "scat":
            ctx.close()
            octx.close()
            return
        with ExitStack() as ctxb:
            enb = ctxb.enter_context
            wg = enb(nc.sbuf_tensor(None, [128, 2, 2, NK, D], BF16))
            ufm = enb(nc.sbuf_tensor(None, [128, 2, NK, CAP], BF16))
            u_tm2 = enb(nc.sbuf_tensor(None, [128, 2, 2, D], BF16))
            h2b = enb(nc.sbuf_tensor(None, [128, NK, CAP], BF16))
            eb1g = enb(nc.sbuf_tensor(None, [128, 2, NK], F32))
            eb2g = enb(nc.sbuf_tensor(None, [1, 2, D], BF16))
            mt = enb(nc.sbuf_tensor(None, [128, 2, 2, 16], F32))
            iy = enb(nc.sbuf_tensor(None, [128, 2, 2], mybir.dt.int32))
            iu = enb(nc.sbuf_tensor(None, [128, 2, 2], mybir.dt.int32))
            ysc = enb(nc.sbuf_tensor(None, [128, 2, 1024], BF16))
            hpsB = enb(nc.psum_tensor(None, [128, 1024], F32))
            yps = enb(nc.psum_tensor(None, [128, 2, 2, 512], F32))
            utpB = enb(nc.psum_tensor(None, [128, 2, 1024], BF16))

            wg_rd = [None, None]
            mt_rd = [None, None]
            h2_rd2 = {}
            hps_rd2 = [None, None]
            ysc_rd = {}
            iy_w = [None, None]
            iu_rd = [None, None]
            u_rd2 = [None, None]
            utk_arr = [None, None]
            cp_arr = [None, None]
            utp_rdB = [None]

            def load_job(j):
                pj = j % 2
                kb.W(g, wg_rd[pj], wi_tok, *scat_all, *joinA)
                g.indirect_dma_start(
                    out=wg[:, pj].rearrange("p t c n -> p (t c n)"), out_offset=None, in_=w12f,
                    in_offset=IndirectOffsetOnAxis(ap=widx[:, j:j + 1], axis=0)).then_inc(kb.dwg[pj].sem, 16)
                kb.dwg[pj].n += 16
                g.indirect_dma_start(
                    out=eb1g[:, pj], out_offset=None, in_=eb1f,
                    in_offset=IndirectOffsetOnAxis(ap=widx[:, j:j + 1], axis=0)).then_inc(kb.dwg[pj].sem, 16)
                kb.dwg[pj].n += 16
                wtk = kb.dwg[pj].tok()
                kb.W(q, mt_rd[pj], wg_rd[pj], *scat_all, ej_store, *joinA)
                q.dma_start(out=mt[:, pj], in_=wbf["metad"][j * CAP:(j + 1) * CAP, :]
                            .rearrange("(c p) n -> p c n", p=128)).then_inc(kb.dmt[pj].sem, 16)
                kb.dmt[pj].n += 16
                mtk = kb.dmt[pj].inc(
                    q.dma_start(out=eb2g[0:1, pj], in_=wbf["eb2j"][j:j + 1, :]), 16)
                return wtk, mtk

            def issue_u_gather(j):
                pj = j % 2
                kb.W(v, jl[j][1], iu_rd[pj])
                v.tensor_copy(iu[:, pj, 0:1], mt[:, pj, 0, 2:3])
                iu_tok = vv.inc(v.tensor_copy(iu[:, pj, 1:2], mt[:, pj, 1, 2:3]))
                kb.W(g, iu_tok, u_rd2[pj])
                utk = None
                for cc in range(2):
                    utk = kb.dut[pj].inc(g.indirect_dma_start(
                        out=u_tm2[:, pj, cc], out_offset=None, in_=wbf["Ud"],
                        in_offset=IndirectOffsetOnAxis(ap=iu[:, pj, cc:cc + 1], axis=0)), 16)
                utk_arr[pj] = utk
                iu_rd[pj] = utk

            jl = {0: load_job(0), 1: load_job(1)}
            if STAGE == "B0":
                ctxb.close()
                octx.close()
                return
            issue_u_gather(0)
            issue_u_gather(1)
            for j in range(NJ):
                pj = j % 2
                wtk, mtk = jl[j]
                if STAGE == "BL":
                    if j + 2 <= NJ - 1:
                        jl[j + 2] = load_job(j + 2)
                    continue
                if 2 <= j + 1 < NJ:
                    issue_u_gather(j + 1)
                kb.W(t, utk_arr[pj], utp_rdB[0])
                cp = None
                for cc in range(2):
                    for dk in range(NK):
                        ii = t.transpose(utpB[:, cc, dk * 128:(dk + 1) * 128],
                                         u_tm2[:, pj, cc, dk * 128:(dk + 1) * 128],
                                         env["ident_bf"][:])
                    tr_tok = mm.inc(ii)
                    kb.W(s, tr_tok)
                    cp = ev.inc(s.copy(
                        ufm[:, pj, :, cc * 128:(cc + 1) * 128],
                        utpB[:, cc].rearrange("p (k f) -> p k f", f=128)))
                u_rd2[pj] = tr_tok
                utp_rdB[0] = cp
                cp_arr[pj] = cp
                kb.W(v, mtk, iy_w[pj])
                c_tok = vv.inc(v.tensor_copy(iy[:, pj, 0:1], mt[:, pj, 0, 1:2]))
                kb.W(v, c_tok)
                c_tok = vv.inc(v.tensor_scalar(mt[:, pj, 1, 2:3], mt[:, pj, 1, 1:2],
                                               SENT, None, ALU.is_ge))
                kb.W(v, c_tok)
                c_tok = vv.inc(v.tensor_add(mt[:, pj, 1, 2:3], mt[:, pj, 1, 2:3],
                                            mt[:, pj, 1, 1:2]))
                kb.W(v, c_tok)
                iy_tok = vv.inc(v.tensor_copy(iy[:, pj, 1:2], mt[:, pj, 1, 2:3]))
                for hk in range(NK):
                    sl2 = hk % 2
                    kb.W(t, wtk, cp_arr[pj], hps_rd2[sl2])
                    for dk in range(NK):
                        ii = t.matmul(hpsB[:, sl2 * 512:sl2 * 512 + CAP],
                                      wg[:, pj, 0, dk, hk * 128:(hk + 1) * 128],
                                      ufm[:, pj, dk], start=(dk == 0), stop=(dk == NK - 1))
                    w1tok = mm.inc(ii)
                    kb.W(s, w1tok, h2_rd2.get(hk % 4))
                    ii = s.activation(h2b[:, hk], hpsB[:, sl2 * 512:sl2 * 512 + CAP],
                                      AF.Relu, bias=eb1g[:, pj, hk:hk + 1])
                    h2tok = ev.inc(ii)
                    hps_rd2[sl2] = h2tok
                    h2_rd2[hk % 4] = h2tok
                for sc2 in (() if STAGE == "BW1" else (0, 1)):
                    for hf in range(2):
                        kb.W(t, h2_rd2[3], mtk, ysc_rd.get((sc2, hf)))
                        for hk in range(NK):
                            kb.W(t, h2_rd2[hk % 4])
                            ii = t.matmul(yps[:, sc2, hf],
                                          h2b[:, hk, sc2 * 128:(sc2 + 1) * 128],
                                          wg[:, pj, 1, hk, hf * 512:(hf + 1) * 512],
                                          start=(hk == 0), stop=False)
                        ii = t.matmul(yps[:, sc2, hf], env["ones_row"][:],
                                      eb2g[0:1, pj, hf * 512:(hf + 1) * 512],
                                      start=False, stop=True)
                        w2tok = mm.inc(ii)
                        h2_rd2[3] = w2tok
                        kb.W(v, w2tok, iy_tok)
                        ii = v.tensor_scalar(ysc[:, sc2, hf * 512:(hf + 1) * 512],
                                             yps[:, sc2, hf], mt[:, pj, sc2, 0:1], None, ALU.mult)
                        ysck = vv.inc(ii)
                        ysc_rd[(sc2, hf)] = None
                        if hf == 1:
                            kb.W(g, ysck)
                            yst = kb.dy[sc2].inc(g.indirect_dma_start(
                                out=wbf["R3d"],
                                out_offset=IndirectOffsetOnAxis(ap=iy[:, pj, sc2:sc2 + 1], axis=0),
                                in_=ysc[:, sc2], in_offset=None), 16)
                            ysc_rd[(sc2, 0)] = yst
                            ysc_rd[(sc2, 1)] = yst
                iy_w[pj] = mm.tok()
                wg_rd[pj] = mm.tok()
                mt_rd[pj] = kb.vv.tok()
                if j + 2 <= NJ - 1:
                    jl[j + 2] = load_job(j + 2)

        if STAGE == "B":
            octx.close()
            return
        with ExitStack() as ctxc:
            enc = ctxc.enter_context
            NPC = 4
            r3s = enc(nc.sbuf_tensor(None, [128, NPC, 3, D], BF16))
            rsum = enc(nc.sbuf_tensor(None, [128, 2, D], F32))
            xc = enc(nc.sbuf_tensor(None, [128, NPC, NK, 128], F32))
            rtp = enc(nc.psum_tensor(None, [128, 2, D], F32))
            r3_rd = [None] * NPC
            rs_rd = [None, None]
            xc_rd = [None] * NPC
            rtp_rd = [None, None]
            ydone = (kb.dy[0].tok(), kb.dy[1].tok())

            def loadC(ch):
                par = ch % NPC
                p2 = ch % 2
                eng = q if p2 == 0 else s
                kb.W(eng, r3_rd[par], xc_rd[par], *ydone)
                rt = None
                for ki in range(3):
                    rt = kb.dR[p2].inc(
                        eng.dma_start(out=r3s[:, par, ki],
                                      in_=wbf["R3d"][ki * T + ch * 128: ki * T + (ch + 1) * 128, :]), 16)
                xl = kb.dxp[p2].inc(
                    eng.dma_start(out=xc[:, par], in_=dst3[:, :, ch * 128:(ch + 1) * 128]), 16)
                return rt, xl

            pre = {c: loadC(c) for c in range(min(NPC, NSLAB))}
            for ch in range(NSLAB):
                par = ch % NPC
                p2 = ch % 2
                rt, xl = pre[ch]
                kb.W(v, rt, rs_rd[p2])
                c_tok = vv.inc(v.tensor_add(rsum[:, p2], r3s[:, par, 0], r3s[:, par, 1]))
                kb.W(v, c_tok)
                rs_tok = vv.inc(v.tensor_add(rsum[:, p2], rsum[:, p2], r3s[:, par, 2]))
                r3_rd[par] = rs_tok
                kb.W(t, rs_tok, rtp_rd[p2])
                for dk in range(NK):
                    ii = t.transpose(rtp[:, p2, dk * 128:(dk + 1) * 128],
                                     rsum[:, p2, dk * 128:(dk + 1) * 128], env["ident_f"][:])
                rtr_tok = mm.inc(ii)
                rs_rd[p2] = rtr_tok
                kb.W(v, rtr_tok, xl)
                for dk in range(NK):
                    ii = v.tensor_add(xc[:, par, dk], xc[:, par, dk],
                                      rtp[:, p2, dk * 128:(dk + 1) * 128])
                xa_tok = vv.inc(ii)
                rtp_rd[p2] = xa_tok
                seng = q if p2 == 0 else s
                kb.W(seng, xa_tok)
                kb.dst.n += 16
                xc_rd[par] = kb.dsp[p2].inc(
                    seng.dma_start(out=dst3[:, :, ch * 128:(ch + 1) * 128], in_=xc[:, par]), 16)
                if ch + NPC < NSLAB:
                    pre[ch + NPC] = loadC(ch + NPC)
        kb.W(q, kb.dsp[0].tok(), kb.dsp[1].tok())
    octx.close()


def emit_ffn(nc, kb, ap, wbf, src, dst, env, toks):
    t, v, s, g, q = kb.t, kb.v, kb.s, kb.g, kb.q
    mm, ev, vv = kb.mm, kb.ev, kb.vv
    bias = env["bias"]
    src3 = src.rearrange("(k p) n -> p k n", p=128)
    dst3 = dst.rearrange("(k p) n -> p k n", p=128)

    entry = [kb.dsp[0].tok(), kb.dsp[1].tok()]
    with ExitStack() as ctx:
        xt = ctx.enter_context(nc.sbuf_tensor(None, [128, 2, NK, FT], F32))
        xbf = ctx.enter_context(nc.sbuf_tensor(None, [128, 2, NK, FT], BF16))
        x2bf = ctx.enter_context(nc.sbuf_tensor(None, [128, NK, FT], BF16))
        u = ctx.enter_context(nc.sbuf_tensor(None, [128, NK, FT], BF16))
        t1 = ctx.enter_context(nc.sbuf_tensor(None, [128, NK, FT], F32))
        bcast = ctx.enter_context(nc.sbuf_tensor(None, [128, 2 * FT], F32))
        srow = ctx.enter_context(nc.sbuf_tensor(None, [1, 3 * FT], F32))
        srbf = ctx.enter_context(nc.sbuf_tensor(None, [1, 2 * FT], BF16))
        Lrt = ctx.enter_context(nc.sbuf_tensor(None, [NEXP, FT], F32))
        Ltm = ctx.enter_context(nc.sbuf_tensor(None, [128, 2, NEXP], F32))
        smx = ctx.enter_context(nc.sbuf_tensor(None, [128, 2, 4, NEXP], F32))
        smk = ctx.enter_context(nc.sbuf_tensor(None, [128, 2, NEXP], mybir.dt.int32))
        smr = ctx.enter_context(nc.sbuf_tensor(None, [128, 2, 8], F32))
        cw_tm = ctx.enter_context(nc.sbuf_tensor(None, [128, 2, NEXP], F32))
        cw_fm = ctx.enter_context(nc.sbuf_tensor(None, [NEXP, FT], BF16))
        cwb = ctx.enter_context(nc.sbuf_tensor(None, [128, NEXP, FT], BF16))
        h_bf = ctx.enter_context(nc.sbuf_tensor(None, [128, 4, FT], BF16))
        h2_bf = ctx.enter_context(nc.sbuf_tensor(None, [128, 4, FT], BF16))
        sw1r = ctx.enter_context(nc.sbuf_tensor(None, [128, NK, D], BF16))
        sw2r = ctx.enter_context(nc.sbuf_tensor(None, [128, NK, D], BF16))
        w1b = ctx.enter_context(nc.sbuf_tensor(None, [128, 2, NK, D], BF16))
        w2b = ctx.enter_context(nc.sbuf_tensor(None, [128, 2, NK, D], BF16))
        acc = ctx.enter_context(nc.psum_tensor(None, [128, NK * FT], F32))
        hps = ctx.enter_context(nc.psum_tensor(None, [128, 1024], F32))
        mA = ctx.enter_context(nc.psum_tensor(None, [128, 512], F32))
        mB = ctx.enter_context(nc.psum_tensor(None, [128, 512], F32))
        for eng in (t, v, s, g, q):
            kb.W(eng, *entry)
        kb.W(s, toks["init"], toks["bias"])
        sw_tok = None
        g.dma_start(out=sw1r[:], in_=ap["sW1"].rearrange("(c p) n -> p c n", p=128)).then_inc(kb.dg.sem, 16)
        kb.dg.n += 16
        sw_tok = kb.dg.inc(g.dma_start(out=sw2r[:], in_=ap["sW2"].rearrange("(c p) n -> p c n", p=128)), 16)

        accs = lambda i: acc[:, i * FT:(i + 1) * FT]
        hsl = lambda b: hps[:, b * 512:b * 512 + FT]
        w1t = lambda pe, dk, hk: w1b[:, pe, dk, hk * 128:(hk + 1) * 128]
        w2t = lambda pe, hk, ik: w2b[:, pe, hk, ik * 128:(ik + 1) * 128]
        s1t = lambda dk, hk: sw1r[:, dk, hk * 128:(hk + 1) * 128]
        s2t = lambda hk, ik: sw2r[:, hk, ik * 128:(ik + 1) * 128]
        m_row = srow[0:1, 0:FT]
        msq_row = srow[0:1, FT:2 * FT]
        var_row = srow[0:1, 2 * FT:3 * FT]
        rstd_bf = srbf[0:1, 0:FT]
        mrstd_bf = srbf[0:1, FT:2 * FT]

        wtok = [None, None]
        xtok = [None, None]
        xbtok = [None, None]
        x2tok = None
        srow_w = None
        srbf_w = None
        bc_rd = None
        u_rd = None
        ltm_w = None
        cw_rd = None
        cwtm_w = None
        cwb_rd = [None] * NEXP
        mA_free = None
        mB_free = None
        acc_free = None
        t1_rd = None
        h_rd = [None] * 4
        h2_rd = [None] * 4
        hps_rd = [None, None]

        def load_x(ti):
            p = ti % 2
            kb.W(q, xtok[p])
            xt_tok = kb.dxp[p].inc(q.dma_start(out=xt[:, p], in_=src3[:, :, ti * FT:(ti + 1) * FT]), 16)
            kb.W(g, xbtok[p])
            xb_tok = kb.dbp[p].inc(g.dma_start(out=xbf[:, p], in_=src3[:, :, ti * FT:(ti + 1) * FT]), 16)
            return xt_tok, xb_tok

        def load_exp(e, ti):
            pe = e % 2
            if ti == 0:
                kb.W(q, toks["cast"][e])
            kb.W(q, wtok[pe])
            kb.dwp[pe].inc(q.dma_start(out=w1b[:, pe], in_=wbf["eW1"][e]), 16)
            return kb.dwp[pe].inc(q.dma_start(out=w2b[:, pe], in_=wbf["eW2"][e]), 16)

        x_next = load_x(0)
        for ti in range(NFT):
            p = ti % 2
            x_tok, xb_tok = x_next
            wld = {0: load_exp(0, ti), 1: load_exp(1, ti)}

            kb.W(s, x_tok, x2tok)
            for k in range(NK):
                ii = s.activation(x2bf[:, k], xt[:, p, k], AF.Square)
            sq_tok = ev.inc(ii)

            kb.W(t, xb_tok, sq_tok, mA_free)
            for k in range(NK):
                t.matmul(mA[0:1, 0:FT], env["ones_col"][:], xbf[:, p, k],
                         start=(k == 0), stop=(k == NK - 1))
            for k in range(NK):
                ii = t.matmul(mA[0:1, FT:2 * FT], env["ones_col"][:], x2bf[:, k],
                              start=(k == 0), stop=(k == NK - 1))
            stats_tok = mm.inc(ii)
            xbtok[p] = stats_tok
            x2tok = stats_tok

            kb.W(v, stats_tok, srow_w)
            c_tok = vv.inc(v.tensor_scalar(m_row, mA[0:1, 0:FT], 1.0 / 1024, None, ALU.mult))
            kb.W(v, c_tok)
            c_tok = vv.inc(v.tensor_mul(msq_row, m_row, m_row))
            kb.W(v, c_tok)
            c_tok = vv.inc(v.scalar_tensor_tensor(var_row, mA[0:1, FT:2 * FT], 1.0 / 1024,
                                                  msq_row, ALU.mult, ALU.subtract))
            kb.W(s, c_tok)
            e_tok = ev.inc(s.activation(var_row, var_row, AF.Ln, bias=1e-5))
            kb.W(s, e_tok, srbf_w)
            e_tok = ev.inc(s.activation(rstd_bf, var_row, AF.Exp, scale=-0.5))
            kb.W(v, e_tok)
            c_tok = vv.inc(v.tensor_mul(mrstd_bf, m_row, rstd_bf))
            srow_w = c_tok

            kb.W(t, c_tok)
            t.matmul(mA[:, 0:FT], env["ones_row"][:], rstd_bf, start=True, stop=True)
            ii = t.matmul(mA[:, FT:2 * FT], env["ones_row"][:], mrstd_bf, start=True, stop=True)
            bc_tok = mm.inc(ii)
            srbf_w = bc_tok
            kb.W(s, bc_tok, bc_rd)
            s.copy(bcast[:, 0:FT], mA[:, 0:FT])
            bce_tok = ev.inc(s.copy(bcast[:, FT:2 * FT], mA[:, FT:2 * FT]))

            kb.W(v, bce_tok, u_rd[p], t1_rd[p])
            for k in range(NK):
                ii = v.tensor_mul(t1[:, p, k], xt[:, p, k], bcast[:, 0:FT])
            mul_tok = vv.inc(ii)
            kb.W(v, mul_tok)
            for k in range(NK):
                ii = v.tensor_sub(u[:, p, k], t1[:, p, k], bcast[:, FT:2 * FT])
            u_tok = vv.inc(ii)
            bc_rd = u_tok

            kb.W(t, u_tok, toks["rw"], mB_free)
            for k in range(NK):
                ii = t.matmul(mB[0:NEXP, 0:FT], env["rw_bf"][:, k * NEXP:(k + 1) * NEXP],
                              u[:, p, k], start=(k == 0), stop=(k == NK - 1))
            rt_tok = mm.inc(ii)
            kb.W(v, rt_tok)
            lrt_tok = vv.inc(v.tensor_scalar(Lrt[:], mB[0:NEXP, 0:FT], env["rb_s"][:], None, ALU.add))
            kb.W(t, lrt_tok, ltm_w)
            t.transpose(mB[:, FT:FT + NEXP], Lrt[:, 0:128], env["ident_f"][0:NEXP, 0:NEXP])
            ii = t.transpose(mB[:, FT + NEXP:FT + 2 * NEXP], Lrt[:, 128:256], env["ident_f"][0:NEXP, 0:NEXP])
            ltr_tok = mm.inc(ii)
            kb.W(v, ltr_tok)
            v.tensor_copy(Ltm[:, 0], mB[:, FT:FT + NEXP])
            ltm_tok = vv.inc(v.tensor_copy(Ltm[:, 1], mB[:, FT + NEXP:FT + 2 * NEXP]))
            ltm_w = ltm_tok

            P = lambda sl: smx[:, sl, 0]
            P1 = lambda sl: smx[:, sl, 1]
            P2 = lambda sl: smx[:, sl, 2]
            MK = lambda sl: smk[:, sl]
            m1n = lambda sl: smr[:, sl, 0:1]
            m1p = lambda sl: smr[:, sl, 1:2]
            m2r = lambda sl: smr[:, sl, 2:3]
            m3r = lambda sl: smr[:, sl, 3:4]
            rr = lambda sl: smr[:, sl, 4:5]

            kb.W(v, ltm_tok, toks["init"])
            for sl in (0, 1):
                ii = v.reduce_max(m1n(sl), Ltm[:, sl], axis=AX.X, negate=True)
            c_tok = vv.inc(ii)
            kb.W(s, c_tok)
            for sl in (0, 1):
                ii = s.activation(P(sl), Ltm[:, sl], AF.Exp, bias=m1n(sl))
            e_tok = ev.inc(ii)
            kb.W(v, e_tok)
            for sl in (0, 1):
                v.reduce_sum(rr(sl), P(sl), axis=AX.X)
            for sl in (0, 1):
                ii = v.reduce_max(m1p(sl), P(sl), axis=AX.X)
            c_tok = vv.inc(ii)
            kb.W(v, c_tok)
            for sl in (0, 1):
                ii = v.reciprocal(rr(sl), rr(sl))
            c_tok = vv.inc(ii)
            kb.W(v, c_tok)
            for sl in (0, 1):
                ii = v.tensor_scalar(MK(sl), P(sl), m1p(sl), None, ALU.is_lt)
            c_tok = vv.inc(ii)
            kb.W(v, c_tok)
            for sl in (0, 1):
                ii = v.select(P1(sl), MK(sl), P(sl), env["negone"][:, 0:NEXP])
            c_tok = vv.inc(ii)
            kb.W(v, c_tok)
            for sl in (0, 1):
                ii = v.reduce_max(m2r(sl), P1(sl), axis=AX.X)
            c_tok = vv.inc(ii)
            kb.W(v, c_tok)
            for sl in (0, 1):
                ii = v.tensor_scalar(MK(sl), P1(sl), m2r(sl), None, ALU.is_lt)
            c_tok = vv.inc(ii)
            kb.W(v, c_tok)
            for sl in (0, 1):
                ii = v.select(P2(sl), MK(sl), P1(sl), env["negone"][:, 0:NEXP])
            c_tok = vv.inc(ii)
            kb.W(v, c_tok)
            for sl in (0, 1):
                ii = v.reduce_max(m3r(sl), P2(sl), axis=AX.X)
            c_tok = vv.inc(ii)
            kb.W(v, c_tok)
            for sl in (0, 1):
                ii = v.tensor_scalar(MK(sl), P(sl), m3r(sl), None, ALU.is_ge)
            c_tok = vv.inc(ii)
            kb.W(v, c_tok)
            for sl in (0, 1):
                ii = v.tensor_mul(P1(sl), P(sl), MK(sl))
            c_tok = vv.inc(ii)
            kb.W(v, c_tok, cwtm_w)
            for sl in (0, 1):
                ii = v.tensor_scalar(cw_tm[:, sl], P1(sl), rr(sl), None, ALU.mult)
            cw_tok = vv.inc(ii)

            kb.W(t, cw_tok)
            t.transpose(mB[0:NEXP, 0:128], cw_tm[:, 0], env["ident_f"][:])
            ii = t.transpose(mB[0:NEXP, 128:256], cw_tm[:, 1], env["ident_f"][:])
            cwt_tok = mm.inc(ii)
            cwtm_w = cwt_tok
            kb.W(v, cwt_tok, cw_rd)
            v.tensor_copy(cw_fm[:, 0:128], mB[0:NEXP, 0:128])
            cwf_tok = vv.inc(v.tensor_copy(cw_fm[:, 128:256], mB[0:NEXP, 128:256]))

            cwb_ev = [None] * NEXP
            for e in range(NEXP):
                dst_ps = mA if e % 2 == 0 else mB
                if e == 0:
                    kb.W(t, cwf_tok, bce_tok)
                elif e == 1:
                    kb.W(t, cwf_tok)
                else:
                    kb.W(t, cwb_ev[e - 2])
                ii = t.matmul(dst_ps[:, 0:FT], env["sel"][:, e * 128:(e + 1) * 128],
                              cw_fm[:], start=True, stop=True)
                cb_tok = mm.inc(ii)
                kb.W(s, cb_tok, cwb_rd[e])
                cwb_ev[e] = ev.inc(s.copy(cwb[:, e], dst_ps[:, 0:FT]))
            mA_free = cwb_ev[14]
            mB_free = cwb_ev[13]

            relu_ev = {}
            h2_vv = {}
            w2_mm = {}
            order = [NEXP] + list(range(NEXP))
            seq = [(e, j) for e in order for j in range(NK)]

            def w1_group(idx):
                e, j = seq[idx]
                ps_slot = idx % 2
                h_slot = idx % 4
                pe = e % 2
                if e < NEXP:
                    kb.W(t, wld[e], hps_rd[ps_slot])
                else:
                    kb.W(t, sw_tok, hps_rd[ps_slot])
                for dk in range(NK):
                    lt = w1t(pe, dk, j) if e < NEXP else s1t(dk, j)
                    ii = t.matmul(hsl(ps_slot), lt, u[:, dk], start=(dk == 0), stop=(dk == NK - 1))
                tok = mm.inc(ii)
                kb.W(s, tok, h_rd[h_slot])
                b = bias["eb1"](e, j) if e < NEXP else bias["sb1"](j)
                ii = s.activation(h_bf[:, h_slot], hsl(ps_slot), AF.Relu, bias=b)
                relu_ev[idx] = ev.inc(ii)
                hps_rd[ps_slot] = relu_ev[idx]
                if e < NEXP:
                    kb.W(v, relu_ev[idx], h2_rd[h_slot])
                    ii = v.tensor_mul(h2_bf[:, h_slot], h_bf[:, h_slot], cwb[:, e])
                    h2_vv[idx] = vv.inc(ii)
                    h_rd[h_slot] = h2_vv[idx]
                    cwb_rd[e] = h2_vv[idx]

            def w2_group(idx, first):
                e, j = seq[idx]
                h_slot = idx % 4
                pe = e % 2
                if e < NEXP:
                    kb.W(t, h2_vv[idx])
                    rhs = h2_bf[:, h_slot]
                else:
                    kb.W(t, relu_ev[idx])
                    rhs = h_bf[:, h_slot]
                if first:
                    kb.W(t, acc_free)
                for ik in range(NK):
                    lt = w2t(pe, j, ik) if e < NEXP else s2t(j, ik)
                    ii = t.matmul(accs(ik), lt, rhs, start=(first and ik % 2 == 0), stop=False)
                tok = mm.inc(ii)
                w2_mm[idx] = tok
                if e < NEXP:
                    h2_rd[h_slot] = tok
                else:
                    h_rd[h_slot] = tok

            for idx in range(len(seq)):
                w1_group(idx)
                if idx >= 2:
                    w2_group(idx - 2, first=(idx == 2))
                e, j = seq[idx]
                if j == NK - 1 and e < NEXP and e + 2 < NEXP:
                    wtok[(e + 2) % 2] = mm.tok()
                    wld[e + 2] = load_exp(e + 2, ti)
                if idx == len(seq) - 4 and ti + 1 < NFT:
                    x_next2 = load_x(ti + 1)
            w2_group(len(seq) - 2, first=False)
            w2_group(len(seq) - 1, first=False)
            wtok[0] = w2_mm[len(seq) - 1]
            wtok[1] = w2_mm[(1 + 13) * NK + NK - 1]

            kb.W(t, toks["rw"], cwf_tok)
            for ik in range(NK):
                ii = t.matmul(accs(ik), env["eb2_bf"][:, ik * 128:(ik + 1) * 128],
                              cw_fm[:], start=False, stop=(ik % 2 == 1))
            acc_tok = mm.inc(ii)
            cw_rd = acc_tok
            u_rd = acc_tok

            kb.W(v, acc_tok, t1_rd)
            for ik in range(NK):
                ii = v.tensor_scalar(t1[:, ik], accs(ik), bias["sb2"](ik), 0.5, ALU.add, ALU.mult)
            fe_tok = vv.inc(ii)
            acc_free = fe_tok
            kb.W(v, fe_tok)
            for ik in range(NK):
                ii = v.tensor_add(xt[:, p, ik], xt[:, p, ik], t1[:, ik])
            add_tok = vv.inc(ii)
            t1_rd = add_tok
            kb.W(q, add_tok)
            kb.dst.n += 16
            xtok[p] = kb.dsp[p].inc(q.dma_start(out=dst3[:, :, ti * FT:(ti + 1) * FT], in_=xt[:, p]), 16)
            if ti + 1 < NFT:
                x_next = x_next2

        kb.W(q, kb.dsp[0].tok(), kb.dsp[1].tok())


def emit_attn(nc, kb, ap, wbf, src, dst, env, toks):
    t, v, s, g, q = kb.t, kb.v, kb.s, kb.g, kb.q
    mm, ev, vv, pp = kb.mm, kb.ev, kb.vv, kb.pp
    bias = env["bias"]
    src3 = src.rearrange("(k p) n -> p k n", p=128)
    dst3 = dst.rearrange("(k p) n -> p k n", p=128)
    A = 512
    ST = 256
    NG = T // A

    entry = [kb.dsp[0].tok(), kb.dsp[1].tok()]
    with ExitStack() as ctx:
        NXP = 2
        xt = ctx.enter_context(nc.sbuf_tensor(None, [128, NXP, NK, ST], F32))
        xbf = ctx.enter_context(nc.sbuf_tensor(None, [128, 2, NK, ST], BF16))
        x2b = ctx.enter_context(nc.sbuf_tensor(None, [128, NK, ST], BF16))
        ug = ctx.enter_context(nc.sbuf_tensor(None, [128, NK, A], BF16))
        t1 = ctx.enter_context(nc.sbuf_tensor(None, [128, NK, ST], BF16))
        bcg = ctx.enter_context(nc.sbuf_tensor(None, [128, 2 * ST], F32))
        srow = ctx.enter_context(nc.sbuf_tensor(None, [1, 3 * ST], F32))
        srbf = ctx.enter_context(nc.sbuf_tensor(None, [1, 2 * ST], BF16))
        Qp = ctx.enter_context(nc.sbuf_tensor(None, [128, NK, A], BF16))
        Kp = ctx.enter_context(nc.sbuf_tensor(None, [128, NK, A], BF16))
        Vb = ctx.enter_context(nc.sbuf_tensor(None, [128, NK, A], BF16))
        phia = ctx.enter_context(nc.sbuf_tensor(None, [128, 2, A], BF16))
        phir = ctx.enter_context(nc.sbuf_tensor(None, [128, 2, A], BF16))
        Ktm = ctx.enter_context(nc.sbuf_tensor(None, [128, 4, D], BF16))
        Vtm = ctx.enter_context(nc.sbuf_tensor(None, [128, 4, D], BF16))
        Ab = ctx.enter_context(nc.sbuf_tensor(None, [128, 2, 128], BF16))
        Ssb = ctx.enter_context(nc.sbuf_tensor(None, [128, NK, D], F32))
        Sbf = ctx.enter_context(nc.sbuf_tensor(None, [128, NK, D], BF16))
        Zc = ctx.enter_context(nc.sbuf_tensor(None, [128, NK], F32))
        Zs = ctx.enter_context(nc.sbuf_tensor(None, [128, NK], F32))
        Zbf = ctx.enter_context(nc.sbuf_tensor(None, [128, NK], BF16))
        dens = ctx.enter_context(nc.sbuf_tensor(None, [1, 2, ST], F32))
        rden = ctx.enter_context(nc.sbuf_tensor(None, [1, 2, ST], F32))
        rdbf = ctx.enter_context(nc.sbuf_tensor(None, [1, 2, ST], BF16))
        rdb = ctx.enter_context(nc.sbuf_tensor(None, [128, 2, ST], F32))
        wst = ctx.enter_context(nc.sbuf_tensor(None, [128, 3, NK, D], BF16))
        pA = ctx.enter_context(nc.psum_tensor(None, [128, 512], F32))
        pB = ctx.enter_context(nc.psum_tensor(None, [128, 512], F32))
        nump = ctx.enter_context(nc.psum_tensor(None, [128, NK, ST], F32))

        for eng in (t, v, s, g, q):
            kb.W(eng, *entry)
        kb.W(s, toks["init"], toks["bias"])

        smem = vv.inc(v.memset(Ssb[:], 0.0))
        zmem = vv.inc(v.memset(Zs[:], 0.0))

        wmat = lambda pe, ck, ok: wst[:, pe, ck, ok * 128:(ok + 1) * 128]
        wseq = ("wqT", "wkT", "wvT", "woT")
        wtok = [None, None, None]
        wpre = {}

        def load_w(gi, wi):
            pe = (gi * 4 + wi) % 3
            nm = wseq[wi]
            if gi == 0:
                kb.W(q, toks["cast"]["woT"])
            kb.W(q, wtok[pe])
            return pe, kb.dwp[wi % 2].inc(q.dma_start(
                out=wst[:, pe], in_=wbf[nm].rearrange("(c p) n -> p c n", p=128)), 16)

        xbtok = [None, None]
        x2tok = None
        srow_w = None
        srbf_w = None
        bc_rd = None
        u_rd = None
        t1_rd = None
        pa_rd = None
        pb_rd = None
        num_rd = None
        qkv_rd = None
        Ab_rd = [None, None]
        O_rd = None
        tm_rd = None
        Sbf_w = None
        zbf_rd = None
        phi_rds = [None, None]
        rdb_rd = [None, None]
        ptr_kv_rd = None

        xtok = [None] * NXP
        def load_x(sti):
            p = sti % NXP
            pb2 = sti % 2
            kb.W(q, xtok[p])
            a_tok = kb.dxp[pb2].inc(q.dma_start(out=xt[:, p], in_=src3[:, :, sti * ST:(sti + 1) * ST]), 16)
            kb.W(g, xbtok[pb2])
            b_tok = kb.dbp[pb2].inc(g.dma_start(out=xgb_in(pb2), in_=src3[:, :, sti * ST:(sti + 1) * ST]), 16)
            return a_tok, b_tok

        def xgb_in(p):
            return xbf[:, p]

        x_next = load_x(0)
        x_next2 = None
        for gi in range(NG):
            wld = {0: wpre.pop((gi, 0), None) or load_w(gi, 0),
                   1: wpre.pop((gi, 1), None) or load_w(gi, 1)}

            u_toks = []
            for sti2 in range(2):
                sti = gi * 2 + sti2
                p = sti % NXP
                pb2 = sti % 2
                if sti2 == 0:
                    x_tok, xb_tok = x_next
                else:
                    x_tok, xb_tok = x_next2
                if sti2 == 0:
                    x_next2 = load_x(sti + 1)
                kb.W(s, x_tok, x2tok)
                for k in range(NK):
                    ii = s.activation(x2b[:, k], xt[:, p, k], AF.Square)
                sq_tok = ev.inc(ii)
                kb.W(t, xb_tok, sq_tok, pa_rd, pb_rd)
                for k in range(NK):
                    t.matmul(pA[0:1, 0:ST], env["ones_col"][:], xbf[:, pb2, k],
                             start=(k == 0), stop=(k == NK - 1))
                for k in range(NK):
                    ii = t.matmul(pA[0:1, ST:2 * ST], env["ones_col"][:], x2b[:, k],
                                  start=(k == 0), stop=(k == NK - 1))
                stats_tok = mm.inc(ii)
                xbtok[pb2] = stats_tok
                x2tok = stats_tok

                m_row = srow[0:1, 0:ST]
                msq_row = srow[0:1, ST:2 * ST]
                var_row = srow[0:1, 2 * ST:3 * ST]
                rstd_bf = srbf[0:1, 0:ST]
                mrstd_bf = srbf[0:1, ST:2 * ST]
                kb.W(v, stats_tok, srow_w)
                c_tok = vv.inc(v.tensor_scalar(m_row, pA[0:1, 0:ST], 1.0 / 1024, None, ALU.mult))
                kb.W(v, c_tok)
                c_tok = vv.inc(v.tensor_mul(msq_row, m_row, m_row))
                kb.W(v, c_tok)
                c_tok = vv.inc(v.scalar_tensor_tensor(var_row, pA[0:1, ST:2 * ST], 1.0 / 1024,
                                                      msq_row, ALU.mult, ALU.subtract))
                kb.W(s, c_tok)
                e_tok = ev.inc(s.activation(var_row, var_row, AF.Ln, bias=1e-5))
                kb.W(s, e_tok, srbf_w)
                e_tok = ev.inc(s.activation(rstd_bf, var_row, AF.Exp, scale=-0.5))
                kb.W(v, e_tok)
                c_tok = vv.inc(v.tensor_mul(mrstd_bf, m_row, rstd_bf))
                srow_w = c_tok
                kb.W(t, c_tok)
                t.matmul(pB[:, 0:ST], env["ones_row"][:], rstd_bf, start=True, stop=True)
                ii = t.matmul(pB[:, ST:2 * ST], env["ones_row"][:], mrstd_bf, start=True, stop=True)
                bc_tok = mm.inc(ii)
                srbf_w = bc_tok
                pa_rd = c_tok
                kb.W(s, bc_tok, bc_rd)
                s.copy(bcg[:, 0:ST], pB[:, 0:ST])
                bce_tok = ev.inc(s.copy(bcg[:, ST:2 * ST], pB[:, ST:2 * ST]))
                pb_rd = bce_tok
                kb.W(v, bce_tok, u_rd, t1_rd)
                for k in range(NK):
                    ii = v.tensor_mul(t1[:, k], xt[:, p, k], bcg[:, 0:ST])
                mul_tok = vv.inc(ii)
                kb.W(v, mul_tok)
                for k in range(NK):
                    ii = v.tensor_sub(ug[:, k, sti2 * ST:(sti2 + 1) * ST], t1[:, k],
                                      bcg[:, ST:2 * ST])
                u_tok = vv.inc(ii)
                bc_rd = u_tok
                t1_rd = u_tok
                u_toks.append(u_tok)

            rot = [pA[:, 0:A], pB[:, 0:A], nump[:, 0:2, :], nump[:, 2:4, :]]
            rfree = [pa_rd, pb_rd, num_rd, num_rd]
            np_i = [0]
            NROT = 4

            def proj(wi, dst_sb, bias_fn, nbias_fn, is_v):
                nonlocal qkv_rd
                pe, wl = wld[wi]
                for ok in range(NK):
                    sl = np_i[0] % NROT
                    ps = rot[sl]
                    pp2 = np_i[0] % 2
                    kb.W(t, u_toks[0], u_toks[1], wl, rfree[sl])
                    for dk in range(NK):
                        ii = t.matmul(ps, wmat(pe, dk, ok), ug[:, dk],
                                      start=(dk == 0), stop=(dk == NK - 1))
                    ptok = mm.inc(ii)
                    if is_v:
                        kb.W(v, ptok, qkv_rd)
                        eetok = vv.inc(v.tensor_scalar(dst_sb[:, ok], ps,
                                                       bias_fn(ok), None, ALU.add))
                    else:
                        kb.W(s, ptok, phi_rds[pp2])
                        a1 = ev.inc(s.activation(phia[:, pp2], ps, AF.Relu,
                                                 scale=-1.0, bias=nbias_fn(ok)))
                        kb.W(s, a1)
                        ev.inc(s.activation(phia[:, pp2], phia[:, pp2], AF.Exp, scale=-1.0))
                        rtok = ev.inc(s.activation(phir[:, pp2], ps, AF.Relu, bias=bias_fn(ok)))
                        kb.W(v, rtok, qkv_rd)
                        eetok = vv.inc(v.tensor_add(dst_sb[:, ok], phia[:, pp2], phir[:, pp2]))
                        phi_rds[pp2] = eetok
                    rfree[sl] = eetok
                    np_i[0] += 1
                return eetok

            qdone = proj(0, Qp, bias["bq"], bias["nbq"], False)
            wtok[wld[0][0]] = mm.tok()
            wld[2] = load_w(gi, 2)
            kdone = proj(1, Kp, bias["bk"], bias["nbk"], False)
            wtok[wld[1][0]] = mm.tok()
            wld[3] = load_w(gi, 3)
            vdone = proj(2, Vb, bias["bv"], None, True)
            wtok[wld[2][0]] = mm.tok()
            if gi + 1 < NG:
                wpre[(gi + 1, 0)] = load_w(gi + 1, 0)
                wpre[(gi + 1, 1)] = load_w(gi + 1, 1)
            pa_rd, pb_rd = rfree[0], rfree[1]
            num_rd = rfree[3] if rfree[3] is not None else num_rd
            kb.W(t, rfree[2])

            with nc.psum_tensor("ptr_%d" % gi, [128, D], BF16, allow_name_mangling=True) as ptr:
                kb.W(t, kdone, vdone, ptr_kv_rd, tm_rd)
                ttoks = []
                for ci in range(4):
                    csl = slice(ci * CH, (ci + 1) * CH)
                    for srcb, dstb in ((Kp, Ktm), (Vb, Vtm)):
                        if ttoks:
                            kb.W(t, ttoks[-1])
                        for dk in range(NK):
                            ii = t.transpose(ptr[:, dk * 128:(dk + 1) * 128],
                                             srcb[:, dk, csl], env["ident_bf"][:])
                        tr_tok = mm.inc(ii)
                        kb.W(v, tr_tok)
                        ttoks.append(vv.inc(v.tensor_copy(dstb[:, ci], ptr[:])))
                tm_done = ttoks[-1]

            numsl = lambda ek, lo, w: nump[:, ek, lo:lo + w]
            den_ps = lambda sti2: pB[0:1, sti2 * ST:(sti2 + 1) * ST]
            abctx = ExitStack()
            _ab0 = abctx.enter_context(nc.psum_tensor("abps0_%d" % gi, [128, 512], F32, allow_name_mangling=True))
            _ab1 = abctx.enter_context(nc.psum_tensor("abps1_%d" % gi, [128, 512], F32, allow_name_mangling=True))
            abps = [_ab0, _ab1]
            den_defer = []
            o_toks = [None, None]
            for sti2 in range(2):
                sti = gi * 2 + sti2
                ssl = slice(sti2 * 2 * CH, (sti2 + 1) * 2 * CH)

                kb.W(t, num_rd)
                if gi > 0:
                    kb.W(t, Sbf_w)
                    for dk in range(NK):
                        for ek in range(NK):
                            ii = t.matmul(numsl(ek, 0, ST),
                                          Sbf[:, dk, ek * 128:(ek + 1) * 128],
                                          Qp[:, dk, ssl],
                                          start=(dk == 0 and ek % 2 == 0), stop=False)

                kb.W(t, tm_done, qdone)
                abm_toks = {}
                den_started = [False]
                first_den_wait = [True]
                for li, c in enumerate((sti2 * 2, sti2 * 2 + 1)):
                    for m in range(c + 1):
                        slot = len(abm_toks) % 2
                        abm_toks[len(abm_toks)] = 1
                        csl_c = slice(c * CH, (c + 1) * CH)
                        csl_m = slice(m * CH, (m + 1) * CH)
                        kb.W(t, Ab_rd[slot])
                        for dk in range(NK):
                            ii = t.matmul(abps[slot][:, 0:128], Kp[:, dk, csl_m], Qp[:, dk, csl_c],
                                          start=(dk == 0), stop=(dk == NK - 1))
                        ablk = mm.inc(ii)
                        kb.W(v, ablk, toks["init"])
                        if m == c:
                            abe = vv.inc(v.tensor_mul(Ab[:, slot], abps[slot][:, 0:128], env["triu"][:]))
                        else:
                            abe = vv.inc(v.tensor_copy(Ab[:, slot], abps[slot][:, 0:128]))
                        lo = (c % 2) * 128
                        if first_den_wait[0]:
                            kb.W(t, abe, pb_rd, zbf_rd)
                            first_den_wait[0] = False
                        else:
                            kb.W(t, abe)
                        dst_flag = not den_started[0]
                        den_started[0] = True
                        t.matmul(den_ps(sti2)[:, lo:lo + 128], env["ones_col"][:], Ab[:, slot],
                                 start=dst_flag, stop=(gi == 0 and li == 1 and m == c))
                        av_start = (gi == 0) and (m == 0)
                        av_stop = (m == c) and (gi == 0 or li == 1)
                        for ek in range(NK):
                            ii = t.matmul(numsl(ek, lo, 128), Vtm[:, m, ek * 128:(ek + 1) * 128],
                                          Ab[:, slot], start=(av_start and ek % 2 == 0),
                                          stop=(av_stop and ek % 2 == 1))
                        av_tok = mm.inc(ii)
                        Ab_rd[slot] = av_tok
                if gi > 0:
                    kb.W(t, zbf_rd)
                    for dk in range(NK):
                        ii = t.matmul(den_ps(sti2)[:], Zbf[:, dk:dk + 1], Qp[:, dk, ssl],
                                      start=False, stop=(dk == NK - 1))
                    den_tok = mm.inc(ii)
                else:
                    den_tok = mm.tok()
                num_done = mm.tok()

                kb.W(s, num_done, O_rd)
                ev_ii = None
                for ek in range(NK):
                    ev_ii = s.copy(ug[:, ek, ssl], nump[:, ek, :])
                evac_tok = ev.inc(ev_ii)
                num_rd = evac_tok
                den_defer.append((sti2, ssl, den_tok, evac_tok))

            abctx.close()
            scan_mm = mm.tok()
            with nc.psum_tensor("kvq_%d" % gi, [128, 2, 512], F32, allow_name_mangling=True) as kvq:
                kvfree = [None, None]
                inter_last = num_done if gi > 0 else None
                for dk in range(NK):
                    for hf in range(2):
                        slot = (dk * 2 + hf) % 2
                        kb.W(t, tm_done, kvfree[slot])
                        for ci in range(4):
                            ii = t.matmul(kvq[:, slot],
                                          Ktm[:, ci, dk * 128:(dk + 1) * 128],
                                          Vtm[:, ci, hf * 512:(hf + 1) * 512],
                                          start=(ci == 0), stop=(ci == 3))
                        kv_tok = mm.inc(ii)
                        kb.W(v, kv_tok, smem)
                        ii = v.tensor_add(Ssb[:, dk, hf * 512:(hf + 1) * 512],
                                          Ssb[:, dk, hf * 512:(hf + 1) * 512],
                                          kvq[:, slot])
                        sadd = vv.inc(ii)
                        kvfree[slot] = sadd
                        kb.W(s, sadd)
                        ii = s.copy(Sbf[:, dk, hf * 512:(hf + 1) * 512],
                                    Ssb[:, dk, hf * 512:(hf + 1) * 512])
                        ev.inc(ii)
                Sbf_w = kb.ev.tok()
                ptr_kv_rd = kvfree[0]
                ptr_kv_rd = kb.vv.tok()
            tm_rd = mm.tok()

            for sti2, ssl, den_tok, evac_tok in den_defer:
                kb.W(s, den_tok)
                de_tok = ev.inc(s.activation(dens[0:1, sti2], den_ps(sti2)[:],
                                             AF.Copy, bias=1e-6))
                pb_rd = de_tok
                kb.W(v, de_tok)
                r_tok = vv.inc(v.reciprocal(rden[0:1, sti2], dens[0:1, sti2]))
                kb.W(v, r_tok)
                r_tok = vv.inc(v.tensor_copy(rdbf[0:1, sti2], rden[0:1, sti2]))
                kb.W(t, r_tok, pa_rd)
                ii = t.matmul(pA[:, sti2 * ST:(sti2 + 1) * ST], env["ones_row"][:],
                              rdbf[0:1, sti2], start=True, stop=True)
                rb_tok = mm.inc(ii)
                kb.W(s, rb_tok, rdb_rd[sti2])
                rdb_tok = ev.inc(s.copy(rdb[:, sti2], pA[:, sti2 * ST:(sti2 + 1) * ST]))
                pa_rd = rdb_tok
                kb.W(v, rdb_tok, evac_tok)
                for ek in range(NK):
                    ii = v.tensor_mul(ug[:, ek, ssl], ug[:, ek, ssl], rdb[:, sti2])
                o_toks[sti2] = vv.inc(ii)
                rdb_rd[sti2] = o_toks[sti2]

            kb.W(v, kdone, zmem)
            for dk in range(NK):
                ii = v.reduce_sum(Zc[:, dk:dk + 1], Kp[:, dk, :], axis=AX.X)
            zr_tok = vv.inc(ii)
            kb.W(v, zr_tok)
            za_tok = vv.inc(v.tensor_add(Zs[:], Zs[:], Zc[:]))
            zmem = za_tok
            kb.W(s, za_tok, zbf_rd, scan_mm)
            zb_tok = ev.inc(s.copy(Zbf[:], Zs[:]))
            zbf_rd = zb_tok

            pe3, wl3 = wld[3]
            for sti2 in range(2):
                sti = gi * 2 + sti2
                p = sti % NXP
                ssl = slice(sti2 * 2 * CH, (sti2 + 1) * 2 * CH)
                wo_ev = [None] * NK
                for jk in range(NK):
                    sl = jk % 2
                    ps = rot[sl]
                    kb.W(t, wl3, o_toks[sti2], pa_rd if sl == 0 else pb_rd)
                    if jk >= 2:
                        kb.W(t, wo_ev[jk - 2])
                    for ek in range(NK):
                        ii = t.matmul(ps[:, 0:ST], wmat(pe3, ek, jk), ug[:, ek, ssl],
                                      start=(ek == 0), stop=(ek == NK - 1))
                    wotok = mm.inc(ii)
                    kb.W(v, wotok)
                    ii = v.scalar_tensor_tensor(xt[:, p, jk], ps[:, 0:ST], bias["bo"](jk),
                                                xt[:, p, jk], ALU.add, ALU.add)
                    wo_ev[jk] = vv.inc(ii)
                pa_rd = wo_ev[NK - 2]
                pb_rd = wo_ev[NK - 1]
                add_tok = wo_ev[NK - 1]
                kb.W(q, add_tok)
                kb.dst.n += 16
                xtok[p] = kb.dsp[sti % 2].inc(
                    q.dma_start(out=dst3[:, :, sti * ST:(sti + 1) * ST], in_=xt[:, p]), 16)
                if sti2 == 0 and gi + 1 < NG:
                    x_next = load_x((gi + 1) * 2)
            womm = mm.tok()
            O_rd = womm
            u_rd = womm
            qkv_rd = womm
            wtok[pe3] = womm

        kb.W(q, kb.dsp[0].tok(), kb.dsp[1].tok())


def _stage(inputs):
    c = lambda a: np.ascontiguousarray(np.asarray(a, dtype=np.float32))
    base = {
        "wqT": c(np.asarray(inputs["L0_attn_Wq"]).T), "wkT": c(np.asarray(inputs["L0_attn_Wk"]).T),
        "wvT": c(np.asarray(inputs["L0_attn_Wv"]).T), "woT": c(np.asarray(inputs["L0_attn_Wo"]).T),
        "sW1": c(inputs["L0_ffn_sW1"]), "sW2": c(inputs["L0_ffn_sW2"]),
        "eW1": c(np.asarray(inputs["L0_ffn_eW1"]).reshape(NEXP, NK, 128, D)
                 .transpose(0, 2, 1, 3)),
        "eW2": c(np.asarray(inputs["L0_ffn_eW2"]).reshape(NEXP, NK, 128, D)
                 .transpose(0, 2, 1, 3)),
        "rW": c(np.asarray(inputs["L0_ffn_router_W"]).reshape(NK, 128, NEXP)
                .transpose(1, 0, 2).reshape(128, NK * NEXP)),
        "rb": c(np.asarray(inputs["L0_ffn_router_b"]).reshape(NEXP, 1)),
        "eb1": c(np.asarray(inputs["L0_ffn_eb1"]).reshape(NEXP, NK, 128)
                 .transpose(2, 0, 1).reshape(128, NEXP * NK)),
        "eb1sp": c(np.asarray(inputs["L0_ffn_eb1"]).reshape(NEXP, NK, 128)
                   .transpose(0, 2, 1).reshape(NEXP * 128, NK)),
        "eb2": c(inputs["L0_ffn_eb2"]),
    }
    for nm, key in (("bq", "L0_attn_bq"), ("bk", "L0_attn_bk"), ("bv", "L0_attn_bv"),
                    ("bo", "L0_attn_bo"), ("sb1", "L0_ffn_sb1"), ("sb2", "L0_ffn_sb2")):
        base[nm] = c(np.asarray(inputs[key]).reshape(NK, 128).T)
    xs = c(np.asarray(inputs["L0_attn_x"]).transpose(0, 2, 1))
    return base, xs


def run_device(inputs, phases=("f1", "at", "f2")):
    base, xs = _stage(inputs)
    nc = build_program(phases)
    in_maps = [dict(base, X=xs[i]) for i in range(8)]
    res = run_bass_kernel_spmd(nc, in_maps, core_ids=list(range(8)))
    Y = np.stack([np.asarray(r["Y"]) for r in res.results])
    return np.ascontiguousarray(Y.transpose(0, 2, 1))


def kernel(**inputs):
    return run_device(inputs)


# version 15 (final)
# speedup vs baseline: 1.0835x; 1.0011x over previous
from contextlib import ExitStack
import numpy as np
import ml_dtypes
import concourse.bass as bass
import concourse.mybir as mybir
from concourse.bass_utils import run_bass_kernel_spmd
from concourse.bass_core_rust import IndirectOffsetOnAxis

F32 = mybir.dt.float32
BF16 = mybir.dt.bfloat16
AF = mybir.ActivationFunctionType
ALU = mybir.AluOpType
AX = mybir.AxisListType

D = 1024
T = 4096
NK = 8
NEXP = 15
FT = 256
NFT = T // FT
CH = 128
NCH = T // CH
AG = 512
NAG = T // AG
BF = ml_dtypes.bfloat16
SPARSE = True
STAGE = "full"
DEBUG_DUMP = False


class Ctr:
    def __init__(self, nc, name):
        self.sem = nc.alloc_semaphore(name=name)
        self.n = 0

    def inc(self, inst, k=1):
        inst.then_inc(self.sem, k)
        self.n += k
        return (self, self.n)

    def tok(self):
        return (self, self.n)


class KB:
    def __init__(self, nc):
        self.nc = nc
        self.t = nc.tensor
        self.v = nc.vector
        self.s = nc.scalar
        self.g = nc.gpsimd
        self.q = nc.sync
        self.seen = {}
        self.mm = Ctr(nc, "mm")
        self.ev = Ctr(nc, "ev")
        self.vv = Ctr(nc, "vv")
        self.pp = Ctr(nc, "pp")
        self.dq = Ctr(nc, "dq")
        self.dg = Ctr(nc, "dg")
        self.dxp = [Ctr(nc, "dx0"), Ctr(nc, "dx1")]
        self.dbp = [Ctr(nc, "db0"), Ctr(nc, "db1")]
        self.dwp = [Ctr(nc, "dw0"), Ctr(nc, "dw1")]
        self.dsp = [Ctr(nc, "ds0"), Ctr(nc, "ds1")]
        self.dst = Ctr(nc, "dst")
        self.ecast = [Ctr(nc, f"ec{i}") for i in range(NEXP)]
        self.acast = Ctr(nc, "acast")
        self.dU = [Ctr(nc, "dUa"), Ctr(nc, "dUb")]
        self.dug = [Ctr(nc, "dug0"), Ctr(nc, "dug1")]
        self.dsc = Ctr(nc, "dsc")
        self.dscp = [Ctr(nc, "dscp0"), Ctr(nc, "dscp1")]
        self.dmcp = [Ctr(nc, "dmcp0"), Ctr(nc, "dmcp1")]
        self.dwg = [Ctr(nc, "dwg0"), Ctr(nc, "dwg1")]
        self.dbg = [Ctr(nc, "dbg0"), Ctr(nc, "dbg1")]
        self.dmt = [Ctr(nc, "dmt0"), Ctr(nc, "dmt1")]
        self.dut = [Ctr(nc, "dut0"), Ctr(nc, "dut1")]
        self.dy = [Ctr(nc, "dya"), Ctr(nc, "dyb")]
        self.dR = [Ctr(nc, "dR0"), Ctr(nc, "dR1")]
        self.dsen = Ctr(nc, "dsen")

    def W(self, eng, *toks):
        for tk in toks:
            if tk is None:
                continue
            ctr, val = tk
            if val <= 0:
                continue
            key = (id(eng), id(ctr))
            if self.seen.get(key, -1) >= val:
                continue
            self.seen[key] = val
            eng.wait_ge(ctr.sem, val)


def _sel_const():
    sel = np.zeros((NEXP, NEXP * 128), dtype=BF)
    for e in range(NEXP):
        sel[e, e * 128:(e + 1) * 128] = 1
    return sel


def build_program(phases=("f1", "at", "f2")):
    nc = bass.Bass("TRN2", target_bir_lowering=False, debug=False)
    kb = KB(nc)
    t, v, s, g, q = kb.t, kb.v, kb.s, kb.g, kb.q

    ap = {}
    def din(name, shape, dtype=F32):
        ap[name] = nc.dram_tensor(name, list(shape), dtype, kind="ExternalInput").ap()
        return ap[name]

    X = din("X", (D, T))
    for nm in ("wqT", "wkT", "wvT", "woT", "sW1", "sW2"):
        din(nm, (D, D))
    din("eW1", (NEXP, 128, NK, D))
    din("eW2", (NEXP, 128, NK, D))
    din("eb1sp", (NEXP * 128, NK))
    din("rW", (128, NK * NEXP))
    din("rb", (NEXP, 1))
    for nm in ("bq", "bk", "bv", "bo", "sb1", "sb2"):
        din(nm, (128, NK))
    din("eb1", (128, NEXP * NK))
    din("eb2", (NEXP, D))
    Y = nc.dram_tensor("Y", [D, T], F32, kind="ExternalOutput").ap()

    X1 = nc.dram_tensor("X1", [D, T], F32, kind="Internal").ap()
    X2 = nc.dram_tensor("X2", [D, T], F32, kind="Internal").ap()
    wbf = {}
    for nm in ("wqT", "wkT", "wvT", "woT"):
        wbf[nm] = nc.dram_tensor(nm + "_bf", [D, D], BF16, kind="Internal").ap()
    eW12 = nc.dram_tensor("eW12_bf", [NEXP, 128, 2, NK, D], BF16, kind="Internal").ap()
    wbf["eW12"] = eW12
    wbf["eW1"] = eW12[:, :, 0]
    wbf["eW2"] = eW12[:, :, 1]
    wbf["eb2d"] = nc.dram_tensor("eb2_bfd", [NEXP, D], BF16, kind="Internal").ap()
    NSLOT = 63 * 256
    wbf["Ud"] = nc.dram_tensor("Ud", [T, D], BF16, kind="Internal").ap()
    wbf["metad"] = nc.dram_tensor("metad", [NSLOT, 16], F32, kind="Internal").ap()
    wbf["R3d"] = nc.dram_tensor("R3d", [3 * T + 2, D], BF16, kind="Internal").ap()
    wbf["eb2j"] = nc.dram_tensor("eb2j", [64, D], BF16, kind="Internal").ap()

    ident_bf_d = nc.inline_tensor(np.eye(128, dtype=BF), name="c_ident_bf").ap()
    ident_f_d = nc.inline_tensor(np.eye(128, dtype=np.float32), name="c_ident_f").ap()
    triu_d = nc.inline_tensor(np.triu(np.ones((128, 128), dtype=np.float32)).astype(BF),
                              name="c_triu").ap()
    sel_d = nc.inline_tensor(_sel_const(), name="c_sel").ap()
    ones_col_d = nc.inline_tensor(np.ones((128, 1), dtype=BF), name="c_ones_col").ap()
    ones_row_d = nc.inline_tensor(np.ones((1, 128), dtype=BF), name="c_ones_row").ap()
    iota128c_d = nc.inline_tensor(np.arange(128, dtype=np.float32).reshape(128, 1), name="c_iota128c").ap()
    iota15b_d = nc.inline_tensor(np.tile(np.arange(NEXP, dtype=np.float32), (128, 1)), name="c_iota15b").ap()
    iota64r_d = nc.inline_tensor(np.arange(64, dtype=np.float32).reshape(1, 64), name="c_iota64r").ap()
    ones15f_d = nc.inline_tensor(np.ones((NEXP, 1), dtype=np.float32), name="c_ones15f").ap()
    ident15f_d = nc.inline_tensor(np.eye(NEXP, dtype=np.float32), name="c_ident15f").ap()

    if DEBUG_DUMP:
        wbf["dbg"] = nc.dram_tensor("DBG", [128, 132], F32, kind="ExternalOutput").ap()
    pio = {"f1": [X, X1], "at": [X1, X2], "f2": [X2, Y]}
    pio[phases[-1]][1] = Y

    with (
        nc.sbuf_tensor("consts", [128, 8], F32) as consts,
        nc.sbuf_tensor("negone", [128, 16], F32) as negone,
        nc.sbuf_tensor("ones_col", [128, 1], BF16) as ones_col,
        nc.sbuf_tensor("ones_row", [1, 128], BF16) as ones_row,
        nc.sbuf_tensor("ident_bf", [128, 128], BF16) as ident_bf,
        nc.sbuf_tensor("ident_f", [128, 128], F32) as ident_f,
        nc.sbuf_tensor("triu", [128, 128], BF16) as triu,
        nc.sbuf_tensor("sel", [NEXP, NEXP * 128], BF16) as sel,
        nc.sbuf_tensor("rw_bf", [128, NK * NEXP], BF16) as rw_bf,
        nc.sbuf_tensor("rb_s", [NEXP, 1], F32) as rb_s,
        nc.sbuf_tensor("eb2_bf", [NEXP, D], BF16) as eb2_bf,
        nc.sbuf_tensor("bcols", [128, 8 * NK], F32) as bcols,
        nc.sbuf_tensor("sbh", [128, NK], F32) as sbh,
        nc.sbuf_tensor("eb1_s", [128, NEXP * NK], F32) as eb1_s,
        nc.sbuf_tensor("iota128c", [128, 1], F32) as iota128c,
        nc.sbuf_tensor("iota15b", [128, NEXP], F32) as iota15b,
        nc.sbuf_tensor("iota64r", [1, 64], F32) as iota64r,
        nc.sbuf_tensor("ones15f", [NEXP, 1], F32) as ones15f,
        nc.sbuf_tensor("ident15f", [NEXP, NEXP], F32) as ident15f,
    ):
        for val, idx in ((0.5, 0), (-0.5, 1), (1.0 / 1024, 2), (1e-5, 3), (1e-6, 4)):
            nc.const_aps.register(F32, val, consts[:, idx:idx + 1])

        col = lambda buf, i, k: buf[:, i * NK + k:i * NK + k + 1]
        bias = dict(
            bq=lambda k: col(bcols, 0, k), bk=lambda k: col(bcols, 1, k),
            bv=lambda k: col(bcols, 2, k), bo=lambda k: col(bcols, 3, k),
            sb1=lambda k: col(bcols, 4, k), sb2=lambda k: col(bcols, 5, k),
            nbq=lambda k: col(bcols, 6, k), nbk=lambda k: col(bcols, 7, k),
            eb1=lambda e, k: eb1_s[:, e * NK + k:e * NK + k + 1],
        )

        for val, idx in ((0.5, 0), (-0.5, 1), (1.0 / 1024, 2), (1e-5, 3), (1e-6, 4)):
            v.memset(consts[:, idx:idx + 1], float(val))
        init_vv = kb.vv.inc(v.memset(negone[:], -1.0))

        ld_tok = None
        for dst_t, src_ap in (
            (ones_col[:], ones_col_d), (ones_row[:], ones_row_d),
            (ident_bf[:], ident_bf_d), (ident_f[:], ident_f_d), (triu[:], triu_d),
            (sel[:], sel_d), (rb_s[:], ap["rb"]), (eb1_s[:], ap["eb1"]),
            (iota128c[:], iota128c_d), (iota15b[:], iota15b_d),
            (iota64r[:], iota64r_d), (ones15f[:], ones15f_d), (ident15f[:], ident15f_d),
        ):
            ld_tok = kb.dq.inc(q.dma_start(out=dst_t, in_=src_ap), 16)
        for i, nm in enumerate(("bq", "bk", "bv", "bo", "sb1", "sb2")):
            ld_tok = kb.dq.inc(q.dma_start(out=bcols[:, i * NK:(i + 1) * NK], in_=ap[nm]), 16)

        g.dma_start(out=rw_bf[:], in_=ap["rW"]).then_inc(kb.dbg[0].sem, 16)
        kb.dbg[0].n += 16
        rw_tok = kb.dbg[0].inc(g.dma_start(out=eb2_bf[:], in_=ap["eb2"]), 16)

        kb.W(v, ld_tok)
        v.tensor_scalar(bcols[:, 6 * NK:7 * NK], bcols[:, 0 * NK:1 * NK], -1.0, None, ALU.mult)
        v.tensor_scalar(bcols[:, 7 * NK:8 * NK], bcols[:, 1 * NK:2 * NK], -1.0, None, ALU.mult)
        bias_tok = kb.vv.inc(
            v.tensor_scalar(sbh[:], bcols[:, 5 * NK:6 * NK], 0.5, None, ALU.mult))

        cast_tok = {}
        cast_jobs = []
        eb2d_tok = kb.dbg[1].inc(g.dma_start(out=wbf["eb2d"], in_=ap["eb2"]), 16)

        def mk_ecast(e):
            def run():
                kb.ecast[e].inc(g.dma_start(out=wbf["eW1"][e], in_=ap["eW1"][e]), 16)
                cast_tok[e] = kb.ecast[e].inc(g.dma_start(out=wbf["eW2"][e], in_=ap["eW2"][e]), 16)
            return run

        def mk_acast(nm):
            def run():
                cast_tok[nm] = kb.acast.inc(g.dma_start(out=wbf[nm], in_=ap[nm]), 16)
            return run
        for e in range(NEXP):
            cast_jobs.append(mk_ecast(e))
        for nm in ("wqT", "wkT", "wvT", "woT"):
            cast_jobs.append(mk_acast(nm))

        toks = dict(bias=bias_tok, rw=rw_tok, init=init_vv, cast=cast_tok,
                    eb2d=eb2d_tok, cast_jobs=cast_jobs)
        env = dict(consts=consts, negone=negone, ones_col=ones_col, sbh=sbh,
                   ones_row=ones_row, ident_bf=ident_bf, ident_f=ident_f,
                   triu=triu, sel=sel, rw_bf=rw_bf, rb_s=rb_s, eb2_bf=eb2_bf,
                   bias=bias, iota128c=iota128c, iota15b=iota15b,
                   iota64r=iota64r, ones15f=ones15f, ident15f=ident15f)

        for ph in phases:
            src, dstp = pio[ph]
            if ph in ("f1", "f2"):
                (emit_ffn_sparse if SPARSE else emit_ffn)(nc, kb, ap, wbf, src, dstp, env, toks)
            else:
                while cast_jobs:
                    cast_jobs.pop(0)()
                emit_attn(nc, kb, ap, wbf, src, dstp, env, toks)

        for ctr in (kb.dq, kb.dg, kb.acast, *kb.ecast, *kb.dxp, *kb.dbp,
                    *kb.dwp, *kb.dsp, *kb.dU, *kb.dug, kb.dsc, *kb.dscp, *kb.dmcp, *kb.dwg,
                    *kb.dbg, *kb.dmt, *kb.dut, *kb.dy, *kb.dR, kb.dsen):
            kb.W(q, ctr.tok())
    return nc




def emit_ffn_sparse(nc, kb, ap, wbf, src, dst, env, toks):
    t, v, s, g, q = kb.t, kb.v, kb.s, kb.g, kb.q
    mm, ev, vv = kb.mm, kb.ev, kb.vv
    bias = env["bias"]
    src3 = src.rearrange("(k p) n -> p k n", p=128)
    dst3 = dst.rearrange("(k p) n -> p k n", p=128)
    NJ = 63
    CAP = 256
    NSLOT = NJ * CAP
    SENT = float(3 * T)
    NSLAB = T // 128
    w12f = wbf["eW12"].rearrange("e p t c n -> (e p) (t c n)")
    eb1f = ap["eb1sp"]
    entry = [kb.dsp[0].tok(), kb.dsp[1].tok(), kb.dy[0].tok(), kb.dy[1].tok(), kb.dR[0].tok(), kb.dR[1].tok()]
    octx = ExitStack()
    widx = octx.enter_context(nc.sbuf_tensor(None, [128, 64], mybir.dt.int32))
    with ExitStack() as ctx:
        en = ctx.enter_context
        xt = en(nc.sbuf_tensor(None, [128, 2, NK, FT], F32))
        xbf = en(nc.sbuf_tensor(None, [128, 2, NK, FT], BF16))
        x2b = en(nc.sbuf_tensor(None, [128, 2, NK, FT], BF16))
        u = en(nc.sbuf_tensor(None, [128, 2, NK, FT], BF16))
        t1 = en(nc.sbuf_tensor(None, [128, NK, FT], F32))
        bcast = en(nc.sbuf_tensor(None, [128, 2 * FT], F32))
        srow = en(nc.sbuf_tensor(None, [1, 3 * FT], F32))
        srbf = en(nc.sbuf_tensor(None, [1, 2 * FT], BF16))
        Lrt = en(nc.sbuf_tensor(None, [NEXP, FT], F32))
        Ltm = en(nc.sbuf_tensor(None, [128, 2, NEXP], F32))
        smx = en(nc.sbuf_tensor(None, [128, 2, 4, NEXP], F32))
        smk = en(nc.sbuf_tensor(None, [128, 2, NEXP], mybir.dt.int32))
        smr = en(nc.sbuf_tensor(None, [128, 2, 8], F32))
        keepm = en(nc.sbuf_tensor(None, [128, 2, NEXP], F32))
        wmtm = en(nc.sbuf_tensor(None, [128, 2, NEXP], F32))
        u_tm = en(nc.sbuf_tensor(None, [128, 2, D], BF16))
        sw1r = en(nc.sbuf_tensor(None, [128, NK, D], BF16))
        sw2r = en(nc.sbuf_tensor(None, [128, NK, D], BF16))
        wmfm = en(nc.sbuf_tensor(None, [NEXP, 3, T], BF16))
        ecols = en(nc.sbuf_tensor(None, [128, NSLAB, 4], F32))
        ccols = en(nc.sbuf_tensor(None, [128, NSLAB, 4], F32))
        msum = en(nc.sbuf_tensor(None, [15, T], F32))
        rankf = en(nc.sbuf_tensor(None, [15, T], F32))
        prodf = en(nc.sbuf_tensor(None, [15, T], F32))
        slotw = en(nc.sbuf_tensor(None, [1, T], F32))
        r15c = en(nc.sbuf_tensor(None, [15, 4], F32))
        rA = en(nc.sbuf_tensor(None, [1, 64], F32))
        rB = en(nc.sbuf_tensor(None, [1, 64], F32))
        rC = en(nc.sbuf_tensor(None, [1, 64], F32))
        rI = en(nc.sbuf_tensor(None, [1, 64], mybir.dt.int32))
        rBf = en(nc.sbuf_tensor(None, [1, 64], BF16))
        ecall = en(nc.sbuf_tensor(None, [128, 64], F32))
        ej32 = en(nc.sbuf_tensor(None, [64, 1], mybir.dt.int32))
        eb2all = en(nc.sbuf_tensor(None, [64, D], BF16))
        sentf = en(nc.sbuf_tensor(None, [128, 1024], F32))
        mcall = en(nc.sbuf_tensor(None, [128, 3, 32, 16], F32))
        slc32 = en(nc.sbuf_tensor(None, [128, 2, 32], mybir.dt.int32))
        acc = en(nc.psum_tensor(None, [128, NK * FT], F32))
        mA = en(nc.psum_tensor(None, [128, 512], F32))
        mB = en(nc.psum_tensor(None, [128, 512], F32))

        for eng in (t, v, s, g, q):
            kb.W(eng, *entry)
        kb.W(s, toks["init"], toks["bias"])

        g.dma_start(out=sw1r[:], in_=ap["sW1"].rearrange("(c p) n -> p c n", p=128)).then_inc(kb.dg.sem, 16)
        kb.dg.n += 16
        sw_tok = kb.dg.inc(g.dma_start(out=sw2r[:], in_=ap["sW2"].rearrange("(c p) n -> p c n", p=128)), 16)

        accs = lambda i: acc[:, i * FT:(i + 1) * FT]
        s1t = lambda dk, hk: sw1r[:, dk, hk * 128:(hk + 1) * 128]
        s2t = lambda hk, ik: sw2r[:, hk, ik * 128:(ik + 1) * 128]
        m_row = srow[0:1, 0:FT]
        msq_row = srow[0:1, FT:2 * FT]
        var_row = srow[0:1, 2 * FT:3 * FT]
        rstd_bf = srbf[0:1, 0:FT]
        mrstd_bf = srbf[0:1, FT:2 * FT]

        xtok = [None, None]
        xbtok = [None, None]
        x2tok = [None, None]
        srow_w = None
        srbf_w = None
        bc_rd = None
        u_rd = [None, None]
        mA_free = None
        mB_free = None
        acc_free = None
        hps_rd = [None, None]
        utp_rd = None
        utm_rd = [None, None]
        ltm_w = None
        cwtm_w = None
        wm_rd = None

        def mk_fill(ki3, sb3):
            def run():
                if (ki3, sb3) == (0, 0):
                    kb.W(g, kb.dq.tok(), kb.dmcp[0].tok(), kb.dmcp[1].tok())
                    g.memset(mcall[:], 0.0)
                g.tensor_scalar(mcall[:, ki3, sb3, 1:2], env["iota128c"][:],
                                float(ki3 * T + sb3 * 128), None, ALU.add)
                kb.pp.inc(g.tensor_scalar(mcall[:, ki3, sb3, 2:3], env["iota128c"][:],
                                          float(sb3 * 128), None, ALU.add))
            return run
        fill_jobs = [mk_fill(ki3, sb3) for ki3 in range(3) for sb3 in range(NSLAB)]
        smem0 = kb.vv.inc(v.memset(sentf[:], SENT))
        kb.W(q, smem0)
        sen_tok = None
        meta3 = wbf["metad"].rearrange("(a p) n -> p a n", p=128)
        for i in range(2):
            sen_tok = kb.dsen.inc(
                q.dma_start(out=meta3[:, i * 63:(i + 1) * 63, :],
                            in_=sentf[:, 0:63 * 16].rearrange("p (a n) -> p a n", n=16)), 16)

        def load_x(ti):
            p = ti % 2
            kb.W(q, xtok[p])
            a_tok = kb.dxp[p].inc(q.dma_start(out=xt[:, p], in_=src3[:, :, ti * FT:(ti + 1) * FT]), 16)
            kb.W(g, xbtok[p])
            b_tok = kb.dbp[p].inc(g.dma_start(out=xbf[:, p], in_=src3[:, :, ti * FT:(ti + 1) * FT]), 16)
            return a_tok, b_tok

        x_next = load_x(0)
        for ti in range(NFT):
            p = ti % 2
            x_tok, xb_tok = x_next
            if ti + 1 < NFT:
                x_next = load_x(ti + 1)
            if toks["cast_jobs"]:
                toks["cast_jobs"].pop(0)()
            for _ in range(7):
                if fill_jobs:
                    fill_jobs.pop(0)()

            kb.W(s, x_tok, x2tok[p])
            for k in range(NK):
                ii = s.activation(x2b[:, p, k], xt[:, p, k], AF.Square)
            sq_tok = ev.inc(ii)
            kb.W(t, xb_tok, sq_tok, mA_free)
            for k in range(NK):
                t.matmul(mA[0:1, 0:FT], env["ones_col"][:], xbf[:, p, k],
                         start=(k == 0), stop=(k == NK - 1))
            for k in range(NK):
                ii = t.matmul(mA[0:1, FT:2 * FT], env["ones_col"][:], x2b[:, p, k],
                              start=(k == 0), stop=(k == NK - 1))
            stats_tok = mm.inc(ii)
            xbtok[p] = stats_tok
            kb.W(v, stats_tok, srow_w)
            c_tok = vv.inc(v.tensor_scalar(m_row, mA[0:1, 0:FT], 1.0 / 1024, None, ALU.mult))
            kb.W(v, c_tok)
            c_tok = vv.inc(v.tensor_mul(msq_row, m_row, m_row))
            kb.W(v, c_tok)
            c_tok = vv.inc(v.scalar_tensor_tensor(var_row, mA[0:1, FT:2 * FT], 1.0 / 1024,
                                                  msq_row, ALU.mult, ALU.subtract))
            kb.W(s, c_tok)
            e_tok = ev.inc(s.activation(var_row, var_row, AF.Ln, bias=1e-5))
            kb.W(s, e_tok, srbf_w)
            e_tok = ev.inc(s.activation(rstd_bf, var_row, AF.Exp, scale=-0.5))
            kb.W(v, e_tok)
            c_tok = vv.inc(v.tensor_mul(mrstd_bf, m_row, rstd_bf))
            srow_w = c_tok
            kb.W(t, c_tok)
            t.matmul(mA[:, 0:FT], env["ones_row"][:], rstd_bf, start=True, stop=True)
            ii = t.matmul(mA[:, FT:2 * FT], env["ones_row"][:], mrstd_bf, start=True, stop=True)
            bc_tok = mm.inc(ii)
            srbf_w = bc_tok
            kb.W(s, bc_tok, bc_rd)
            s.copy(bcast[:, 0:FT], mA[:, 0:FT])
            bce_tok = ev.inc(s.copy(bcast[:, FT:2 * FT], mA[:, FT:2 * FT]))
            mA_free = bce_tok
            kb.W(v, bce_tok, u_rd[p])
            for k in range(NK):
                ii = v.tensor_mul(t1[:, k], xt[:, p, k], bcast[:, 0:FT])
            mul_tok = vv.inc(ii)
            kb.W(v, mul_tok)
            for k in range(NK):
                ii = v.tensor_sub(u[:, p, k], t1[:, k], bcast[:, FT:2 * FT])
            u_tok = vv.inc(ii)
            bc_rd = u_tok

            with nc.psum_tensor("utp%d" % (ti % 2), [128, D], BF16, allow_name_mangling=True) as utp:
                for cc in range(2):
                    kb.W(t, u_tok, utp_rd)
                    for dk in range(NK):
                        ii = t.transpose(utp[:, dk * 128:(dk + 1) * 128],
                                         u[:, p, dk, cc * 128:(cc + 1) * 128], env["ident_bf"][:])
                    tr_tok = mm.inc(ii)
                    kb.W(v, tr_tok, utm_rd[cc])
                    uc_tok = vv.inc(v.tensor_copy(u_tm[:, cc], utp[:]))
                    utp_rd = uc_tok
                    kb.W(q, uc_tok)
                    utm_rd[cc] = kb.dU[cc].inc(
                        q.dma_start(out=wbf["Ud"][ti * FT + cc * 128: ti * FT + (cc + 1) * 128, :],
                                    in_=u_tm[:, cc]), 16)


            with nc.psum_tensor("hps%d" % (ti % 2), [128, 1024], F32, allow_name_mangling=True) as hps:
                hsl = lambda b: hps[:, b * 512:b * 512 + FT]
                h_rd = mm.tok()
                for j in range(NK):
                    slot = j % 2
                    kb.W(t, sw_tok, u_tok, hps_rd[slot], utp_rd)
                    for dk in range(NK):
                        ii = t.matmul(hsl(slot), s1t(dk, j), u[:, p, dk],
                                      start=(dk == 0), stop=(dk == NK - 1))
                    w1tok = mm.inc(ii)
                    kb.W(s, w1tok)
                    ii = s.activation(x2b[:, p, j], hsl(slot), AF.Relu, bias=bias["sb1"](j))
                    rl_tok = ev.inc(ii)
                    hps_rd[slot] = rl_tok
                    kb.W(t, rl_tok)
                    if j == 0:
                        kb.W(t, acc_free)
                    for ik in range(NK):
                        ii = t.matmul(accs(ik), s2t(j, ik), x2b[:, p, j],
                                      start=(j == 0 and ik % 2 == 0),
                                      stop=(j == NK - 1 and ik % 2 == 1))
                    mm.inc(ii)
            x2tok[p] = mm.tok()

            kb.W(t, u_tok, toks["rw"], mB_free)
            for k in range(NK):
                ii = t.matmul(mB[0:NEXP, 0:FT], env["rw_bf"][:, k * NEXP:(k + 1) * NEXP],
                              u[:, p, k], start=(k == 0), stop=(k == NK - 1))
            rt_tok = mm.inc(ii)
            kb.W(v, rt_tok)
            lrt_tok = vv.inc(v.tensor_scalar(Lrt[:], mB[0:NEXP, 0:FT], env["rb_s"][:], None, ALU.add))
            kb.W(t, lrt_tok, ltm_w)
            t.transpose(mB[:, FT:FT + NEXP], Lrt[:, 0:128], env["ident_f"][0:NEXP, 0:NEXP])
            ii = t.transpose(mB[:, FT + NEXP:FT + 2 * NEXP], Lrt[:, 128:256],
                             env["ident_f"][0:NEXP, 0:NEXP])
            ltr_tok = mm.inc(ii)
            kb.W(v, ltr_tok)
            v.tensor_copy(Ltm[:, 0], mB[:, FT:FT + NEXP])
            ltm_tok = vv.inc(v.tensor_copy(Ltm[:, 1], mB[:, FT + NEXP:FT + 2 * NEXP]))
            ltm_w = ltm_tok

            P = lambda sl: smx[:, sl, 0]
            P1 = lambda sl: smx[:, sl, 1]
            P2 = lambda sl: smx[:, sl, 2]
            WMT = lambda sl: smx[:, sl, 3]
            MK = lambda sl: smk[:, sl]
            m1n = lambda sl: smr[:, sl, 0:1]
            m1p = lambda sl: smr[:, sl, 1:2]
            m2r = lambda sl: smr[:, sl, 2:3]
            m3r = lambda sl: smr[:, sl, 3:4]
            rr = lambda sl: smr[:, sl, 4:5]
            ek_c = lambda sl, k: ecols[:, ti * 2 + sl, k:k + 1]
            cw_c = lambda sl, k: ccols[:, ti * 2 + sl, k:k + 1]

            kb.W(v, ltm_tok, toks["init"])
            for sl in (0, 1):
                ii = v.reduce_max(m1n(sl), Ltm[:, sl], axis=AX.X, negate=True)
            c_tok = vv.inc(ii)
            kb.W(s, c_tok)
            for sl in (0, 1):
                ii = s.activation(P(sl), Ltm[:, sl], AF.Exp, bias=m1n(sl))
            e_tok = ev.inc(ii)
            kb.W(v, e_tok)
            for sl in (0, 1):
                v.reduce_sum(rr(sl), P(sl), axis=AX.X)
            for sl in (0, 1):
                ii = v.reduce_max(m1p(sl), P(sl), axis=AX.X)
            c_tok = vv.inc(ii)
            kb.W(v, c_tok)
            for sl in (0, 1):
                ii = v.reciprocal(rr(sl), rr(sl))
            c_tok = vv.inc(ii)
            kb.W(v, c_tok)
            for sl in (0, 1):
                ii = v.tensor_scalar(WMT(sl), P(sl), m1p(sl), None, ALU.is_ge)
            c_tok = vv.inc(ii)
            kb.W(v, c_tok)
            for sl in (0, 1):
                ii = v.scalar_tensor_tensor(P1(sl), WMT(sl), -2.0, P(sl), ALU.mult, ALU.add)
            c_tok = vv.inc(ii)
            kb.W(v, c_tok)
            for sl in (0, 1):
                ii = v.reduce_max(m2r(sl), P1(sl), axis=AX.X)
            c_tok = vv.inc(ii)
            kb.W(v, c_tok)
            for sl in (0, 1):
                ii = v.tensor_scalar(WMT(sl), P1(sl), m2r(sl), None, ALU.is_ge)
            c_tok = vv.inc(ii)
            kb.W(v, c_tok)
            for sl in (0, 1):
                ii = v.scalar_tensor_tensor(P2(sl), WMT(sl), -2.0, P1(sl), ALU.mult, ALU.add)
            c_tok = vv.inc(ii)
            kb.W(v, c_tok)
            for sl in (0, 1):
                ii = v.reduce_max(m3r(sl), P2(sl), axis=AX.X)
            top_tok = vv.inc(ii)

            way_tok = top_tok
            for ki, mk in ((0, m1p), (1, m2r), (2, m3r)):
                kb.W(v, way_tok)
                for sl in (0, 1):
                    ii = v.tensor_scalar(wmtm[:, sl], P(sl), mk(sl), None, ALU.is_equal)
                wt1 = vv.inc(ii)
                kb.W(v, wt1)
                for sl in (0, 1):
                    ii = v.tensor_mul(WMT(sl), wmtm[:, sl], env["iota15b"][:])
                wt2 = vv.inc(ii)
                kb.W(v, wt2)
                for sl in (0, 1):
                    ii = v.reduce_sum(ek_c(sl, ki), WMT(sl), axis=AX.X)
                vv.inc(ii)
                for sl in (0, 1):
                    ii = v.tensor_scalar(cw_c(sl, ki), mk(sl), rr(sl), 0.5, ALU.mult, ALU.mult)
                wt3 = vv.inc(ii)
                kb.W(t, wt1, cwtm_w)
                for sl in (0, 1):
                    ii = t.transpose(mB[0:NEXP, 256 + sl * 128: 256 + (sl + 1) * 128],
                                     wmtm[:, sl], env["ident_f"][:])
                wt4 = mm.inc(ii)
                kb.W(v, wt4)
                for sl in (0, 1):
                    stile = ti * 2 + sl
                    ii = v.tensor_copy(wmfm[:, ki, stile * 128:(stile + 1) * 128],
                                       mB[0:NEXP, 256 + sl * 128:256 + (sl + 1) * 128])
                way_tok = vv.inc(ii)
                cwtm_w = way_tok
            wm_rd = way_tok
            mB_free = way_tok


            acc_tok = mm.tok()
            kb.W(v, acc_tok)
            for ik in range(NK):
                ii = v.scalar_tensor_tensor(xt[:, p, ik], accs(ik), 0.5, xt[:, p, ik],
                                            ALU.mult, ALU.add)
            fe_tok = vv.inc(ii)
            acc_free = fe_tok
            kb.W(v, fe_tok)
            for ik in range(NK):
                ii = v.tensor_scalar(xt[:, p, ik], xt[:, p, ik],
                                     env["sbh"][:, ik:ik + 1], None, ALU.add)
            add_tok = vv.inc(ii)
            u_rd[p] = mm.tok()
            kb.W(q, add_tok)
            kb.dst.n += 16
            xtok[p] = kb.dsp[p].inc(q.dma_start(out=dst3[:, :, ti * FT:(ti + 1) * FT], in_=xt[:, p]), 16)

        while toks["cast_jobs"]:
            toks["cast_jobs"].pop(0)()
        while fill_jobs:
            fill_jobs.pop(0)()
        static_fill = kb.pp.tok()

        if STAGE == "A":
            ctx.close()
            octx.close()
            return
        kb.W(v, wm_rd)
        c_tok = vv.inc(v.tensor_add(msum[:], wmfm[:, 0, :], wmfm[:, 1, :]))
        kb.W(v, c_tok)
        ms_tok = vv.inc(v.tensor_add(msum[:], msum[:], wmfm[:, 2, :]))
        kb.W(v, ms_tok)
        cnt_tok = vv.inc(v.reduce_sum(r15c[:, 0:1], msum[:], axis=AX.X))
        kb.W(t, cnt_tok, mA_free)
        ii = t.transpose(mA[0:1, 0:NEXP], r15c[:, 0:1], env["ident15f"][:])
        ct_tok = mm.inc(ii)
        kb.W(v, ct_tok)
        c_tok = vv.inc(v.tensor_copy(rA[0:1, 0:NEXP], mA[0:1, 0:NEXP]))
        kb.W(v, c_tok)
        c_tok = vv.inc(v.memset(rB[0:1, 0:NEXP], 0.0))
        for m2 in range(16):
            kb.W(v, c_tok)
            c_tok = vv.inc(v.tensor_scalar(rC[0:1, 0:NEXP], rA[0:1, 0:NEXP],
                                           float(256 * m2), None, ALU.is_gt))
            kb.W(v, c_tok)
            c_tok = vv.inc(v.tensor_add(rB[0:1, 0:NEXP], rB[0:1, 0:NEXP], rC[0:1, 0:NEXP]))
        kb.W(v, c_tok)
        c_tok = vv.inc(v.memset(rC[0:1, 0:NEXP], 0.0))
        kb.W(v, c_tok)
        c_tok = vv.inc(v.tensor_tensor_scan(rA[0:1, 0:NEXP], rB[0:1, 0:NEXP],
                                            rC[0:1, 0:NEXP], 0.0, ALU.add, ALU.add))
        kb.W(v, c_tok)
        jb_tok = vv.inc(v.tensor_sub(rA[0:1, 0:NEXP], rA[0:1, 0:NEXP], rB[0:1, 0:NEXP]))
        kb.W(v, jb_tok)
        c_tok = vv.inc(v.tensor_scalar(rC[0:1, 0:NEXP], rA[0:1, 0:NEXP], 256.0, None, ALU.mult))
        kb.W(t, c_tok)
        ii = t.transpose(mA[0:NEXP, 256:257], rC[0:1, 0:NEXP], env["ident_f"][0:1, 0:1])
        sb_tok = mm.inc(ii)
        kb.W(v, sb_tok)
        sbc_tok = vv.inc(v.tensor_copy(r15c[:, 1:2], mA[0:NEXP, 256:257]))
        kb.W(v, sbc_tok)
        c_tok = vv.inc(v.memset(rB[0:1, 0:64], 0.0))
        for e in range(NEXP):
            kb.W(v, c_tok)
            c_tok = vv.inc(v.tensor_scalar(rC[0:1, 0:64], env["iota64r"][:],
                                           rA[0:1, e:e + 1], None, ALU.is_ge))
            kb.W(v, c_tok)
            c_tok = vv.inc(v.tensor_add(rB[0:1, 0:64], rB[0:1, 0:64], rC[0:1, 0:64]))
        kb.W(v, c_tok)
        c_tok = vv.inc(v.tensor_scalar(rB[0:1, 0:64], rB[0:1, 0:64], -1.0, None, ALU.add))
        kb.W(v, c_tok)
        ebf_tok = vv.inc(v.tensor_copy(rBf[0:1, 0:64], rB[0:1, 0:64]))
        kb.W(t, ebf_tok)
        ii = t.matmul(mA[:, 384:448], env["ones_row"][:], rBf[0:1, 0:64], start=True, stop=True)
        ec_tok = mm.inc(ii)
        kb.W(v, ec_tok)
        ecall_tok = vv.inc(v.tensor_copy(ecall[:], mA[:, 384:448]))
        kb.W(v, ecall_tok)
        for j in range(NJ):
            ii = v.scalar_tensor_tensor(t1[:, 0, j:j + 1], ecall[:, j:j + 1], 128.0,
                                        env["iota128c"][:], ALU.mult, ALU.add)
        c_tok = vv.inc(ii)
        kb.W(v, c_tok)
        wi_tok = vv.inc(v.tensor_copy(widx[:, 0:NJ], t1[:, 0, 0:NJ]))
        kb.W(t, ecall_tok)
        ii = t.transpose(mA[0:64, 257:258], rB[0:1, 0:64], env["ident_f"][0:1, 0:1])
        ejt_tok = mm.inc(ii)
        kb.W(v, ejt_tok)
        ej_tok = vv.inc(v.tensor_copy(ej32[:], mA[0:64, 257:258]))
        kb.W(g, ej_tok, toks["eb2d"])
        g.indirect_dma_start(
            out=eb2all[:], out_offset=None, in_=wbf["eb2d"],
            in_offset=IndirectOffsetOnAxis(ap=ej32[:], axis=0)).then_inc(kb.dsc.sem, 16)
        kb.dsc.n += 16
        kb.W(q, kb.dsc.tok())
        ej_store = kb.dsen.inc(q.dma_start(out=wbf["eb2j"], in_=eb2all[:]), 16)
        kb.W(v, ms_tok)
        c_tok = vv.inc(v.memset(prodf[:], 0.0))
        kb.W(v, c_tok)
        c_tok = vv.inc(v.tensor_tensor_scan(rankf[:], msum[:], prodf[:], 0.0, ALU.add, ALU.add))
        kb.W(v, c_tok)
        c_tok = vv.inc(v.tensor_sub(rankf[:], rankf[:], msum[:]))
        kb.W(v, c_tok, sbc_tok)
        slot_tok = vv.inc(v.tensor_scalar(rankf[:], rankf[:], r15c[:, 1:2], None, ALU.add))
        interlude_mm = mm.tok()

        if DEBUG_DUMP:
            with nc.sbuf_tensor("dbgs", [128, 132], F32) as dbgs:
                kb.W(v, wi_tok, ecall_tok, ej_tok)
                v.tensor_copy(dbgs[:, 0:64], t1[:, 0, 0:64])
                v.tensor_copy(dbgs[:, 64:128], widx[:, 0:64])
                d_tok = vv.inc(v.tensor_copy(dbgs[0:64, 129:130], ej32[:]))
                kb.W(q, d_tok)
                dmp = kb.dq.inc(q.dma_start(out=wbf["dbg"], in_=dbgs[:]), 16)
                kb.W(q, dmp)

        kb.W(g, kb.vv.tok())
        for ki in range(3):
            for sb in range(NSLAB):
                ii = g.tensor_copy(mcall[:, ki, sb, 0:1], ccols[:, sb, ki:ki + 1])
        cw_fill = kb.pp.inc(ii)
        scat_mm = interlude_mm
        sc32_tok = [None, None]
        way_scat = [None, None]
        mAcol_rd = None
        for ki in range(3):
            kp = ki % 2
            kb.W(v, slot_tok, scat_mm)
            pr_tok = vv.inc(v.tensor_mul(prodf[:], rankf[:], wmfm[:, ki, :]))
            sl_tok = None
            for pc in range(T // 512):
                kb.W(t, pr_tok, interlude_mm, mB_free)
                ii = t.matmul(mB[0:1, 0:512], env["ones15f"][:],
                              prodf[:, pc * 512:(pc + 1) * 512], start=True, stop=True)
                pm_tok = mm.inc(ii)
                kb.W(v, pm_tok)
                ii = v.tensor_copy(slotw[0:1, pc * 512:(pc + 1) * 512], mB[0:1, 0:512])
                sl_tok = vv.inc(ii)
                mB_free = sl_tok
            kb.W(t, sl_tok, ecall_tok, mAcol_rd)
            for sb in range(NSLAB):
                ii = t.transpose(mA[:, 448 + sb:449 + sb],
                                 slotw[0:1, sb * 128:(sb + 1) * 128], env["ident_f"][0:1, 0:1])
            st_tok = mm.inc(ii)
            kb.W(v, st_tok, way_scat[kp])
            sc32_tok[kp] = vv.inc(v.tensor_copy(slc32[:, kp], mA[:, 448:480]))
            mAcol_rd = sc32_tok[kp]
            kb.W(g, sc32_tok[kp], cw_fill, static_fill, ej_tok, kb.dsen.tok())
            ms2_tok = None
            for sb in range(NSLAB):
                ii = g.indirect_dma_start(
                    out=wbf["metad"],
                    out_offset=IndirectOffsetOnAxis(ap=slc32[:, kp, sb:sb + 1], axis=0),
                    in_=mcall[:, ki, sb], in_offset=None)
                ms2_tok = kb.dmcp[kp].inc(ii, 16)
            way_scat[kp] = ms2_tok
            scat_mm = mm.tok()
        scat_all = (kb.dmcp[0].tok(), kb.dmcp[1].tok())

        ctx.close()
        joinA = tuple(c.tok() for c in (
            kb.vv, kb.ev, kb.mm, kb.dsp[0], kb.dsp[1], kb.dxp[0],
            kb.dxp[1], kb.dbp[0], kb.dbp[1], kb.dg, kb.dU[0], kb.dU[1], kb.dug[0],
            kb.dug[1], kb.dsen, kb.dsc, *kb.dscp, *kb.dmcp, *kb.ecast) if c.n > 0)

        if STAGE == "scat":
            ctx.close()
            octx.close()
            return
        with ExitStack() as ctxb:
            enb = ctxb.enter_context
            wg = enb(nc.sbuf_tensor(None, [128, 2, 2, NK, D], BF16))
            ufm = enb(nc.sbuf_tensor(None, [128, 2, NK, CAP], BF16))
            u_tm2 = enb(nc.sbuf_tensor(None, [128, 2, 2, D], BF16))
            h2b = enb(nc.sbuf_tensor(None, [128, NK, CAP], BF16))
            eb1g = enb(nc.sbuf_tensor(None, [128, 2, NK], F32))
            eb2g = enb(nc.sbuf_tensor(None, [1, 2, D], BF16))
            mt = enb(nc.sbuf_tensor(None, [128, 2, 2, 16], F32))
            iy = enb(nc.sbuf_tensor(None, [128, 2, 2], mybir.dt.int32))
            iu = enb(nc.sbuf_tensor(None, [128, 2, 2], mybir.dt.int32))
            ysc = enb(nc.sbuf_tensor(None, [128, 2, 1024], BF16))
            hpsB = enb(nc.psum_tensor(None, [128, 1024], F32))
            yps = enb(nc.psum_tensor(None, [128, 2, 2, 512], F32))
            utpB = enb(nc.psum_tensor(None, [128, 2, 1024], BF16))

            wg_rd = [None, None]
            mt_rd = [None, None]
            h2_rd2 = {}
            hps_rd2 = [None, None]
            ysc_rd = {}
            iy_w = [None, None]
            iu_rd = [None, None]
            u_rd2 = [None, None]
            utk_arr = [None, None]
            cp_arr = [None, None]
            utp_rdB = [None]

            def load_job(j):
                pj = j % 2
                kb.W(g, wg_rd[pj], wi_tok, *scat_all, *joinA)
                g.indirect_dma_start(
                    out=wg[:, pj].rearrange("p t c n -> p (t c n)"), out_offset=None, in_=w12f,
                    in_offset=IndirectOffsetOnAxis(ap=widx[:, j:j + 1], axis=0)).then_inc(kb.dwg[pj].sem, 16)
                kb.dwg[pj].n += 16
                g.indirect_dma_start(
                    out=eb1g[:, pj], out_offset=None, in_=eb1f,
                    in_offset=IndirectOffsetOnAxis(ap=widx[:, j:j + 1], axis=0)).then_inc(kb.dwg[pj].sem, 16)
                kb.dwg[pj].n += 16
                wtk = kb.dwg[pj].tok()
                kb.W(q, mt_rd[pj], wg_rd[pj], *scat_all, ej_store, *joinA)
                q.dma_start(out=mt[:, pj], in_=wbf["metad"][j * CAP:(j + 1) * CAP, :]
                            .rearrange("(c p) n -> p c n", p=128)).then_inc(kb.dmt[pj].sem, 16)
                kb.dmt[pj].n += 16
                mtk = kb.dmt[pj].inc(
                    q.dma_start(out=eb2g[0:1, pj], in_=wbf["eb2j"][j:j + 1, :]), 16)
                return wtk, mtk

            def issue_u_gather(j):
                pj = j % 2
                kb.W(v, jl[j][1], iu_rd[pj])
                v.tensor_copy(iu[:, pj, 0:1], mt[:, pj, 0, 2:3])
                iu_tok = vv.inc(v.tensor_copy(iu[:, pj, 1:2], mt[:, pj, 1, 2:3]))
                kb.W(g, iu_tok, u_rd2[pj])
                utk = None
                for cc in range(2):
                    utk = kb.dut[pj].inc(g.indirect_dma_start(
                        out=u_tm2[:, pj, cc], out_offset=None, in_=wbf["Ud"],
                        in_offset=IndirectOffsetOnAxis(ap=iu[:, pj, cc:cc + 1], axis=0)), 16)
                utk_arr[pj] = utk
                iu_rd[pj] = utk

            jl = {0: load_job(0), 1: load_job(1)}
            if STAGE == "B0":
                ctxb.close()
                octx.close()
                return
            vv.inc(v.memset(u_tm2[:], 0.0))
            issue_u_gather(0)
            issue_u_gather(1)
            for j in range(NJ):
                pj = j % 2
                wtk, mtk = jl[j]
                if STAGE == "BL":
                    if j + 2 <= NJ - 1:
                        jl[j + 2] = load_job(j + 2)
                    continue
                if 2 <= j + 1 < NJ:
                    issue_u_gather(j + 1)
                kb.W(t, utk_arr[pj], utp_rdB[0])
                cp = None
                for cc in range(2):
                    for dk in range(NK):
                        ii = t.transpose(utpB[:, cc, dk * 128:(dk + 1) * 128],
                                         u_tm2[:, pj, cc, dk * 128:(dk + 1) * 128],
                                         env["ident_bf"][:])
                    tr_tok = mm.inc(ii)
                    kb.W(s, tr_tok)
                    cp = ev.inc(s.copy(
                        ufm[:, pj, :, cc * 128:(cc + 1) * 128],
                        utpB[:, cc].rearrange("p (k f) -> p k f", f=128)))
                u_rd2[pj] = tr_tok
                utp_rdB[0] = cp
                cp_arr[pj] = cp
                kb.W(v, mtk, iy_w[pj])
                c_tok = vv.inc(v.tensor_copy(iy[:, pj, 0:1], mt[:, pj, 0, 1:2]))
                kb.W(v, c_tok)
                c_tok = vv.inc(v.tensor_scalar(mt[:, pj, 1, 2:3], mt[:, pj, 1, 1:2],
                                               SENT, None, ALU.is_ge))
                kb.W(v, c_tok)
                c_tok = vv.inc(v.tensor_add(mt[:, pj, 1, 2:3], mt[:, pj, 1, 2:3],
                                            mt[:, pj, 1, 1:2]))
                kb.W(v, c_tok)
                iy_tok = vv.inc(v.tensor_copy(iy[:, pj, 1:2], mt[:, pj, 1, 2:3]))
                for hk in range(NK):
                    sl2 = hk % 2
                    kb.W(t, wtk, cp_arr[pj], hps_rd2[sl2])
                    for dk in range(NK):
                        ii = t.matmul(hpsB[:, sl2 * 512:sl2 * 512 + CAP],
                                      wg[:, pj, 0, dk, hk * 128:(hk + 1) * 128],
                                      ufm[:, pj, dk], start=(dk == 0), stop=(dk == NK - 1))
                    w1tok = mm.inc(ii)
                    kb.W(s, w1tok, h2_rd2.get(hk % 4))
                    ii = s.activation(h2b[:, hk], hpsB[:, sl2 * 512:sl2 * 512 + CAP],
                                      AF.Relu, bias=eb1g[:, pj, hk:hk + 1])
                    h2tok = ev.inc(ii)
                    hps_rd2[sl2] = h2tok
                    h2_rd2[hk % 4] = h2tok
                for sc2 in (() if STAGE == "BW1" else (0, 1)):
                    for hf in range(2):
                        kb.W(t, h2_rd2[3], mtk, ysc_rd.get((sc2, hf)))
                        for hk in range(NK):
                            kb.W(t, h2_rd2[hk % 4])
                            ii = t.matmul(yps[:, sc2, hf],
                                          h2b[:, hk, sc2 * 128:(sc2 + 1) * 128],
                                          wg[:, pj, 1, hk, hf * 512:(hf + 1) * 512],
                                          start=(hk == 0), stop=False)
                        ii = t.matmul(yps[:, sc2, hf], env["ones_row"][:],
                                      eb2g[0:1, pj, hf * 512:(hf + 1) * 512],
                                      start=False, stop=True)
                        w2tok = mm.inc(ii)
                        h2_rd2[3] = w2tok
                        kb.W(v, w2tok, iy_tok)
                        ii = v.tensor_scalar(ysc[:, sc2, hf * 512:(hf + 1) * 512],
                                             yps[:, sc2, hf], mt[:, pj, sc2, 0:1], None, ALU.mult)
                        ysck = vv.inc(ii)
                        ysc_rd[(sc2, hf)] = None
                        if hf == 1:
                            kb.W(g, ysck)
                            yst = kb.dy[sc2].inc(g.indirect_dma_start(
                                out=wbf["R3d"],
                                out_offset=IndirectOffsetOnAxis(ap=iy[:, pj, sc2:sc2 + 1], axis=0),
                                in_=ysc[:, sc2], in_offset=None), 16)
                            ysc_rd[(sc2, 0)] = yst
                            ysc_rd[(sc2, 1)] = yst
                iy_w[pj] = mm.tok()
                wg_rd[pj] = mm.tok()
                mt_rd[pj] = kb.vv.tok()
                if j + 2 <= NJ - 1:
                    jl[j + 2] = load_job(j + 2)

        if STAGE == "B":
            octx.close()
            return
        with ExitStack() as ctxc:
            enc = ctxc.enter_context
            NPC = 4
            r3s = enc(nc.sbuf_tensor(None, [128, NPC, 3, D], BF16))
            rsum = enc(nc.sbuf_tensor(None, [128, 2, D], F32))
            xc = enc(nc.sbuf_tensor(None, [128, NPC, NK, 128], F32))
            rtp = enc(nc.psum_tensor(None, [128, 2, D], F32))
            r3_rd = [None] * NPC
            rs_rd = [None, None]
            xc_rd = [None] * NPC
            rtp_rd = [None, None]
            ydone = (kb.dy[0].tok(), kb.dy[1].tok())

            def loadC(ch):
                par = ch % NPC
                p2 = ch % 2
                eng = q if p2 == 0 else s
                kb.W(eng, r3_rd[par], xc_rd[par], *ydone)
                rt = None
                for ki in range(3):
                    rt = kb.dR[p2].inc(
                        eng.dma_start(out=r3s[:, par, ki],
                                      in_=wbf["R3d"][ki * T + ch * 128: ki * T + (ch + 1) * 128, :]), 16)
                xl = kb.dxp[p2].inc(
                    eng.dma_start(out=xc[:, par], in_=dst3[:, :, ch * 128:(ch + 1) * 128]), 16)
                return rt, xl

            pre = {c: loadC(c) for c in range(min(NPC, NSLAB))}
            for ch in range(NSLAB):
                par = ch % NPC
                p2 = ch % 2
                rt, xl = pre[ch]
                kb.W(v, rt, rs_rd[p2])
                c_tok = vv.inc(v.tensor_add(rsum[:, p2], r3s[:, par, 0], r3s[:, par, 1]))
                kb.W(v, c_tok)
                rs_tok = vv.inc(v.tensor_add(rsum[:, p2], rsum[:, p2], r3s[:, par, 2]))
                r3_rd[par] = rs_tok
                kb.W(t, rs_tok, rtp_rd[p2])
                for dk in range(NK):
                    ii = t.transpose(rtp[:, p2, dk * 128:(dk + 1) * 128],
                                     rsum[:, p2, dk * 128:(dk + 1) * 128], env["ident_f"][:])
                rtr_tok = mm.inc(ii)
                rs_rd[p2] = rtr_tok
                kb.W(v, rtr_tok, xl)
                for dk in range(NK):
                    ii = v.tensor_add(xc[:, par, dk], xc[:, par, dk],
                                      rtp[:, p2, dk * 128:(dk + 1) * 128])
                xa_tok = vv.inc(ii)
                rtp_rd[p2] = xa_tok
                seng = q if p2 == 0 else s
                kb.W(seng, xa_tok)
                kb.dst.n += 16
                xc_rd[par] = kb.dsp[p2].inc(
                    seng.dma_start(out=dst3[:, :, ch * 128:(ch + 1) * 128], in_=xc[:, par]), 16)
                if ch + NPC < NSLAB:
                    pre[ch + NPC] = loadC(ch + NPC)
        kb.W(q, kb.dsp[0].tok(), kb.dsp[1].tok())
    octx.close()


def emit_ffn(nc, kb, ap, wbf, src, dst, env, toks):
    t, v, s, g, q = kb.t, kb.v, kb.s, kb.g, kb.q
    mm, ev, vv = kb.mm, kb.ev, kb.vv
    bias = env["bias"]
    src3 = src.rearrange("(k p) n -> p k n", p=128)
    dst3 = dst.rearrange("(k p) n -> p k n", p=128)

    entry = [kb.dsp[0].tok(), kb.dsp[1].tok()]
    with ExitStack() as ctx:
        xt = ctx.enter_context(nc.sbuf_tensor(None, [128, 2, NK, FT], F32))
        xbf = ctx.enter_context(nc.sbuf_tensor(None, [128, 2, NK, FT], BF16))
        x2bf = ctx.enter_context(nc.sbuf_tensor(None, [128, NK, FT], BF16))
        u = ctx.enter_context(nc.sbuf_tensor(None, [128, NK, FT], BF16))
        t1 = ctx.enter_context(nc.sbuf_tensor(None, [128, NK, FT], F32))
        bcast = ctx.enter_context(nc.sbuf_tensor(None, [128, 2 * FT], F32))
        srow = ctx.enter_context(nc.sbuf_tensor(None, [1, 3 * FT], F32))
        srbf = ctx.enter_context(nc.sbuf_tensor(None, [1, 2 * FT], BF16))
        Lrt = ctx.enter_context(nc.sbuf_tensor(None, [NEXP, FT], F32))
        Ltm = ctx.enter_context(nc.sbuf_tensor(None, [128, 2, NEXP], F32))
        smx = ctx.enter_context(nc.sbuf_tensor(None, [128, 2, 4, NEXP], F32))
        smk = ctx.enter_context(nc.sbuf_tensor(None, [128, 2, NEXP], mybir.dt.int32))
        smr = ctx.enter_context(nc.sbuf_tensor(None, [128, 2, 8], F32))
        cw_tm = ctx.enter_context(nc.sbuf_tensor(None, [128, 2, NEXP], F32))
        cw_fm = ctx.enter_context(nc.sbuf_tensor(None, [NEXP, FT], BF16))
        cwb = ctx.enter_context(nc.sbuf_tensor(None, [128, NEXP, FT], BF16))
        h_bf = ctx.enter_context(nc.sbuf_tensor(None, [128, 4, FT], BF16))
        h2_bf = ctx.enter_context(nc.sbuf_tensor(None, [128, 4, FT], BF16))
        sw1r = ctx.enter_context(nc.sbuf_tensor(None, [128, NK, D], BF16))
        sw2r = ctx.enter_context(nc.sbuf_tensor(None, [128, NK, D], BF16))
        w1b = ctx.enter_context(nc.sbuf_tensor(None, [128, 2, NK, D], BF16))
        w2b = ctx.enter_context(nc.sbuf_tensor(None, [128, 2, NK, D], BF16))
        acc = ctx.enter_context(nc.psum_tensor(None, [128, NK * FT], F32))
        hps = ctx.enter_context(nc.psum_tensor(None, [128, 1024], F32))
        mA = ctx.enter_context(nc.psum_tensor(None, [128, 512], F32))
        mB = ctx.enter_context(nc.psum_tensor(None, [128, 512], F32))
        for eng in (t, v, s, g, q):
            kb.W(eng, *entry)
        kb.W(s, toks["init"], toks["bias"])
        sw_tok = None
        g.dma_start(out=sw1r[:], in_=ap["sW1"].rearrange("(c p) n -> p c n", p=128)).then_inc(kb.dg.sem, 16)
        kb.dg.n += 16
        sw_tok = kb.dg.inc(g.dma_start(out=sw2r[:], in_=ap["sW2"].rearrange("(c p) n -> p c n", p=128)), 16)

        accs = lambda i: acc[:, i * FT:(i + 1) * FT]
        hsl = lambda b: hps[:, b * 512:b * 512 + FT]
        w1t = lambda pe, dk, hk: w1b[:, pe, dk, hk * 128:(hk + 1) * 128]
        w2t = lambda pe, hk, ik: w2b[:, pe, hk, ik * 128:(ik + 1) * 128]
        s1t = lambda dk, hk: sw1r[:, dk, hk * 128:(hk + 1) * 128]
        s2t = lambda hk, ik: sw2r[:, hk, ik * 128:(ik + 1) * 128]
        m_row = srow[0:1, 0:FT]
        msq_row = srow[0:1, FT:2 * FT]
        var_row = srow[0:1, 2 * FT:3 * FT]
        rstd_bf = srbf[0:1, 0:FT]
        mrstd_bf = srbf[0:1, FT:2 * FT]

        wtok = [None, None]
        xtok = [None, None]
        xbtok = [None, None]
        x2tok = None
        srow_w = None
        srbf_w = None
        bc_rd = None
        u_rd = None
        ltm_w = None
        cw_rd = None
        cwtm_w = None
        cwb_rd = [None] * NEXP
        mA_free = None
        mB_free = None
        acc_free = None
        t1_rd = None
        h_rd = [None] * 4
        h2_rd = [None] * 4
        hps_rd = [None, None]

        def load_x(ti):
            p = ti % 2
            kb.W(q, xtok[p])
            xt_tok = kb.dxp[p].inc(q.dma_start(out=xt[:, p], in_=src3[:, :, ti * FT:(ti + 1) * FT]), 16)
            kb.W(g, xbtok[p])
            xb_tok = kb.dbp[p].inc(g.dma_start(out=xbf[:, p], in_=src3[:, :, ti * FT:(ti + 1) * FT]), 16)
            return xt_tok, xb_tok

        def load_exp(e, ti):
            pe = e % 2
            if ti == 0:
                kb.W(q, toks["cast"][e])
            kb.W(q, wtok[pe])
            kb.dwp[pe].inc(q.dma_start(out=w1b[:, pe], in_=wbf["eW1"][e]), 16)
            return kb.dwp[pe].inc(q.dma_start(out=w2b[:, pe], in_=wbf["eW2"][e]), 16)

        x_next = load_x(0)
        for ti in range(NFT):
            p = ti % 2
            x_tok, xb_tok = x_next
            wld = {0: load_exp(0, ti), 1: load_exp(1, ti)}

            kb.W(s, x_tok, x2tok)
            for k in range(NK):
                ii = s.activation(x2bf[:, k], xt[:, p, k], AF.Square)
            sq_tok = ev.inc(ii)

            kb.W(t, xb_tok, sq_tok, mA_free)
            for k in range(NK):
                t.matmul(mA[0:1, 0:FT], env["ones_col"][:], xbf[:, p, k],
                         start=(k == 0), stop=(k == NK - 1))
            for k in range(NK):
                ii = t.matmul(mA[0:1, FT:2 * FT], env["ones_col"][:], x2bf[:, k],
                              start=(k == 0), stop=(k == NK - 1))
            stats_tok = mm.inc(ii)
            xbtok[p] = stats_tok
            x2tok = stats_tok

            kb.W(v, stats_tok, srow_w)
            c_tok = vv.inc(v.tensor_scalar(m_row, mA[0:1, 0:FT], 1.0 / 1024, None, ALU.mult))
            kb.W(v, c_tok)
            c_tok = vv.inc(v.tensor_mul(msq_row, m_row, m_row))
            kb.W(v, c_tok)
            c_tok = vv.inc(v.scalar_tensor_tensor(var_row, mA[0:1, FT:2 * FT], 1.0 / 1024,
                                                  msq_row, ALU.mult, ALU.subtract))
            kb.W(s, c_tok)
            e_tok = ev.inc(s.activation(var_row, var_row, AF.Ln, bias=1e-5))
            kb.W(s, e_tok, srbf_w)
            e_tok = ev.inc(s.activation(rstd_bf, var_row, AF.Exp, scale=-0.5))
            kb.W(v, e_tok)
            c_tok = vv.inc(v.tensor_mul(mrstd_bf, m_row, rstd_bf))
            srow_w = c_tok

            kb.W(t, c_tok)
            t.matmul(mA[:, 0:FT], env["ones_row"][:], rstd_bf, start=True, stop=True)
            ii = t.matmul(mA[:, FT:2 * FT], env["ones_row"][:], mrstd_bf, start=True, stop=True)
            bc_tok = mm.inc(ii)
            srbf_w = bc_tok
            kb.W(s, bc_tok, bc_rd)
            s.copy(bcast[:, 0:FT], mA[:, 0:FT])
            bce_tok = ev.inc(s.copy(bcast[:, FT:2 * FT], mA[:, FT:2 * FT]))

            kb.W(v, bce_tok, u_rd[p], t1_rd[p])
            for k in range(NK):
                ii = v.tensor_mul(t1[:, p, k], xt[:, p, k], bcast[:, 0:FT])
            mul_tok = vv.inc(ii)
            kb.W(v, mul_tok)
            for k in range(NK):
                ii = v.tensor_sub(u[:, p, k], t1[:, p, k], bcast[:, FT:2 * FT])
            u_tok = vv.inc(ii)
            bc_rd = u_tok

            kb.W(t, u_tok, toks["rw"], mB_free)
            for k in range(NK):
                ii = t.matmul(mB[0:NEXP, 0:FT], env["rw_bf"][:, k * NEXP:(k + 1) * NEXP],
                              u[:, p, k], start=(k == 0), stop=(k == NK - 1))
            rt_tok = mm.inc(ii)
            kb.W(v, rt_tok)
            lrt_tok = vv.inc(v.tensor_scalar(Lrt[:], mB[0:NEXP, 0:FT], env["rb_s"][:], None, ALU.add))
            kb.W(t, lrt_tok, ltm_w)
            t.transpose(mB[:, FT:FT + NEXP], Lrt[:, 0:128], env["ident_f"][0:NEXP, 0:NEXP])
            ii = t.transpose(mB[:, FT + NEXP:FT + 2 * NEXP], Lrt[:, 128:256], env["ident_f"][0:NEXP, 0:NEXP])
            ltr_tok = mm.inc(ii)
            kb.W(v, ltr_tok)
            v.tensor_copy(Ltm[:, 0], mB[:, FT:FT + NEXP])
            ltm_tok = vv.inc(v.tensor_copy(Ltm[:, 1], mB[:, FT + NEXP:FT + 2 * NEXP]))
            ltm_w = ltm_tok

            P = lambda sl: smx[:, sl, 0]
            P1 = lambda sl: smx[:, sl, 1]
            P2 = lambda sl: smx[:, sl, 2]
            MK = lambda sl: smk[:, sl]
            m1n = lambda sl: smr[:, sl, 0:1]
            m1p = lambda sl: smr[:, sl, 1:2]
            m2r = lambda sl: smr[:, sl, 2:3]
            m3r = lambda sl: smr[:, sl, 3:4]
            rr = lambda sl: smr[:, sl, 4:5]

            kb.W(v, ltm_tok, toks["init"])
            for sl in (0, 1):
                ii = v.reduce_max(m1n(sl), Ltm[:, sl], axis=AX.X, negate=True)
            c_tok = vv.inc(ii)
            kb.W(s, c_tok)
            for sl in (0, 1):
                ii = s.activation(P(sl), Ltm[:, sl], AF.Exp, bias=m1n(sl))
            e_tok = ev.inc(ii)
            kb.W(v, e_tok)
            for sl in (0, 1):
                v.reduce_sum(rr(sl), P(sl), axis=AX.X)
            for sl in (0, 1):
                ii = v.reduce_max(m1p(sl), P(sl), axis=AX.X)
            c_tok = vv.inc(ii)
            kb.W(v, c_tok)
            for sl in (0, 1):
                ii = v.reciprocal(rr(sl), rr(sl))
            c_tok = vv.inc(ii)
            kb.W(v, c_tok)
            for sl in (0, 1):
                ii = v.tensor_scalar(MK(sl), P(sl), m1p(sl), None, ALU.is_lt)
            c_tok = vv.inc(ii)
            kb.W(v, c_tok)
            for sl in (0, 1):
                ii = v.select(P1(sl), MK(sl), P(sl), env["negone"][:, 0:NEXP])
            c_tok = vv.inc(ii)
            kb.W(v, c_tok)
            for sl in (0, 1):
                ii = v.reduce_max(m2r(sl), P1(sl), axis=AX.X)
            c_tok = vv.inc(ii)
            kb.W(v, c_tok)
            for sl in (0, 1):
                ii = v.tensor_scalar(MK(sl), P1(sl), m2r(sl), None, ALU.is_lt)
            c_tok = vv.inc(ii)
            kb.W(v, c_tok)
            for sl in (0, 1):
                ii = v.select(P2(sl), MK(sl), P1(sl), env["negone"][:, 0:NEXP])
            c_tok = vv.inc(ii)
            kb.W(v, c_tok)
            for sl in (0, 1):
                ii = v.reduce_max(m3r(sl), P2(sl), axis=AX.X)
            c_tok = vv.inc(ii)
            kb.W(v, c_tok)
            for sl in (0, 1):
                ii = v.tensor_scalar(MK(sl), P(sl), m3r(sl), None, ALU.is_ge)
            c_tok = vv.inc(ii)
            kb.W(v, c_tok)
            for sl in (0, 1):
                ii = v.tensor_mul(P1(sl), P(sl), MK(sl))
            c_tok = vv.inc(ii)
            kb.W(v, c_tok, cwtm_w)
            for sl in (0, 1):
                ii = v.tensor_scalar(cw_tm[:, sl], P1(sl), rr(sl), None, ALU.mult)
            cw_tok = vv.inc(ii)

            kb.W(t, cw_tok)
            t.transpose(mB[0:NEXP, 0:128], cw_tm[:, 0], env["ident_f"][:])
            ii = t.transpose(mB[0:NEXP, 128:256], cw_tm[:, 1], env["ident_f"][:])
            cwt_tok = mm.inc(ii)
            cwtm_w = cwt_tok
            kb.W(v, cwt_tok, cw_rd)
            v.tensor_copy(cw_fm[:, 0:128], mB[0:NEXP, 0:128])
            cwf_tok = vv.inc(v.tensor_copy(cw_fm[:, 128:256], mB[0:NEXP, 128:256]))

            cwb_ev = [None] * NEXP
            for e in range(NEXP):
                dst_ps = mA if e % 2 == 0 else mB
                if e == 0:
                    kb.W(t, cwf_tok, bce_tok)
                elif e == 1:
                    kb.W(t, cwf_tok)
                else:
                    kb.W(t, cwb_ev[e - 2])
                ii = t.matmul(dst_ps[:, 0:FT], env["sel"][:, e * 128:(e + 1) * 128],
                              cw_fm[:], start=True, stop=True)
                cb_tok = mm.inc(ii)
                kb.W(s, cb_tok, cwb_rd[e])
                cwb_ev[e] = ev.inc(s.copy(cwb[:, e], dst_ps[:, 0:FT]))
            mA_free = cwb_ev[14]
            mB_free = cwb_ev[13]

            relu_ev = {}
            h2_vv = {}
            w2_mm = {}
            order = [NEXP] + list(range(NEXP))
            seq = [(e, j) for e in order for j in range(NK)]

            def w1_group(idx):
                e, j = seq[idx]
                ps_slot = idx % 2
                h_slot = idx % 4
                pe = e % 2
                if e < NEXP:
                    kb.W(t, wld[e], hps_rd[ps_slot])
                else:
                    kb.W(t, sw_tok, hps_rd[ps_slot])
                for dk in range(NK):
                    lt = w1t(pe, dk, j) if e < NEXP else s1t(dk, j)
                    ii = t.matmul(hsl(ps_slot), lt, u[:, dk], start=(dk == 0), stop=(dk == NK - 1))
                tok = mm.inc(ii)
                kb.W(s, tok, h_rd[h_slot])
                b = bias["eb1"](e, j) if e < NEXP else bias["sb1"](j)
                ii = s.activation(h_bf[:, h_slot], hsl(ps_slot), AF.Relu, bias=b)
                relu_ev[idx] = ev.inc(ii)
                hps_rd[ps_slot] = relu_ev[idx]
                if e < NEXP:
                    kb.W(v, relu_ev[idx], h2_rd[h_slot])
                    ii = v.tensor_mul(h2_bf[:, h_slot], h_bf[:, h_slot], cwb[:, e])
                    h2_vv[idx] = vv.inc(ii)
                    h_rd[h_slot] = h2_vv[idx]
                    cwb_rd[e] = h2_vv[idx]

            def w2_group(idx, first):
                e, j = seq[idx]
                h_slot = idx % 4
                pe = e % 2
                if e < NEXP:
                    kb.W(t, h2_vv[idx])
                    rhs = h2_bf[:, h_slot]
                else:
                    kb.W(t, relu_ev[idx])
                    rhs = h_bf[:, h_slot]
                if first:
                    kb.W(t, acc_free)
                for ik in range(NK):
                    lt = w2t(pe, j, ik) if e < NEXP else s2t(j, ik)
                    ii = t.matmul(accs(ik), lt, rhs, start=(first and ik % 2 == 0), stop=False)
                tok = mm.inc(ii)
                w2_mm[idx] = tok
                if e < NEXP:
                    h2_rd[h_slot] = tok
                else:
                    h_rd[h_slot] = tok

            for idx in range(len(seq)):
                w1_group(idx)
                if idx >= 2:
                    w2_group(idx - 2, first=(idx == 2))
                e, j = seq[idx]
                if j == NK - 1 and e < NEXP and e + 2 < NEXP:
                    wtok[(e + 2) % 2] = mm.tok()
                    wld[e + 2] = load_exp(e + 2, ti)
                if idx == len(seq) - 4 and ti + 1 < NFT:
                    x_next2 = load_x(ti + 1)
            w2_group(len(seq) - 2, first=False)
            w2_group(len(seq) - 1, first=False)
            wtok[0] = w2_mm[len(seq) - 1]
            wtok[1] = w2_mm[(1 + 13) * NK + NK - 1]

            kb.W(t, toks["rw"], cwf_tok)
            for ik in range(NK):
                ii = t.matmul(accs(ik), env["eb2_bf"][:, ik * 128:(ik + 1) * 128],
                              cw_fm[:], start=False, stop=(ik % 2 == 1))
            acc_tok = mm.inc(ii)
            cw_rd = acc_tok
            u_rd = acc_tok

            kb.W(v, acc_tok, t1_rd)
            for ik in range(NK):
                ii = v.tensor_scalar(t1[:, ik], accs(ik), bias["sb2"](ik), 0.5, ALU.add, ALU.mult)
            fe_tok = vv.inc(ii)
            acc_free = fe_tok
            kb.W(v, fe_tok)
            for ik in range(NK):
                ii = v.tensor_add(xt[:, p, ik], xt[:, p, ik], t1[:, ik])
            add_tok = vv.inc(ii)
            t1_rd = add_tok
            kb.W(q, add_tok)
            kb.dst.n += 16
            xtok[p] = kb.dsp[p].inc(q.dma_start(out=dst3[:, :, ti * FT:(ti + 1) * FT], in_=xt[:, p]), 16)
            if ti + 1 < NFT:
                x_next = x_next2

        kb.W(q, kb.dsp[0].tok(), kb.dsp[1].tok())


def emit_attn(nc, kb, ap, wbf, src, dst, env, toks):
    t, v, s, g, q = kb.t, kb.v, kb.s, kb.g, kb.q
    mm, ev, vv, pp = kb.mm, kb.ev, kb.vv, kb.pp
    bias = env["bias"]
    src3 = src.rearrange("(k p) n -> p k n", p=128)
    dst3 = dst.rearrange("(k p) n -> p k n", p=128)
    A = 512
    ST = 256
    NG = T // A

    entry = [kb.dsp[0].tok(), kb.dsp[1].tok()]
    with ExitStack() as ctx:
        NXP = 2
        xt = ctx.enter_context(nc.sbuf_tensor(None, [128, NXP, NK, ST], F32))
        xbf = ctx.enter_context(nc.sbuf_tensor(None, [128, 2, NK, ST], BF16))
        x2b = ctx.enter_context(nc.sbuf_tensor(None, [128, NK, ST], BF16))
        ug = ctx.enter_context(nc.sbuf_tensor(None, [128, NK, A], BF16))
        t1 = ctx.enter_context(nc.sbuf_tensor(None, [128, NK, ST], BF16))
        bcg = ctx.enter_context(nc.sbuf_tensor(None, [128, 2 * ST], F32))
        srow = ctx.enter_context(nc.sbuf_tensor(None, [1, 3 * ST], F32))
        srbf = ctx.enter_context(nc.sbuf_tensor(None, [1, 2 * ST], BF16))
        Qp = ctx.enter_context(nc.sbuf_tensor(None, [128, NK, A], BF16))
        Kp = ctx.enter_context(nc.sbuf_tensor(None, [128, NK, A], BF16))
        Vb = ctx.enter_context(nc.sbuf_tensor(None, [128, NK, A], BF16))
        phia = ctx.enter_context(nc.sbuf_tensor(None, [128, 2, A], BF16))
        phir = ctx.enter_context(nc.sbuf_tensor(None, [128, 2, A], BF16))
        Ktm = ctx.enter_context(nc.sbuf_tensor(None, [128, 4, D], BF16))
        Vtm = ctx.enter_context(nc.sbuf_tensor(None, [128, 4, D], BF16))
        Ab = ctx.enter_context(nc.sbuf_tensor(None, [128, 2, 128], BF16))
        Ssb = ctx.enter_context(nc.sbuf_tensor(None, [128, NK, D], F32))
        Sbf = ctx.enter_context(nc.sbuf_tensor(None, [128, NK, D], BF16))
        Zc = ctx.enter_context(nc.sbuf_tensor(None, [128, NK], F32))
        Zs = ctx.enter_context(nc.sbuf_tensor(None, [128, NK], F32))
        Zbf = ctx.enter_context(nc.sbuf_tensor(None, [128, NK], BF16))
        dens = ctx.enter_context(nc.sbuf_tensor(None, [1, 2, ST], F32))
        rden = ctx.enter_context(nc.sbuf_tensor(None, [1, 2, ST], F32))
        rdbf = ctx.enter_context(nc.sbuf_tensor(None, [1, 2, ST], BF16))
        rdb = ctx.enter_context(nc.sbuf_tensor(None, [128, 2, ST], F32))
        wst = ctx.enter_context(nc.sbuf_tensor(None, [128, 3, NK, D], BF16))
        pA = ctx.enter_context(nc.psum_tensor(None, [128, 512], F32))
        pB = ctx.enter_context(nc.psum_tensor(None, [128, 512], F32))
        nump = ctx.enter_context(nc.psum_tensor(None, [128, NK, ST], F32))

        for eng in (t, v, s, g, q):
            kb.W(eng, *entry)
        kb.W(s, toks["init"], toks["bias"])

        smem = vv.inc(v.memset(Ssb[:], 0.0))
        zmem = vv.inc(v.memset(Zs[:], 0.0))

        wmat = lambda pe, ck, ok: wst[:, pe, ck, ok * 128:(ok + 1) * 128]
        wseq = ("wqT", "wkT", "wvT", "woT")
        wtok = [None, None, None]
        wpre = {}

        def load_w(gi, wi):
            pe = (gi * 4 + wi) % 3
            nm = wseq[wi]
            if gi == 0:
                kb.W(q, toks["cast"]["woT"])
            kb.W(q, wtok[pe])
            return pe, kb.dwp[wi % 2].inc(q.dma_start(
                out=wst[:, pe], in_=wbf[nm].rearrange("(c p) n -> p c n", p=128)), 16)

        xbtok = [None, None]
        x2tok = None
        srow_w = None
        srbf_w = None
        bc_rd = None
        u_rd = None
        t1_rd = None
        pa_rd = None
        pb_rd = None
        num_rd = None
        qkv_rd = None
        Ab_rd = [None, None]
        O_rd = None
        tm_rd = None
        Sbf_w = None
        zbf_rd = None
        phi_rds = [None, None]
        rdb_rd = [None, None]
        ptr_kv_rd = None

        xtok = [None] * NXP
        def load_x(sti):
            p = sti % NXP
            pb2 = sti % 2
            kb.W(q, xtok[p])
            a_tok = kb.dxp[pb2].inc(q.dma_start(out=xt[:, p], in_=src3[:, :, sti * ST:(sti + 1) * ST]), 16)
            kb.W(g, xbtok[pb2])
            b_tok = kb.dbp[pb2].inc(g.dma_start(out=xgb_in(pb2), in_=src3[:, :, sti * ST:(sti + 1) * ST]), 16)
            return a_tok, b_tok

        def xgb_in(p):
            return xbf[:, p]

        x_next = load_x(0)
        x_next2 = None
        for gi in range(NG):
            wld = {0: wpre.pop((gi, 0), None) or load_w(gi, 0),
                   1: wpre.pop((gi, 1), None) or load_w(gi, 1)}

            u_toks = []
            for sti2 in range(2):
                sti = gi * 2 + sti2
                p = sti % NXP
                pb2 = sti % 2
                if sti2 == 0:
                    x_tok, xb_tok = x_next
                else:
                    x_tok, xb_tok = x_next2
                if sti2 == 0:
                    x_next2 = load_x(sti + 1)
                kb.W(s, x_tok, x2tok)
                for k in range(NK):
                    ii = s.activation(x2b[:, k], xt[:, p, k], AF.Square)
                sq_tok = ev.inc(ii)
                kb.W(t, xb_tok, sq_tok, pa_rd, pb_rd)
                for k in range(NK):
                    t.matmul(pA[0:1, 0:ST], env["ones_col"][:], xbf[:, pb2, k],
                             start=(k == 0), stop=(k == NK - 1))
                for k in range(NK):
                    ii = t.matmul(pA[0:1, ST:2 * ST], env["ones_col"][:], x2b[:, k],
                                  start=(k == 0), stop=(k == NK - 1))
                stats_tok = mm.inc(ii)
                xbtok[pb2] = stats_tok
                x2tok = stats_tok

                m_row = srow[0:1, 0:ST]
                msq_row = srow[0:1, ST:2 * ST]
                var_row = srow[0:1, 2 * ST:3 * ST]
                rstd_bf = srbf[0:1, 0:ST]
                mrstd_bf = srbf[0:1, ST:2 * ST]
                kb.W(v, stats_tok, srow_w)
                c_tok = vv.inc(v.tensor_scalar(m_row, pA[0:1, 0:ST], 1.0 / 1024, None, ALU.mult))
                kb.W(v, c_tok)
                c_tok = vv.inc(v.tensor_mul(msq_row, m_row, m_row))
                kb.W(v, c_tok)
                c_tok = vv.inc(v.scalar_tensor_tensor(var_row, pA[0:1, ST:2 * ST], 1.0 / 1024,
                                                      msq_row, ALU.mult, ALU.subtract))
                kb.W(s, c_tok)
                e_tok = ev.inc(s.activation(var_row, var_row, AF.Ln, bias=1e-5))
                kb.W(s, e_tok, srbf_w)
                e_tok = ev.inc(s.activation(rstd_bf, var_row, AF.Exp, scale=-0.5))
                kb.W(v, e_tok)
                c_tok = vv.inc(v.tensor_mul(mrstd_bf, m_row, rstd_bf))
                srow_w = c_tok
                kb.W(t, c_tok)
                t.matmul(pB[:, 0:ST], env["ones_row"][:], rstd_bf, start=True, stop=True)
                ii = t.matmul(pB[:, ST:2 * ST], env["ones_row"][:], mrstd_bf, start=True, stop=True)
                bc_tok = mm.inc(ii)
                srbf_w = bc_tok
                pa_rd = c_tok
                kb.W(s, bc_tok, bc_rd)
                s.copy(bcg[:, 0:ST], pB[:, 0:ST])
                bce_tok = ev.inc(s.copy(bcg[:, ST:2 * ST], pB[:, ST:2 * ST]))
                pb_rd = bce_tok
                kb.W(v, bce_tok, u_rd, t1_rd)
                for k in range(NK):
                    ii = v.tensor_mul(t1[:, k], xt[:, p, k], bcg[:, 0:ST])
                mul_tok = vv.inc(ii)
                kb.W(v, mul_tok)
                for k in range(NK):
                    ii = v.tensor_sub(ug[:, k, sti2 * ST:(sti2 + 1) * ST], t1[:, k],
                                      bcg[:, ST:2 * ST])
                u_tok = vv.inc(ii)
                bc_rd = u_tok
                t1_rd = u_tok
                u_toks.append(u_tok)

            rot = [pA[:, 0:A], pB[:, 0:A], nump[:, 0:2, :], nump[:, 2:4, :]]
            rfree = [pa_rd, pb_rd, num_rd, num_rd]
            np_i = [0]
            NROT = 4

            def proj(wi, dst_sb, bias_fn, nbias_fn, is_v):
                nonlocal qkv_rd
                pe, wl = wld[wi]
                for ok in range(NK):
                    sl = np_i[0] % NROT
                    ps = rot[sl]
                    pp2 = np_i[0] % 2
                    kb.W(t, u_toks[0], u_toks[1], wl, rfree[sl])
                    for dk in range(NK):
                        ii = t.matmul(ps, wmat(pe, dk, ok), ug[:, dk],
                                      start=(dk == 0), stop=(dk == NK - 1))
                    ptok = mm.inc(ii)
                    if is_v:
                        kb.W(v, ptok, qkv_rd)
                        eetok = vv.inc(v.tensor_scalar(dst_sb[:, ok], ps,
                                                       bias_fn(ok), None, ALU.add))
                    else:
                        kb.W(s, ptok, phi_rds[pp2])
                        a1 = ev.inc(s.activation(phia[:, pp2], ps, AF.Relu,
                                                 scale=-1.0, bias=nbias_fn(ok)))
                        kb.W(s, a1)
                        ev.inc(s.activation(phia[:, pp2], phia[:, pp2], AF.Exp, scale=-1.0))
                        rtok = ev.inc(s.activation(phir[:, pp2], ps, AF.Relu, bias=bias_fn(ok)))
                        kb.W(v, rtok, qkv_rd)
                        eetok = vv.inc(v.tensor_add(dst_sb[:, ok], phia[:, pp2], phir[:, pp2]))
                        phi_rds[pp2] = eetok
                    rfree[sl] = eetok
                    np_i[0] += 1
                return eetok

            qdone = proj(0, Qp, bias["bq"], bias["nbq"], False)
            wtok[wld[0][0]] = mm.tok()
            wld[2] = load_w(gi, 2)
            kdone = proj(1, Kp, bias["bk"], bias["nbk"], False)
            wtok[wld[1][0]] = mm.tok()
            wld[3] = load_w(gi, 3)
            vdone = proj(2, Vb, bias["bv"], None, True)
            wtok[wld[2][0]] = mm.tok()
            if gi + 1 < NG:
                wpre[(gi + 1, 0)] = load_w(gi + 1, 0)
                wpre[(gi + 1, 1)] = load_w(gi + 1, 1)
            pa_rd, pb_rd = rfree[0], rfree[1]
            num_rd = rfree[3] if rfree[3] is not None else num_rd
            kb.W(t, rfree[2])

            with nc.psum_tensor("ptr_%d" % gi, [128, D], BF16, allow_name_mangling=True) as ptr:
                kb.W(t, kdone, vdone, ptr_kv_rd, tm_rd)
                ttoks = []
                for ci in range(4):
                    csl = slice(ci * CH, (ci + 1) * CH)
                    for srcb, dstb in ((Kp, Ktm), (Vb, Vtm)):
                        if ttoks:
                            kb.W(t, ttoks[-1])
                        for dk in range(NK):
                            ii = t.transpose(ptr[:, dk * 128:(dk + 1) * 128],
                                             srcb[:, dk, csl], env["ident_bf"][:])
                        tr_tok = mm.inc(ii)
                        kb.W(v, tr_tok)
                        ttoks.append(vv.inc(v.tensor_copy(dstb[:, ci], ptr[:])))
                tm_done = ttoks[-1]

            numsl = lambda ek, lo, w: nump[:, ek, lo:lo + w]
            den_ps = lambda sti2: pB[0:1, sti2 * ST:(sti2 + 1) * ST]
            abctx = ExitStack()
            _ab0 = abctx.enter_context(nc.psum_tensor("abps0_%d" % gi, [128, 512], F32, allow_name_mangling=True))
            _ab1 = abctx.enter_context(nc.psum_tensor("abps1_%d" % gi, [128, 512], F32, allow_name_mangling=True))
            abps = [_ab0, _ab1]
            den_defer = []
            o_toks = [None, None]
            for sti2 in range(2):
                sti = gi * 2 + sti2
                ssl = slice(sti2 * 2 * CH, (sti2 + 1) * 2 * CH)

                kb.W(t, num_rd)
                if gi > 0:
                    kb.W(t, Sbf_w)
                    for dk in range(NK):
                        for ek in range(NK):
                            ii = t.matmul(numsl(ek, 0, ST),
                                          Sbf[:, dk, ek * 128:(ek + 1) * 128],
                                          Qp[:, dk, ssl],
                                          start=(dk == 0 and ek % 2 == 0), stop=False)

                kb.W(t, tm_done, qdone)
                abm_toks = {}
                den_started = [False]
                first_den_wait = [True]
                for li, c in enumerate((sti2 * 2, sti2 * 2 + 1)):
                    for m in range(c + 1):
                        slot = len(abm_toks) % 2
                        abm_toks[len(abm_toks)] = 1
                        csl_c = slice(c * CH, (c + 1) * CH)
                        csl_m = slice(m * CH, (m + 1) * CH)
                        kb.W(t, Ab_rd[slot])
                        for dk in range(NK):
                            ii = t.matmul(abps[slot][:, 0:128], Kp[:, dk, csl_m], Qp[:, dk, csl_c],
                                          start=(dk == 0), stop=(dk == NK - 1))
                        ablk = mm.inc(ii)
                        kb.W(v, ablk, toks["init"])
                        if m == c:
                            abe = vv.inc(v.tensor_mul(Ab[:, slot], abps[slot][:, 0:128], env["triu"][:]))
                        else:
                            abe = vv.inc(v.tensor_copy(Ab[:, slot], abps[slot][:, 0:128]))
                        lo = (c % 2) * 128
                        if first_den_wait[0]:
                            kb.W(t, abe, pb_rd, zbf_rd)
                            first_den_wait[0] = False
                        else:
                            kb.W(t, abe)
                        dst_flag = not den_started[0]
                        den_started[0] = True
                        t.matmul(den_ps(sti2)[:, lo:lo + 128], env["ones_col"][:], Ab[:, slot],
                                 start=dst_flag, stop=(gi == 0 and li == 1 and m == c))
                        av_start = (gi == 0) and (m == 0)
                        av_stop = (m == c) and (gi == 0 or li == 1)
                        for ek in range(NK):
                            ii = t.matmul(numsl(ek, lo, 128), Vtm[:, m, ek * 128:(ek + 1) * 128],
                                          Ab[:, slot], start=(av_start and ek % 2 == 0),
                                          stop=(av_stop and ek % 2 == 1))
                        av_tok = mm.inc(ii)
                        Ab_rd[slot] = av_tok
                if gi > 0:
                    kb.W(t, zbf_rd)
                    for dk in range(NK):
                        ii = t.matmul(den_ps(sti2)[:], Zbf[:, dk:dk + 1], Qp[:, dk, ssl],
                                      start=False, stop=(dk == NK - 1))
                    den_tok = mm.inc(ii)
                else:
                    den_tok = mm.tok()
                num_done = mm.tok()

                kb.W(s, num_done, O_rd)
                ev_ii = None
                for ek in range(NK):
                    ev_ii = s.copy(ug[:, ek, ssl], nump[:, ek, :])
                evac_tok = ev.inc(ev_ii)
                num_rd = evac_tok
                den_defer.append((sti2, ssl, den_tok, evac_tok))

            abctx.close()
            scan_mm = mm.tok()
            with nc.psum_tensor("kvq_%d" % gi, [128, 2, 512], F32, allow_name_mangling=True) as kvq:
                kvfree = [None, None]
                inter_last = num_done if gi > 0 else None
                for dk in range(NK):
                    for hf in range(2):
                        slot = (dk * 2 + hf) % 2
                        kb.W(t, tm_done, kvfree[slot])
                        for ci in range(4):
                            ii = t.matmul(kvq[:, slot],
                                          Ktm[:, ci, dk * 128:(dk + 1) * 128],
                                          Vtm[:, ci, hf * 512:(hf + 1) * 512],
                                          start=(ci == 0), stop=(ci == 3))
                        kv_tok = mm.inc(ii)
                        kb.W(v, kv_tok, smem)
                        ii = v.tensor_add(Ssb[:, dk, hf * 512:(hf + 1) * 512],
                                          Ssb[:, dk, hf * 512:(hf + 1) * 512],
                                          kvq[:, slot])
                        sadd = vv.inc(ii)
                        kvfree[slot] = sadd
                        kb.W(s, sadd)
                        ii = s.copy(Sbf[:, dk, hf * 512:(hf + 1) * 512],
                                    Ssb[:, dk, hf * 512:(hf + 1) * 512])
                        ev.inc(ii)
                Sbf_w = kb.ev.tok()
                ptr_kv_rd = kvfree[0]
                ptr_kv_rd = kb.vv.tok()
            tm_rd = mm.tok()

            for sti2, ssl, den_tok, evac_tok in den_defer:
                kb.W(s, den_tok)
                de_tok = ev.inc(s.activation(dens[0:1, sti2], den_ps(sti2)[:],
                                             AF.Copy, bias=1e-6))
                pb_rd = de_tok
                kb.W(v, de_tok)
                r_tok = vv.inc(v.reciprocal(rden[0:1, sti2], dens[0:1, sti2]))
                kb.W(v, r_tok)
                r_tok = vv.inc(v.tensor_copy(rdbf[0:1, sti2], rden[0:1, sti2]))
                kb.W(t, r_tok, pa_rd)
                ii = t.matmul(pA[:, sti2 * ST:(sti2 + 1) * ST], env["ones_row"][:],
                              rdbf[0:1, sti2], start=True, stop=True)
                rb_tok = mm.inc(ii)
                kb.W(s, rb_tok, rdb_rd[sti2])
                rdb_tok = ev.inc(s.copy(rdb[:, sti2], pA[:, sti2 * ST:(sti2 + 1) * ST]))
                pa_rd = rdb_tok
                kb.W(v, rdb_tok, evac_tok)
                for ek in range(NK):
                    ii = v.tensor_mul(ug[:, ek, ssl], ug[:, ek, ssl], rdb[:, sti2])
                o_toks[sti2] = vv.inc(ii)
                rdb_rd[sti2] = o_toks[sti2]

            kb.W(v, kdone, zmem)
            for dk in range(NK):
                ii = v.reduce_sum(Zc[:, dk:dk + 1], Kp[:, dk, :], axis=AX.X)
            zr_tok = vv.inc(ii)
            kb.W(v, zr_tok)
            za_tok = vv.inc(v.tensor_add(Zs[:], Zs[:], Zc[:]))
            zmem = za_tok
            kb.W(s, za_tok, zbf_rd, scan_mm)
            zb_tok = ev.inc(s.copy(Zbf[:], Zs[:]))
            zbf_rd = zb_tok

            pe3, wl3 = wld[3]
            for sti2 in range(2):
                sti = gi * 2 + sti2
                p = sti % NXP
                ssl = slice(sti2 * 2 * CH, (sti2 + 1) * 2 * CH)
                wo_ev = [None] * NK
                for jk in range(NK):
                    sl = jk % 2
                    ps = rot[sl]
                    kb.W(t, wl3, o_toks[sti2], pa_rd if sl == 0 else pb_rd)
                    if jk >= 2:
                        kb.W(t, wo_ev[jk - 2])
                    for ek in range(NK):
                        ii = t.matmul(ps[:, 0:ST], wmat(pe3, ek, jk), ug[:, ek, ssl],
                                      start=(ek == 0), stop=(ek == NK - 1))
                    wotok = mm.inc(ii)
                    kb.W(v, wotok)
                    ii = v.scalar_tensor_tensor(xt[:, p, jk], ps[:, 0:ST], bias["bo"](jk),
                                                xt[:, p, jk], ALU.add, ALU.add)
                    wo_ev[jk] = vv.inc(ii)
                pa_rd = wo_ev[NK - 2]
                pb_rd = wo_ev[NK - 1]
                add_tok = wo_ev[NK - 1]
                kb.W(q, add_tok)
                kb.dst.n += 16
                xtok[p] = kb.dsp[sti % 2].inc(
                    q.dma_start(out=dst3[:, :, sti * ST:(sti + 1) * ST], in_=xt[:, p]), 16)
                if sti2 == 0 and gi + 1 < NG:
                    x_next = load_x((gi + 1) * 2)
            womm = mm.tok()
            O_rd = womm
            u_rd = womm
            qkv_rd = womm
            wtok[pe3] = womm

        kb.W(q, kb.dsp[0].tok(), kb.dsp[1].tok())


def _stage(inputs):
    c = lambda a: np.ascontiguousarray(np.asarray(a, dtype=np.float32))
    base = {
        "wqT": c(np.asarray(inputs["L0_attn_Wq"]).T), "wkT": c(np.asarray(inputs["L0_attn_Wk"]).T),
        "wvT": c(np.asarray(inputs["L0_attn_Wv"]).T), "woT": c(np.asarray(inputs["L0_attn_Wo"]).T),
        "sW1": c(inputs["L0_ffn_sW1"]), "sW2": c(inputs["L0_ffn_sW2"]),
        "eW1": c(np.asarray(inputs["L0_ffn_eW1"]).reshape(NEXP, NK, 128, D)
                 .transpose(0, 2, 1, 3)),
        "eW2": c(np.asarray(inputs["L0_ffn_eW2"]).reshape(NEXP, NK, 128, D)
                 .transpose(0, 2, 1, 3)),
        "rW": c(np.asarray(inputs["L0_ffn_router_W"]).reshape(NK, 128, NEXP)
                .transpose(1, 0, 2).reshape(128, NK * NEXP)),
        "rb": c(np.asarray(inputs["L0_ffn_router_b"]).reshape(NEXP, 1)),
        "eb1": c(np.asarray(inputs["L0_ffn_eb1"]).reshape(NEXP, NK, 128)
                 .transpose(2, 0, 1).reshape(128, NEXP * NK)),
        "eb1sp": c(np.asarray(inputs["L0_ffn_eb1"]).reshape(NEXP, NK, 128)
                   .transpose(0, 2, 1).reshape(NEXP * 128, NK)),
        "eb2": c(inputs["L0_ffn_eb2"]),
    }
    for nm, key in (("bq", "L0_attn_bq"), ("bk", "L0_attn_bk"), ("bv", "L0_attn_bv"),
                    ("bo", "L0_attn_bo"), ("sb1", "L0_ffn_sb1"), ("sb2", "L0_ffn_sb2")):
        base[nm] = c(np.asarray(inputs[key]).reshape(NK, 128).T)
    xs = c(np.asarray(inputs["L0_attn_x"]).transpose(0, 2, 1))
    return base, xs


def run_device(inputs, phases=("f1", "at", "f2")):
    base, xs = _stage(inputs)
    nc = build_program(phases)
    in_maps = [dict(base, X=xs[i]) for i in range(8)]
    res = run_bass_kernel_spmd(nc, in_maps, core_ids=list(range(8)))
    Y = np.stack([np.asarray(r["Y"]) for r in res.results])
    return np.ascontiguousarray(Y.transpose(0, 2, 1))


def kernel(**inputs):
    return run_device(inputs)
